# version 15
# baseline (speedup 1.0000x reference)
import numpy as np
import concourse.bass as bass
import concourse.mybir as mybir
from concourse.bass_utils import run_bass_kernel_spmd
from contextlib import ExitStack

F32 = mybir.dt.float32
BF16 = mybir.dt.bfloat16
AF = mybir.ActivationFunctionType
ALU = mybir.AluOpType
AX = mybir.AxisListType.X

D = 1024
DI = 2048
DIH = 1024
NST = 16
T = 512
NSUB = T // 128
KD = D // 128
NCH = DI // 128
NCHH = DIH // 128
RC = float(np.float32(1.5 * 2 ** 23))
EPS = 1e-5
I127 = float(np.float32(1.0 / 127.0))
GROUPS = [[0, 1], [2, 3], [4, 5], [6, 7]]


class Sem:
    def __init__(self, ctx, nc, name):
        self.h = ctx.enter_context(nc.semaphore(name))
        self.n = 0

    def add(self, k):
        self.n += k
        return self.n


class Guards:

    def __init__(self):
        self.g = {}

    def wait(self, eng, key):
        if key in self.g:
            s, v = self.g.pop(key)
            eng.wait_ge(s.h, v)

    def set(self, key, sem, val):
        self.g[key] = (sem, val)


def build_nc(L=8192, debug=False):
    nchunk = L // T
    nslot = nchunk // 2
    LH = L // 2

    nc = bass.Bass("TRN2", target_bir_lowering=False)
    dk = "ExternalOutput" if debug else "Internal"

    ap_x = nc.dram_tensor("x", [L, D], F32, kind="ExternalInput").ap()
    ap_xo = nc.dram_tensor("xown", [LH, D], F32, kind="ExternalInput").ap()
    ap_wix = nc.dram_tensor("wix", [KD, 128, DI], F32, kind="ExternalInput").ap()
    ap_wiz = nc.dram_tensor("wiz", [KD, 128, DIH], F32, kind="ExternalInput").ap()
    ap_cw = nc.dram_tensor("convw", [128, NCH * 4], F32, kind="ExternalInput").ap()
    ap_cb = nc.dram_tensor("convb", [128, NCH], F32, kind="ExternalInput").ap()
    ap_wxp = nc.dram_tensor("wxp", [NCH, 128, 128], F32, kind="ExternalInput").ap()
    ap_wdt = nc.dram_tensor("wdt", [128, DIH], F32, kind="ExternalInput").ap()
    ap_bdt = nc.dram_tensor("bdt", [128, NCHH], F32, kind="ExternalInput").ap()
    ap_al = nc.dram_tensor("alog", [128, 128], F32, kind="ExternalInput").ap()
    ap_ds = nc.dram_tensor("dskip", [128, NCHH], F32, kind="ExternalInput").ap()
    ap_wo = nc.dram_tensor("wout", [NCH, 128, D], F32, kind="ExternalInput").ap()
    ap_rhs = nc.dram_tensor("rhs", [KD, 128, D], F32, kind="ExternalInput").ap()

    ap_out = nc.dram_tensor("out_half", [LH, D], F32, kind="ExternalOutput").ap()

    d_delta = nc.dram_tensor("d_delta", [nchunk, 128, NCHH, T], F32, kind=dk).ap()
    d_u = nc.dram_tensor("d_u", [nchunk, 128, NCHH, T], F32, kind=dk).ap()
    d_sz = nc.dram_tensor("d_sz", [nchunk, 128, NCHH, T], F32, kind=dk).ap()
    d_bc = nc.dram_tensor("d_bc", [nchunk, 32, T], F32, kind=dk).ap()
    d_y = nc.dram_tensor("d_y", [nchunk, 128, NCHH * T], BF16, kind=dk).ap()
    d_qr = nc.dram_tensor("d_qr", [128, KD * D], BF16).ap()
    d_rs = nc.dram_tensor("d_rs", [D], F32).ap()
    d_y4p = nc.dram_tensor("d_y4p", [nslot, 128, NCHH * T], BF16).ap()
    d_yag = nc.dram_tensor("d_yag", [2, nslot, 128, NCHH * T], BF16).ap()
    d_yown = nc.dram_tensor("d_yown", [nslot, 128, NCHH * T], BF16).ap()
    d_ypeer = nc.dram_tensor("d_ypeer", [nslot, 128, NCHH * T], BF16).ap()

    ctx = ExitStack()
    with ctx:
        identb = ctx.enter_context(nc.sbuf_tensor("identb", [128, 128], BF16))
        identf = ctx.enter_context(nc.sbuf_tensor("identf", [128, 128], F32))
        a_sb = ctx.enter_context(nc.sbuf_tensor("a_sb", [128, 128], F32))
        cw_sb = ctx.enter_context(nc.sbuf_tensor("cw_sb", [128, NCH * 4], F32))
        cb_sb = ctx.enter_context(nc.sbuf_tensor("cb_sb", [128, NCH], F32))
        bdt_sb = ctx.enter_context(nc.sbuf_tensor("bdt_sb", [128, NCHH], F32))
        ds_sb = ctx.enter_context(nc.sbuf_tensor("ds_sb", [128, NCHH], F32))
        hcar = ctx.enter_context(nc.sbuf_tensor("hcar", [128, 128], F32))
        ones1 = ctx.enter_context(nc.sbuf_tensor("ones1", [1, 128], F32))
        rsb = ctx.enter_context(nc.sbuf_tensor("rsb", [128, D], F32))
        big8 = ctx.enter_context(nc.psum_tensor("big8", [128, 8 * T], F32))
        ps = [big8[:, i * T:(i + 1) * T] for i in range(8)]
        hps = big8[:, :].rearrange("p (a t) -> p a t", t=T)

        LDS = Sem(ctx, nc, "LDS")
        SPS = Sem(ctx, nc, "SPS")
        TPS = Sem(ctx, nc, "TPS")
        VES = Sem(ctx, nc, "VES")
        SCS = Sem(ctx, nc, "SCS")
        GPS = Sem(ctx, nc, "GPS")
        CCS = Sem(ctx, nc, "CCS")
        GPC = Sem(ctx, nc, "GPC")
        BCS = [Sem(ctx, nc, f"BCS{k}") for k in range(4)]
        LD2 = Sem(ctx, nc, "LD2")
        LD3 = Sem(ctx, nc, "LD3")
        STS = Sem(ctx, nc, "STS")

        block = ctx.enter_context(nc.Block())
        G = Guards()

        def on(eng, fn):
            getattr(block, eng)(fn)

        _dyn = {}

        def gdyn(g, key):
            if "phi" not in _dyn:
                _dyn["phi"] = g.partition_id() % 2
            phi = _dyn["phi"]
            if key not in _dyn:
                if key == "uoff":
                    _dyn[key] = phi * NCHH
                elif key == "own":
                    _dyn[key] = phi * nslot
                elif key == "peer":
                    _dyn[key] = (1 - phi) * nslot
                elif key == "peerag":
                    _dyn[key] = 1 - phi
            return _dyn[key]

        def qdma(eng, sem, **kw):
            eng.wait_ge(sem.h, sem.n)
            sem.n += 16
            return eng.dma_start(**kw).then_inc(sem.h, 16)

        RSQRT_MAGIC = 0x5F3759DF

        def emit_ln(xb, xn_out, st, sqscr, ld_wait):
            def _v1(vector):
                def vch(ins):
                    VES.n += 1
                    ins.then_inc(VES.h, 1)
                    vector.wait_ge(VES.h, VES.n)

                vector.wait_ge(LDS.h, ld_wait)
                for s in range(NSUB):
                    vector.tensor_reduce(
                        st["xsum"][:, s:s + 1], xb[:, s, :], AX, ALU.add)
                    vch(vector.tensor_tensor(
                        sqscr[:, :], xb[:, s, :], xb[:, s, :], ALU.mult))
                    vch(vector.tensor_reduce(
                        st["xsq"][:, s:s + 1], sqscr[:, :], AX, ALU.add))
                vch(vector.tensor_scalar(
                    st["mcol"][:, :], st["xsum"][:, :], 1.0 / D, None, ALU.mult))
                vch(vector.tensor_tensor(
                    st["sc1"][:, :], st["mcol"][:, :], st["mcol"][:, :],
                    ALU.mult))
                vch(vector.scalar_tensor_tensor(
                    st["varc"][:, :], st["xsq"][:, :], 1.0 / D,
                    st["sc1"][:, :], ALU.mult, ALU.subtract))
                vch(vector.tensor_scalar(
                    st["veps"][:, :], st["varc"][:, :], EPS, None, ALU.add))
                iv = st["veps"][:, :].bitcast(mybir.dt.int32)
                io = st["rstd"][:, :].bitcast(mybir.dt.int32)
                vch(vector.tensor_scalar(
                    io, iv, 1, None, ALU.arith_shift_right))
                vch(vector.tensor_scalar(
                    io, io, -1, RSQRT_MAGIC, ALU.mult, ALU.add))
                for _ in range(3):
                    vch(vector.tensor_tensor(
                        st["sc1"][:, :], st["rstd"][:, :], st["rstd"][:, :],
                        ALU.mult))
                    vch(vector.tensor_tensor(
                        st["sc2"][:, :], st["veps"][:, :], st["sc1"][:, :],
                        ALU.mult))
                    vch(vector.tensor_scalar(
                        st["sc2"][:, :], st["sc2"][:, :], -0.5, 1.5,
                        ALU.mult, ALU.add))
                    vch(vector.tensor_tensor(
                        st["rstd"][:, :], st["rstd"][:, :], st["sc2"][:, :],
                        ALU.mult))
                ins = None
                for s in range(NSUB):
                    ins = vector.tensor_scalar(
                        xn_out[:, s, :], xb[:, s, :], st["mcol"][:, s:s + 1],
                        st["rstd"][:, s:s + 1], ALU.subtract, ALU.mult)
                ins.then_inc(VES.h, 1)
            on("vector", _v1)
            VES.add(1)
            return VES.n

        s1 = ExitStack()
        wix_sb = s1.enter_context(nc.sbuf_tensor("wix_sb", [128, KD, DI], BF16))
        wiz_sb = s1.enter_context(nc.sbuf_tensor("wiz_sb", [128, KD, DIH], BF16))
        wxp_sb = s1.enter_context(nc.sbuf_tensor("wxp_sb", [128, NCH, 128], BF16))
        wdt_sb = s1.enter_context(nc.sbuf_tensor("wdt_sb", [128, DIH], BF16))
        diag_sb = s1.enter_context(
            nc.sbuf_tensor("diag_sb", [128, NCH * 4, 128], BF16))

        pro = ExitStack()
        stgA = pro.enter_context(nc.sbuf_tensor("stgA", [128, KD, D], F32))
        stgB = pro.enter_context(nc.sbuf_tensor("stgB", [128, KD, D], F32))
        qrT = pro.enter_context(nc.sbuf_tensor("qrT", [128, KD, D], BF16))
        qrS = pro.enter_context(nc.sbuf_tensor("qrS", [128, KD, D], BF16))
        ram8 = pro.enter_context(nc.sbuf_tensor("ram8", [128, KD], F32))
        rs8 = pro.enter_context(nc.sbuf_tensor("rs8", [128, KD], F32))
        rrs8 = pro.enter_context(nc.sbuf_tensor("rrs8", [128, KD], F32))
        qtmp = pro.enter_context(nc.sbuf_tensor("qtmp", [128, D], F32))

        def _sy0(sync):
            qdma(sync, LDS, out=cw_sb[:, :], in_=ap_cw)
            qdma(sync, LDS, out=cb_sb[:, :], in_=ap_cb)
            qdma(sync, LDS, out=bdt_sb[:, :], in_=ap_bdt)
            qdma(sync, LDS, out=ds_sb[:, :], in_=ap_ds)
            qdma(sync, LDS, out=a_sb[:, :], in_=ap_al)
            qdma(sync, LDS, out=stgA[:, :, :],
                 in_=ap_rhs.rearrange("k p e -> p k e"))
        on("sync", _sy0)
        ld_c = LDS.n

        def _gp0(gpsimd):
            for idt in (identb, identf):
                GPC.n += 1
                gpsimd.memset(idt[:, :], 0.0).then_inc(GPC.h, 1)
                gpsimd.wait_ge(GPC.h, GPC.n)
                GPC.n += 1
                gpsimd.affine_select(
                    out=idt[:, :], in_=idt[:, :],
                    compare_op=ALU.not_equal, fill=1.0, base=0,
                    pattern=[[-1, 128]], channel_multiplier=1
                ).then_inc(GPC.h, 1)
                gpsimd.wait_ge(GPC.h, GPC.n)
            GPC.n += 1
            gpsimd.memset(hcar[:, :], 0.0).then_inc(GPC.h, 1)
            GPC.n += 1
            gpsimd.memset(ones1[0:1, :], 1.0).then_inc(GPC.h, 1)
        on("gpsimd", _gp0)
        gp_id = GPC.n

        def _sc0(scalar):
            scalar.wait_ge(LDS.h, ld_c)
            scalar.activation(a_sb[:, :], a_sb[:, :], AF.Exp).then_inc(SCS.h, 1)
        sc_a = SCS.add(1)
        on("scalar", _sc0)

        def _v0(vector):
            vector.wait_ge(SCS.h, sc_a)
            vector.wait_ge(GPC.h, gp_id)
            vector.wait_ge(LDS.h, ld_c)
            vector.tensor_scalar(a_sb[:, :], a_sb[:, :], -1.0, None, ALU.mult)
            ins = None
            for jk in range(NCH * 4):
                ins = vector.tensor_scalar(
                    diag_sb[:, jk, :], identb[:, :], cw_sb[:, jk:jk + 1],
                    None, ALU.mult)
            ins.then_inc(VES.h, 1)
        ve_diag = VES.add(1)
        on("vector", _v0)

        def tr_block(src_ap, dst_fn, wait_fn=None, ident=None, bf=False):
            for idx in range(KD * KD):
                k, ec = divmod(idx, KD)
                bk = idx % 2

                def _pe(tensor, k=k, ec=ec, bk=bk, first=(idx == 0)):
                    if first and wait_fn is not None:
                        wait_fn(tensor)
                    G.wait(tensor, f"ps{bk}")
                    pv = (ps[bk][:, 0:64].bitcast(BF16) if bf
                          else ps[bk][:, 0:128])
                    tensor.transpose(
                        pv, src_ap(k, ec), ident[:, :]).then_inc(TPS.h, 1)
                tp = TPS.add(1)
                on("tensor", _pe)

                def _v(vector, k=k, ec=ec, bk=bk, tp=tp):
                    vector.wait_ge(TPS.h, tp)
                    pv = (ps[bk][:, 0:64].bitcast(BF16) if bf
                          else ps[bk][:, 0:128])
                    vector.tensor_copy(dst_fn(k, ec), pv).then_inc(VES.h, 1)
                ve = VES.add(1)
                on("vector", _v)
                G.set(f"ps{bk}", VES, ve)

        def _w_rhs(tensor):
            tensor.wait_ge(LDS.h, ld_c)
            tensor.wait_ge(VES.h, ve_diag)

        tr_block(lambda k, ec: stgA[:, k, ec * 128:(ec + 1) * 128],
                 lambda k, ec: stgB[:, ec, k * 128:(k + 1) * 128],
                 wait_fn=_w_rhs, ident=identf)
        ve_rT = VES.n

        def _v1(vector):
            def vch(ins):
                VES.n += 1
                ins.then_inc(VES.h, 1)
                vector.wait_ge(VES.h, VES.n)

            vector.wait_ge(VES.h, ve_rT)
            ins = None
            for ec in range(KD):
                ins = vector.tensor_reduce(
                    ram8[:, ec:ec + 1], stgB[:, ec, :], AX, ALU.max,
                    apply_absolute_value=True)
            vch(ins)
            vch(vector.tensor_scalar(
                rs8[:, :], ram8[:, :], 1e-12, I127, ALU.max, ALU.mult))
            vch(vector.reciprocal(rrs8[:, :], rs8[:, :]))
            ins = None
            for ec in range(KD):
                vch(vector.tensor_scalar(
                    qtmp[:, :], stgB[:, ec, :], rrs8[:, ec:ec + 1], None,
                    ALU.mult))
                ins = vector.tensor_scalar(
                    qrT[:, ec, :], qtmp[:, :], RC, -RC, ALU.add, ALU.add)
                if ec < KD - 1:
                    vch(ins)
            ins.then_inc(VES.h, 1)
        on("vector", _v1)
        ve_q = VES.add(1)

        def _sy1(sync):
            sync.wait_ge(VES.h, ve_q)
            for ec in range(KD):
                qdma(sync, SPS, out=d_rs[ec * 128:(ec + 1) * 128],
                     in_=rs8[:, ec:ec + 1])
        on("sync", _sy1)
        sp_rs = SPS.n

        def _sy2(sync):
            sync.wait_ge(SPS.h, sp_rs)
            qdma(sync, LDS, out=rsb[:, :], in_=d_rs.partition_broadcast(128))
        on("sync", _sy2)

        def _w_q(tensor):
            tensor.wait_ge(VES.h, ve_q)

        tr_block(lambda k, ec: qrT[:, ec, k * 128:(k + 1) * 128],
                 lambda k, ec: qrS[:, k, ec * 128:(ec + 1) * 128],
                 wait_fn=_w_q, ident=identb, bf=True)
        ve_qS = VES.n

        def _sy3(sync):
            sync.wait_ge(VES.h, ve_qS)
            qdma(sync, SPS, out=d_qr,
                 in_=qrS[:, :, :].rearrange("p k e -> p (k e)"))
        on("sync", _sy3)

        def ld_cast(src_ap, stg_view, dst_view, stg_guard_key):
            def _s(sync):
                G.wait(sync, stg_guard_key)
                qdma(sync, LDS, out=stg_view, in_=src_ap)
            on("sync", _s)
            ldn = LDS.n

            def _v(vector, ldn=ldn):
                vector.wait_ge(LDS.h, ldn)
                vector.tensor_copy(dst_view, stg_view).then_inc(VES.h, 1)
            ve = VES.add(1)
            on("vector", _v)
            G.set(stg_guard_key, VES, ve)

        sAf = stgA[:, :, :].rearrange("p k e -> p (k e)")
        sBf = stgB[:, :, :].rearrange("p k e -> p (k e)")
        G.set("stgA", VES, ve_rT)
        G.set("stgB", VES, ve_q)
        for h in range(2):
            ld_cast(ap_wix[:, :, h * 1024:(h + 1) * 1024].rearrange(
                        "k p c -> p k c"),
                    stgA[:, :, :],
                    wix_sb[:, :, h * 1024:(h + 1) * 1024], "stgA")
        ld_cast(ap_wiz.rearrange("k p c -> p k c"), stgB[:, :, :],
                wiz_sb[:, :, :], "stgB")
        ld_cast(ap_wxp.rearrange("j p c -> p j c"),
                sAf[:, 0:NCH * 128].rearrange("p (j c) -> p j c", c=128),
                wxp_sb[:, :, :], "stgA")
        ld_cast(ap_wdt, sBf[:, 0:DIH], wdt_sb[:, :], "stgB")
        G.g.pop("stgA", None)
        G.g.pop("stgB", None)

        ve_pro, sp_pro, ld_pro, tp_pro, sc_pro = VES.n, SPS.n, LDS.n, TPS.n, SCS.n

        def _bar(eng):
            eng.wait_ge(VES.h, ve_pro)
            eng.wait_ge(SPS.h, sp_pro)
            eng.wait_ge(LDS.h, ld_pro)
            eng.wait_ge(TPS.h, tp_pro)
            eng.wait_ge(SCS.h, sc_pro)
            eng.wait_ge(GPC.h, gp_id)
        for en in ("sync", "vector", "scalar", "tensor", "gpsimd"):
            on(en, _bar)
        pro.close()

        s1w = ExitStack()
        xsb = [s1w.enter_context(nc.sbuf_tensor(f"xsb{b}", [128, NSUB, D], F32))
               for b in range(2)]
        xnb = s1w.enter_context(nc.sbuf_tensor("xnb", [128, NSUB, D], BF16))
        xnT = s1w.enter_context(nc.sbuf_tensor("xnT", [128, KD, T], BF16))
        xinT = s1w.enter_context(nc.sbuf_tensor("xinT", [128, NCH, T + 3], BF16))
        ubf = s1w.enter_context(nc.sbuf_tensor("ubf", [128, NCH, T], BF16))
        xdT = s1w.enter_context(nc.sbuf_tensor("xdT", [128, T], BF16))
        bcst = [s1w.enter_context(nc.sbuf_tensor(f"bcst{b}", [128, T], F32))
                for b in range(2)]
        et8 = s1w.enter_context(nc.sbuf_tensor("et8", [128, NCHH, T], F32))
        szst = [s1w.enter_context(nc.sbuf_tensor(f"szst{b}", [128, T], F32))
                for b in range(2)]
        ufst = [s1w.enter_context(nc.sbuf_tensor(f"ufst{b}", [128, T], F32))
                for b in range(2)]
        sqscr = s1w.enter_context(nc.sbuf_tensor("sqscr", [128, D], F32))
        st4 = {n: s1w.enter_context(nc.sbuf_tensor("st_" + n, [128, NSUB], F32))
               for n in ("xsum", "xsq", "mcol", "varc", "veps", "rstd",
                         "sc1", "sc2")}

        def _gpz(gpsimd):
            GPC.n += 1
            gpsimd.memset(xinT[:, :, 0:3], 0.0).then_inc(GPC.h, 1)
        on("gpsimd", _gpz)
        gp_h0 = GPC.n

        xfree = {}
        szst_free = [0, 0]
        ufst_free = [0, 0]
        et_free = [0] * NCHH
        for i in range(nchunk):
            xb = xsb[i % 2]

            def _sx(sync, i=i):
                if i >= 2:
                    s, v = xfree[i - 2]
                    sync.wait_ge(s.h, v)
                qdma(sync, LDS, out=xsb[i % 2][:, :, :],
                     in_=ap_x[i * T:(i + 1) * T, :].rearrange(
                         "(s p) d -> p s d", p=128))
            on("sync", _sx)
            ld_x = LDS.n

            ve_xn1 = emit_ln(xb, xb, st4, sqscr, ld_x)

            def _vcast(vector, xb=xb, ve_xn1=ve_xn1):
                vector.wait_ge(VES.h, ve_xn1)
                vector.tensor_copy(
                    xnb[:, :, :].rearrange("p s d -> p (s d)"),
                    xb[:, :, :].rearrange("p s d -> p (s d)")
                ).then_inc(VES.h, 1)
            ve_xnb = VES.add(1)
            on("vector", _vcast)
            xfree[i] = (VES, ve_xnb)

            for j in range(KD):
                bk = j % 2

                def _pt(tensor, j=j, bk=bk, ve_xnb=ve_xnb, i=i):
                    if j == 0:
                        tensor.wait_ge(VES.h, ve_xnb)
                        if i == 0:
                            tensor.wait_ge(GPC.h, gp_h0)
                    G.wait(tensor, f"ps{bk}")
                    pv = ps[bk][:, 0:256].bitcast(BF16)
                    ins = None
                    for s in range(NSUB):
                        ins = tensor.transpose(
                            pv[:, s * 128:(s + 1) * 128],
                            xnb[:, s, j * 128:(j + 1) * 128], identb[:, :])
                    ins.then_inc(TPS.h, 1)
                tp = TPS.add(1)
                on("tensor", _pt)

                def _vt(vector, j=j, bk=bk, tp=tp):
                    vector.wait_ge(TPS.h, tp)
                    vector.tensor_copy(
                        xnT[:, j, :], ps[bk][:, 0:256].bitcast(BF16)
                    ).then_inc(VES.h, 1)
                ve = VES.add(1)
                on("vector", _vt)
                G.set(f"ps{bk}", VES, ve)
            ve_xnT = VES.n

            sc_zu = SCS.n
            sps_zu = SPS.n
            ve_xin_evac = {}
            tp_w = {}
            for m in range(NCH + NCHH):
                bk = 4 + m % 4

                def _pw(tensor, m=m, bk=bk, ve_xnT=ve_xnT):
                    if m == 0:
                        tensor.wait_ge(VES.h, ve_xnT)
                    G.wait(tensor, f"ps{bk}")
                    ins = None
                    for k in range(KD):
                        if m < NCH:
                            stat = wix_sb[:, k, m * 128:(m + 1) * 128]
                        else:
                            stat = wiz_sb[:, k, (m - NCH) * 128:(m - NCH + 1) * 128]
                        ins = tensor.matmul(
                            ps[bk][:, :], stat, xnT[:, k, :],
                            start=(k == 0), stop=(k == KD - 1))
                    ins.then_inc(TPS.h, 1)
                tp_w[m] = TPS.add(1)
                on("tensor", _pw)

                if m < NCH:
                    def _vw(vector, m=m, bk=bk, tp=tp_w[m]):
                        vector.wait_ge(TPS.h, tp)
                        vector.tensor_copy(
                            xinT[:, m, 3:T + 3], ps[bk][:, :]).then_inc(VES.h, 1)
                    ve = VES.add(1)
                    on("vector", _vw)
                    G.set(f"ps{bk}", VES, ve)
                    ve_xin_evac[m] = ve
                else:
                    G.set(f"ps{bk}", SCS, sc_zu + (m - NCH) + 1)

            tp_c = {}
            for j in range(NCH):
                bk = j % 2

                def _pc(tensor, j=j, bk=bk, ve_evj=ve_xin_evac[j]):
                    tensor.wait_ge(VES.h, ve_evj)
                    G.wait(tensor, f"ps{bk}")
                    ins = None
                    for k in range(4):
                        ins = tensor.matmul(
                            ps[bk][:, :], diag_sb[:, j * 4 + k, :],
                            xinT[:, j, k:k + T], start=(k == 0), stop=(k == 3))
                    ins.then_inc(TPS.h, 1)
                tp_c[j] = TPS.add(1)
                on("tensor", _pc)
                G.set(f"ps{bk}", SCS, sc_zu + NCHH + j + 1)

            def _szu(scalar, i=i, tp_w=dict(tp_w), tp_c=dict(tp_c),
                     sc_zu=sc_zu, sps_zu=sps_zu, szf=tuple(szst_free),
                     uff=tuple(ufst_free)):
                for zc in range(NCHH):
                    scalar.wait_ge(TPS.h, tp_w[NCH + zc])
                    if zc >= 2:
                        scalar.wait_ge(SPS.h, sps_zu + (zc - 1) * 16)
                    elif szf[zc % 2]:
                        scalar.wait_ge(SPS.h, szf[zc % 2])
                    SCS.n += 1
                    scalar.activation(
                        szst[zc % 2][:, :], ps[4 + zc % 4][:, :], AF.Silu
                    ).then_inc(SCS.h, 1)
                for j in range(NCH):
                    scalar.wait_ge(TPS.h, tp_c[j])
                    if j < NCHH:
                        if j >= 2:
                            scalar.wait_ge(SPS.h, sps_zu + 128 + (j - 1) * 16)
                        elif uff[j % 2]:
                            scalar.wait_ge(SPS.h, uff[j % 2])
                        scalar.activation(
                            ubf[:, j, :], ps[j % 2][:, :], AF.Silu,
                            bias=cb_sb[:, j:j + 1])
                        SCS.n += 1
                        scalar.activation(
                            ufst[j % 2][:, :], ps[j % 2][:, :], AF.Silu,
                            bias=cb_sb[:, j:j + 1]).then_inc(SCS.h, 1)
                    else:
                        SCS.n += 1
                        scalar.activation(
                            ubf[:, j, :], ps[j % 2][:, :], AF.Silu,
                            bias=cb_sb[:, j:j + 1]).then_inc(SCS.h, 1)
            on("scalar", _szu)
            assert SCS.n == sc_zu + NCHH + NCH

            for zc in range(NCHH):
                def _syz(sync, i=i, zc=zc, sc=sc_zu + zc + 1):
                    sync.wait_ge(SCS.h, sc)
                    qdma(sync, SPS, out=d_sz[i, :, zc, :],
                         in_=szst[zc % 2][:, :])
                on("sync", _syz)
            for j in range(NCHH):
                def _syu(sync, i=i, j=j, sc=sc_zu + NCHH + j + 1):
                    sync.wait_ge(SCS.h, sc)
                    qdma(sync, SPS, out=d_u[i, :, j, :], in_=ufst[j % 2][:, :])
                on("sync", _syu)
            assert SPS.n == sps_zu + 256
            szst_free[0] = sps_zu + 7 * 16
            szst_free[1] = sps_zu + 8 * 16
            ufst_free[0] = sps_zu + 128 + 7 * 16
            ufst_free[1] = sps_zu + 128 + 8 * 16
            sc_u_all = SCS.n

            def _pxd(tensor, sc_u_all=sc_u_all):
                tensor.wait_ge(SCS.h, sc_u_all)
                G.wait(tensor, "ps6")
                ins = None
                for k in range(NCH):
                    ins = tensor.matmul(
                        ps[6][:, :], wxp_sb[:, k, :], ubf[:, k, :],
                        start=(k == 0), stop=(k == NCH - 1))
                ins.then_inc(TPS.h, 1)
            tp_xd = TPS.add(1)
            on("tensor", _pxd)

            bcb = i % 2

            def _vxd(vector, bcb=bcb, tp_xd=tp_xd):
                vector.wait_ge(TPS.h, tp_xd)
                G.wait(vector, f"bcst{bcb}")
                vector.tensor_copy(xdT[:, :], ps[6][:, :])
                vector.tensor_copy(
                    bcst[bcb][64:96, :], ps[6][64:96, :]).then_inc(VES.h, 1)
            ve_xd = VES.add(1)
            on("vector", _vxd)
            G.set("ps6", VES, ve_xd)

            def _sybc(sync, i=i, bcb=bcb, ve_xd=ve_xd):
                sync.wait_ge(VES.h, ve_xd)
                qdma(sync, SPS, out=d_bc[i, :, :], in_=bcst[bcb][64:96, :])
            on("sync", _sybc)
            sp = SPS.n
            G.set(f"bcst{bcb}", SPS, sp)

            scd = SCS.n
            tp_d = {}
            for dh in range(NCHH):
                bk = 2 + dh % 2

                def _pd(tensor, dh=dh, bk=bk, ve_xd=ve_xd):
                    if dh == 0:
                        tensor.wait_ge(VES.h, ve_xd)
                    G.wait(tensor, f"ps{bk}")
                    tensor.matmul(
                        ps[bk][:, :], wdt_sb[:, dh * 128:(dh + 1) * 128],
                        xdT[:, :], start=True, stop=True).then_inc(TPS.h, 1)
                tp_d[dh] = TPS.add(1)
                on("tensor", _pd)
                G.set(f"ps{bk}", SCS, scd + dh + 1)

            def _sdel(scalar, i=i, tp_d=dict(tp_d), scd=scd,
                      etf=tuple(et_free)):
                for dh in range(NCHH):
                    scalar.wait_ge(TPS.h, tp_d[dh])
                    if etf[dh]:
                        scalar.wait_ge(SPS.h, etf[dh])
                    SCS.n += 1
                    scalar.activation(
                        et8[:, dh, :], ps[2 + dh % 2][:, :], AF.Exp,
                        bias=bdt_sb[:, dh:dh + 1]).then_inc(SCS.h, 1)
                scalar.wait_ge(SCS.h, scd + NCHH)
                for dh in range(NCHH):
                    SCS.n += 1
                    scalar.activation(
                        et8[:, dh, :], et8[:, dh, :], AF.Ln, bias=1.0
                    ).then_inc(SCS.h, 1)
            on("scalar", _sdel)
            assert SCS.n == scd + 2 * NCHH

            for dh in range(NCHH):
                def _syd(sync, i=i, dh=dh, lnv=scd + NCHH + dh + 1):
                    sync.wait_ge(SCS.h, lnv)
                    qdma(sync, SPS, out=d_delta[i, :, dh, :],
                         in_=et8[:, dh, :])
                on("sync", _syd)
                et_free[dh] = SPS.n

            tp_now = TPS.n

            ve_xin_last = ve_xin_evac[NCH - 1]

            def _vh(vector, tp_now=tp_now, ve_xin_last=ve_xin_last):
                vector.wait_ge(TPS.h, tp_now)
                vector.wait_ge(VES.h, ve_xin_last)
                vector.tensor_copy(
                    xinT[:, :, 0:3], xinT[:, :, T:T + 3]).then_inc(VES.h, 1)
            VES.add(1)
            on("vector", _vh)

        m1 = (VES.n, SPS.n, LDS.n, TPS.n, SCS.n, GPS.n)

        def _bar1(eng, m1=m1):
            eng.wait_ge(VES.h, m1[0])
            eng.wait_ge(SPS.h, m1[1])
            eng.wait_ge(LDS.h, m1[2])
            eng.wait_ge(TPS.h, m1[3])
            eng.wait_ge(SCS.h, m1[4])
            eng.wait_ge(GPS.h, m1[5])
        for en in ("sync", "vector", "scalar", "tensor", "gpsimd"):
            on(en, _bar1)
        s1w.close()
        s1.close()

        s2 = ExitStack()
        delta_a = s2.enter_context(nc.sbuf_tensor("delta_a", [128, NCHH * T], F32))
        u_a = s2.enter_context(nc.sbuf_tensor("u_a", [128, NCHH * T], F32))
        sz_a = s2.enter_context(nc.sbuf_tensor("sz_a", [128, NCHH * T], F32))
        dtu_a = s2.enter_context(nc.sbuf_tensor("dtu_a", [128, NCHH * T], F32))
        yacc = [s2.enter_context(nc.sbuf_tensor(f"yacc{b}", [128, NCHH * T], F32))
                for b in range(2)]
        yp_blk = [s2.enter_context(nc.sbuf_tensor(f"yp_blk{b}", [128, NCHH * T], F32))
                  for b in range(2)]
        dbu_b = s2.enter_context(nc.sbuf_tensor("dbu_b", [128, NCHH * T], F32))
        dAb = [s2.enter_context(nc.sbuf_tensor(f"dAb{b}", [128, T], F32))
               for b in range(4)]
        Bb = [s2.enter_context(nc.sbuf_tensor(f"Bb{b}", [128, T], F32))
              for b in range(2)]
        Cb = [s2.enter_context(nc.sbuf_tensor(f"Cb{b}", [128, T], F32))
              for b in range(2)]
        ybf = s2.enter_context(nc.sbuf_tensor("ybf", [128, NCHH * T], BF16))

        prev_vn_base = None
        prev_tts_last = None
        prev_tts_last2 = None

        def bc_load(i, n):
            nb = n % 2

            def _s(sync, i=i, n=n, nb=nb):
                G.wait(sync, f"Bb{nb}")
                qdma(sync, BCS[nb], out=Bb[nb][:, :],
                     in_=d_bc[i, n].unsqueeze(0).partition_broadcast(128))
                G.wait(sync, f"Cb{nb}")
                qdma(sync, BCS[2 + nb], out=Cb[nb][:, :],
                     in_=d_bc[i, 16 + n].unsqueeze(0).partition_broadcast(128))
            on("sync", _s)
            return (BCS[nb].n, BCS[2 + nb].n)

        for i in range(nchunk):
            def _sl(sync, i=i):
                G.wait(sync, "delta_a")
                qdma(sync, LDS, out=delta_a[:, :],
                     in_=d_delta[i].rearrange("p a t -> p (a t)"))
                G.wait(sync, "sz_a")
                qdma(sync, LD2, out=sz_a[:, :],
                     in_=d_sz[i].rearrange("p a t -> p (a t)"))
                G.wait(sync, "u_a")
                qdma(sync, LD3, out=u_a[:, :],
                     in_=d_u[i].rearrange("p a t -> p (a t)"))
            on("sync", _sl)
            ld_ds = LDS.n
            gp_u = LD3.n
            ld_sz = LD2.n

            def _vdtu(vector, ld_ds=ld_ds, gp_u=gp_u):
                vector.wait_ge(LDS.h, ld_ds)
                vector.wait_ge(LD3.h, gp_u)
                G.wait(vector, "dtu_a")
                vector.tensor_tensor(
                    dtu_a[:, :], delta_a[:, :], u_a[:, :], ALU.mult
                ).then_inc(VES.h, 1)
            ve_dtu = VES.add(1)
            on("vector", _vdtu)
            ve_dtu_g = [ve_dtu]

            if i == 0:
                bc_ready = [bc_load(0, 0), bc_load(0, 1)]

            gpc_carry_prev = GPC.n
            for n in range(NST):
                nb = n % 2
                vn_base = VES.n
                sc_bc = bc_ready[n]

                sc_base = SCS.n

                def _se(scalar, i=i, n=n, ld_ds=ld_ds, vn_base=vn_base,
                        prev_vn=prev_vn_base):
                    for dh in range(NCHH):
                        if dh == 0 and n == 0:
                            scalar.wait_ge(LDS.h, ld_ds)
                        if dh >= 4:
                            fa = vn_base + (dh - 4) + 2
                        elif prev_vn is not None:
                            fa = prev_vn + (dh + 4) + 2
                        else:
                            fa = 0
                        if fa:
                            scalar.wait_ge(VES.h, fa)
                        SCS.n += 1
                        scalar.activation(
                            dAb[dh % 4][:, :], delta_a[:, dh * T:(dh + 1) * T],
                            AF.Exp, scale=a_sb[:, dh * 16 + n:dh * 16 + n + 1]
                        ).then_inc(SCS.h, 1)
                on("scalar", _se)
                sc_base2 = sc_base

                def _vn(vector, i=i, n=n, nb=nb, vn_base=vn_base,
                        sc_bc=sc_bc, sc_base=sc_base2, ve_dtu=ve_dtu,
                        gpcw=gpc_carry_prev):
                    if n == 0:
                        vector.wait_ge(VES.h, ve_dtu)
                    vector.wait_ge(VES.h, vn_base)
                    vector.wait_ge(GPC.h, gpcw)
                    vector.wait_ge(BCS[nb].h, sc_bc[0])
                    vector.wait_ge(BCS[2 + nb].h, sc_bc[1])
                    VES.n += 1
                    vector.tensor_tensor(
                        dbu_b[:, :].rearrange("p (a t) -> p a t", t=T),
                        dtu_a[:, :].rearrange("p (a t) -> p a t", t=T),
                        Bb[nb][:, :].unsqueeze(1).broadcast_to([128, NCHH, T]),
                        ALU.mult).then_inc(VES.h, 1)
                    vector.wait_ge(VES.h, VES.n)
                    for dh in range(NCHH):
                        vector.wait_ge(SCS.h, sc_base + dh + 1)
                        VES.n += 1
                        vector.tensor_tensor_scan(
                            hps[:, dh, :], dAb[dh % 4][:, :],
                            dbu_b[:, dh * T:(dh + 1) * T],
                            hcar[:, n * 8 + dh:n * 8 + dh + 1],
                            ALU.mult, ALU.add).then_inc(VES.h, 1)
                    vector.wait_ge(VES.h, VES.n)
                    VES.n += 1
                    if n == 0:
                        vector.tensor_tensor(
                            yacc[0][:, :].rearrange("p (a t) -> p a t", t=T),
                            hps[:, :, :],
                            Cb[nb][:, :].unsqueeze(1).broadcast_to(
                                [128, NCHH, T]), ALU.mult).then_inc(VES.h, 1)
                    else:
                        vector.tensor_tensor(
                            yp_blk[nb][:, :].rearrange("p (a t) -> p a t", t=T),
                            hps[:, :, :],
                            Cb[nb][:, :].unsqueeze(1).broadcast_to(
                                [128, NCHH, T]), ALU.mult).then_inc(VES.h, 1)
                on("vector", _vn)
                tts_last = vn_base + 1 + NCHH
                ve_y = VES.n
                G.set(f"Bb{nb}", VES, vn_base + 1)
                G.set(f"Cb{nb}", VES, ve_y)

                def _sc(scalar, n=n, vn_base=vn_base):
                    for dh in range(NCHH):
                        scalar.wait_ge(VES.h, vn_base + 2 + dh)
                        SCS.n += 1
                        scalar.copy(
                            hcar[:, n * 8 + dh:n * 8 + dh + 1],
                            big8[:, dh * T + T - 1:dh * T + T]
                        ).then_inc(SCS.h, 1)
                on("scalar", _sc)

                gpc_carry_prev = GPC.n
                if n >= 2:
                    def _gp(gpsimd, pn=n - 1, gate=vn_base + 1):
                        gpsimd.wait_ge(VES.h, gate)
                        gpsimd.wait_ge(GPC.h, GPC.n)
                        GPC.n += 1
                        gpsimd.tensor_tensor(
                            yacc[pn % 2][:, :], yacc[1 - pn % 2][:, :],
                            yp_blk[pn % 2][:, :], ALU.add).then_inc(GPC.h, 1)
                    on("gpsimd", _gp)
                if n == NST - 1:
                    def _gpl(gpsimd, n=n, ve_y=ve_y):
                        gpsimd.wait_ge(VES.h, ve_y)
                        gpsimd.wait_ge(GPC.h, GPC.n)
                        GPC.n += 1
                        gpsimd.tensor_tensor(
                            yacc[n % 2][:, :], yacc[1 - n % 2][:, :],
                            yp_blk[n % 2][:, :], ALU.add).then_inc(GPC.h, 1)
                    on("gpsimd", _gpl)
                gpc_add_last = GPC.n

                if n + 2 < NST:
                    bc_ready.append(bc_load(i, n + 2))
                elif i + 1 < nchunk:
                    bc_ready.append(bc_load(i + 1, n + 2 - NST))
                else:
                    bc_ready.append(None)

                prev_vn_base = vn_base
            bc_ready = bc_ready[NST:]

            def _vyf(vector, i=i, ld_sz=ld_sz, gpl=gpc_add_last):
                vector.wait_ge(VES.h, VES.n)
                vector.wait_ge(GPC.h, gpl)
                vector.wait_ge(LD2.h, ld_sz)
                ins = None
                for dh in range(NCHH):
                    ins = vector.scalar_tensor_tensor(
                        yp_blk[0][:, dh * T:(dh + 1) * T],
                        u_a[:, dh * T:(dh + 1) * T],
                        ds_sb[:, dh:dh + 1],
                        yacc[(NST - 1) % 2][:, dh * T:(dh + 1) * T],
                        ALU.mult, ALU.add)
                VES.n += 1
                ins.then_inc(VES.h, 1)
                vector.wait_ge(VES.h, VES.n)
                G.wait(vector, "ybf")
                vector.tensor_tensor(
                    ybf[:, :], yp_blk[0][:, :], sz_a[:, :],
                    ALU.mult).then_inc(VES.h, 1)
            on("vector", _vyf)
            ve_yf = VES.add(1)
            G.set("delta_a", SCS, SCS.n)
            G.set("u_a", VES, ve_yf)
            G.set("sz_a", VES, ve_yf)
            G.set("dtu_a", VES, ve_yf)

            def _syy(sync, i=i, ve_yf=ve_yf):
                sync.wait_ge(VES.h, ve_yf)
                qdma(sync, SPS, out=d_y[i], in_=ybf[:, :])
            on("sync", _syy)
            sp_y = SPS.n
            G.set("ybf", SPS, sp_y)

        sp_all_y = SPS.n

        def _gx(gpsimd, sp_all_y=sp_all_y):
            gpsimd.wait_ge(SPS.h, sp_all_y)
            pb = gdyn(gpsimd, "peer")
            own = gdyn(gpsimd, "own")
            pag = gdyn(gpsimd, "peerag")
            qdma(gpsimd, GPS, out=d_y4p,
                 in_=d_y[bass.ds(pb, nslot)])
            qdma(gpsimd, GPS, out=d_yown,
                 in_=d_y[bass.ds(own, nslot)])
            gpsimd.wait_ge(GPS.h, GPS.n)
            CCS.n += 1
            gpsimd.collective_compute(
                "AllGather", ALU.bypass, replica_groups=GROUPS,
                ins=[d_y4p], outs=[d_yag]).then_inc(CCS.h, 1)
            gpsimd.wait_ge(CCS.h, CCS.n)
            qdma(gpsimd, GPS, out=d_ypeer,
                 in_=d_yag[bass.ds(pag, 1)].squeeze(0))
        on("gpsimd", _gx)
        gp_xch = GPS.n

        m2 = (VES.n, SPS.n, LDS.n, SCS.n, GPS.n)

        def _bar2(eng, m2=m2):
            eng.wait_ge(VES.h, m2[0])
            eng.wait_ge(SPS.h, m2[1])
            eng.wait_ge(LDS.h, m2[2])
            eng.wait_ge(SCS.h, m2[3])
            eng.wait_ge(GPS.h, m2[4])
        for en in ("sync", "vector", "scalar", "tensor", "gpsimd"):
            on(en, _bar2)
        s2.close()

        s3 = ExitStack()
        wout_sb = s3.enter_context(nc.sbuf_tensor("wout_sb", [128, NCH, D], BF16))
        qr_sb = s3.enter_context(nc.sbuf_tensor("qr_sb", [128, KD, D], BF16))
        xs3 = s3.enter_context(nc.sbuf_tensor("xs3", [128, NSUB, D], F32))
        xn3 = s3.enter_context(nc.sbuf_tensor("xn3", [128, NSUB, D], F32))
        q3 = s3.enter_context(nc.sbuf_tensor("q3", [128, NSUB, D], BF16))
        qT3 = s3.enter_context(nc.sbuf_tensor("qT3", [128, KD, T], BF16))
        ffn3 = s3.enter_context(nc.sbuf_tensor("ffn3", [128, NSUB, D], F32))
        comb3 = s3.enter_context(nc.sbuf_tensor("comb3", [128, NSUB, D], F32))
        ybf3 = s3.enter_context(nc.sbuf_tensor("ybf3", [128, NCH, T], BF16))
        outs3 = [s3.enter_context(nc.sbuf_tensor(f"out{b}", [128, D], F32))
                 for b in range(2)]
        qt3 = s3.enter_context(nc.sbuf_tensor("qt3", [128, D], F32))
        am3 = s3.enter_context(nc.sbuf_tensor("am3", [128, NSUB], F32))
        ls3 = s3.enter_context(nc.sbuf_tensor("ls3", [128, NSUB], F32))
        rls3 = s3.enter_context(nc.sbuf_tensor("rls3", [128, NSUB], F32))
        sq3 = s3.enter_context(nc.sbuf_tensor("sq3", [128, D], F32))
        st3 = {n: s3.enter_context(nc.sbuf_tensor("s3_" + n, [128, NSUB], F32))
               for n in ("xsum", "xsq", "mcol", "varc", "veps", "rstd",
                         "sc1", "sc2")}

        w3 = ExitStack()
        wstg = w3.enter_context(nc.sbuf_tensor("wstg", [128, KD * D], F32))
        for hh in range(2):
            def _sw(sync, hh=hh):
                if hh == 1:
                    G.wait(sync, "wstg")
                qdma(sync, LDS,
                     out=wstg[:, :].rearrange("p (j d) -> p j d", d=D),
                     in_=ap_wo[hh * KD:(hh + 1) * KD].rearrange(
                         "j p d -> p j d"))
            on("sync", _sw)
            ldw = LDS.n

            def _vw3(vector, hh=hh, ldw=ldw):
                vector.wait_ge(LDS.h, ldw)
                vector.tensor_copy(
                    wout_sb[:, hh * KD:(hh + 1) * KD, :],
                    wstg[:, :].rearrange("p (j d) -> p j d", d=D)
                ).then_inc(VES.h, 1)
            ve = VES.add(1)
            on("vector", _vw3)
            G.set("wstg", VES, ve)

        def _sq3(sync):
            qdma(sync, LDS,
                 out=qr_sb[:, :, :].rearrange("p k e -> p (k e)"), in_=d_qr)
        on("sync", _sq3)
        ld_qr = LDS.n

        ve_w3 = VES.n

        def _bar3(eng, ve_w3=ve_w3):
            eng.wait_ge(VES.h, ve_w3)
        for en in ("sync", "vector", "scalar", "tensor", "gpsimd"):
            on(en, _bar3)
        w3.close()

        for j in range(nslot):
            def _sx3(sync, j=j):
                G.wait(sync, "xs3")
                qdma(sync, LDS, out=xs3[:, :, :],
                     in_=ap_xo[j * T:(j + 1) * T, :].rearrange(
                         "(s p) d -> p s d", p=128))
            on("sync", _sx3)
            ld_x3 = LDS.n

            def _gy(sync, j=j):
                G.wait(sync, "ybf3")
                sync.wait_ge(GPS.h, gp_xch)
                qdma(sync, LDS,
                     out=ybf3[:, 0:NCHH, :].rearrange("p a t -> p (a t)"),
                     in_=d_yown[j])
                qdma(sync, LDS,
                     out=ybf3[:, NCHH:NCH, :].rearrange("p a t -> p (a t)"),
                     in_=d_ypeer[j])
            on("sync", _gy)
            gp_y = LDS.n

            ve_xn3 = emit_ln(xs3, xn3, st3, sq3, ld_x3)

            def _vq(vector, j=j, ve_xn3=ve_xn3):
                def vch(ins):
                    VES.n += 1
                    ins.then_inc(VES.h, 1)
                    vector.wait_ge(VES.h, VES.n)

                vector.wait_ge(VES.h, ve_xn3)
                ins = None
                for s in range(NSUB):
                    ins = vector.tensor_reduce(
                        am3[:, s:s + 1], xn3[:, s, :], AX, ALU.max,
                        apply_absolute_value=True)
                vch(ins)
                vch(vector.tensor_scalar(
                    ls3[:, :], am3[:, :], 1e-12, I127, ALU.max, ALU.mult))
                vch(vector.reciprocal(rls3[:, :], ls3[:, :]))
                ins = None
                for s in range(NSUB):
                    vch(vector.tensor_scalar(
                        qt3[:, :], xn3[:, s, :], rls3[:, s:s + 1], None,
                        ALU.mult))
                    ins = vector.tensor_scalar(
                        q3[:, s, :], qt3[:, :], RC, -RC, ALU.add, ALU.add)
                    if s < NSUB - 1:
                        vch(ins)
                ins.then_inc(VES.h, 1)
            on("vector", _vq)
            ve_q3 = VES.add(1)

            for k in range(KD):
                bk = k % 2

                def _pt3(tensor, k=k, bk=bk, ve_q3=ve_q3):
                    if k == 0:
                        tensor.wait_ge(VES.h, ve_q3)
                    G.wait(tensor, f"ps{bk}")
                    pv = ps[bk][:, 0:256].bitcast(BF16)
                    ins = None
                    for s in range(NSUB):
                        ins = tensor.transpose(
                            pv[:, s * 128:(s + 1) * 128],
                            q3[:, s, k * 128:(k + 1) * 128], identb[:, :])
                    ins.then_inc(TPS.h, 1)
                tp = TPS.add(1)
                on("tensor", _pt3)

                def _vt3(vector, k=k, bk=bk, tp=tp):
                    vector.wait_ge(TPS.h, tp)
                    vector.tensor_copy(
                        qT3[:, k, :], ps[bk][:, 0:256].bitcast(BF16)
                    ).then_inc(VES.h, 1)
                ve = VES.add(1)
                on("vector", _vt3)
                G.set(f"ps{bk}", VES, ve)
            ve_qT = VES.n

            for s in range(NSUB):
                def _pf(tensor, j=j, s=s, ve_qT=ve_qT, ld_qr=ld_qr):
                    if j == 0 and s == 0:
                        tensor.wait_ge(LDS.h, ld_qr)
                    if s == 0:
                        tensor.wait_ge(VES.h, ve_qT)
                    G.wait(tensor, "ps2")
                    G.wait(tensor, "ps3")
                    ins = None
                    for k in range(KD):
                        stat = qT3[:, k, s * 128:(s + 1) * 128]
                        tensor.matmul(
                            ps[2][:, :], stat, qr_sb[:, k, 0:T],
                            start=(k == 0), stop=(k == KD - 1))
                        ins = tensor.matmul(
                            ps[3][:, :], stat, qr_sb[:, k, T:D],
                            start=(k == 0), stop=(k == KD - 1))
                    ins.then_inc(TPS.h, 1)
                tp_f = TPS.add(1)
                on("tensor", _pf)

                def _sdq(scalar, s=s, tp_f=tp_f):
                    scalar.wait_ge(TPS.h, tp_f)
                    scalar.activation(
                        ffn3[:, s, 0:T], ps[2][:, :], AF.Copy,
                        scale=ls3[:, s:s + 1])
                    scalar.activation(
                        ffn3[:, s, T:D], ps[3][:, :], AF.Copy,
                        scale=ls3[:, s:s + 1]).then_inc(SCS.h, 1)
                sc_dq = SCS.add(1)
                on("scalar", _sdq)
                G.set("ps2", SCS, sc_dq)
                G.set("ps3", SCS, sc_dq)

                def _vdq(vector, s=s, sc_dq=sc_dq):
                    vector.wait_ge(SCS.h, sc_dq)
                    VES.n += 1
                    vector.tensor_tensor(
                        ffn3[:, s, :], ffn3[:, s, :], rsb[:, :], ALU.mult
                    ).then_inc(VES.h, 1)
                    vector.wait_ge(VES.h, VES.n)
                    vector.tensor_tensor(
                        comb3[:, s, :], xs3[:, s, :], ffn3[:, s, :], ALU.add
                    ).then_inc(VES.h, 1)
                on("vector", _vdq)
                ve_cmb = VES.add(1)

                def _po(tensor, j=j, s=s, gp_y=gp_y):
                    if s == 0:
                        tensor.wait_ge(LDS.h, gp_y)
                    G.wait(tensor, "ps4")
                    G.wait(tensor, "ps5")
                    ins = None
                    for jj in range(NCH):
                        stat = ybf3[:, jj, s * 128:(s + 1) * 128]
                        tensor.matmul(
                            ps[4][:, :], stat, wout_sb[:, jj, 0:T],
                            start=(jj == 0), stop=(jj == NCH - 1))
                        ins = tensor.matmul(
                            ps[5][:, :], stat, wout_sb[:, jj, T:D],
                            start=(jj == 0), stop=(jj == NCH - 1))
                    ins.then_inc(TPS.h, 1)
                tp_o = TPS.add(1)
                on("tensor", _po)

                ob = s % 2

                def _vo(vector, s=s, ob=ob, tp_o=tp_o, ve_cmb=ve_cmb):
                    vector.wait_ge(TPS.h, tp_o)
                    vector.wait_ge(VES.h, ve_cmb)
                    G.wait(vector, f"out{ob}")
                    vector.tensor_tensor(
                        outs3[ob][:, 0:T], comb3[:, s, 0:T], ps[4][:, :],
                        ALU.add)
                    vector.tensor_tensor(
                        outs3[ob][:, T:D], comb3[:, s, T:D], ps[5][:, :],
                        ALU.add).then_inc(VES.h, 1)
                ve_o = VES.add(1)
                on("vector", _vo)
                G.set("ps4", VES, ve_o)
                G.set("ps5", VES, ve_o)

                def _syo(sync, j=j, s=s, ob=ob, ve_o=ve_o):
                    sync.wait_ge(VES.h, ve_o)
                    qdma(sync, STS,
                         out=ap_out[j * T + s * 128:j * T + (s + 1) * 128, :],
                         in_=outs3[ob][:, :])
                on("sync", _syo)
                stn = STS.n
                G.set(f"out{ob}", STS, stn)

            G.set("xs3", VES, VES.n)
            G.set("ybf3", TPS, TPS.n)

        def _fin(sync):
            sync.wait_ge(STS.h, STS.n)
        on("sync", _fin)

        s3.close()
    return nc


def kernel(L0_attn_x, L0_attn_W_in, L0_attn_conv_w, L0_attn_conv_b,
           L0_attn_W_xproj, L0_attn_W_dt, L0_attn_b_dt, L0_attn_A_log,
           L0_attn_D_skip, L0_attn_W_out, L0_ffn_rhs):
    B, L, _ = L0_attn_x.shape
    LH = L // 2
    f32 = np.float32
    x_np = np.asarray(L0_attn_x, f32)
    w_in = np.asarray(L0_attn_W_in, f32)
    conv_w = np.asarray(L0_attn_conv_w, f32)
    conv_b = np.asarray(L0_attn_conv_b, f32)
    wxp = np.asarray(L0_attn_W_xproj, f32)
    wdt = np.asarray(L0_attn_W_dt, f32)
    bdt = np.asarray(L0_attn_b_dt, f32)
    alog = np.asarray(L0_attn_A_log, f32)
    dskip = np.asarray(L0_attn_D_skip, f32)
    wout = np.asarray(L0_attn_W_out, f32)
    rhs = np.asarray(L0_ffn_rhs, f32)

    wxp_pad = np.concatenate(
        [wxp, np.zeros((DI, 128 - wxp.shape[1]), f32)], axis=1)
    wdt_pad = np.concatenate([wdt, np.zeros((128 - wdt.shape[0], DI), f32)],
                             axis=0)

    in_maps = []
    for c in range(8):
        beta, phi = divmod(c, 2)
        sl = slice(phi * DIH, (phi + 1) * DIH)
        slp = slice((1 - phi) * DIH, (2 - phi) * DIH)
        xb = np.ascontiguousarray(x_np[beta])
        perm = np.concatenate([np.arange(phi * DIH, (phi + 1) * DIH),
                               np.arange((1 - phi) * DIH, (2 - phi) * DIH)])
        m = {
            "x": xb,
            "xown": np.ascontiguousarray(xb[phi * LH:(phi + 1) * LH]),
            "wix": np.ascontiguousarray(
                w_in[:, 0:DI][:, perm]).reshape(KD, 128, DI),
            "wiz": np.ascontiguousarray(
                w_in[:, DI + phi * DIH: DI + (phi + 1) * DIH]
            ).reshape(KD, 128, DIH),
            "convw": np.ascontiguousarray(
                conv_w[:, 0, :][perm].reshape(NCH, 128, 4).transpose(1, 0, 2)
            ).reshape(128, NCH * 4),
            "convb": np.ascontiguousarray(conv_b[perm].reshape(NCH, 128).T),
            "wxp": np.ascontiguousarray(
                wxp_pad[perm].reshape(NCH, 128, 128)),
            "wdt": np.ascontiguousarray(wdt_pad[:, sl]),
            "bdt": np.ascontiguousarray(bdt[sl].reshape(NCHH, 128).T),
            "alog": np.ascontiguousarray(
                alog[sl].reshape(NCHH, 128, NST).transpose(1, 0, 2)
            ).reshape(128, 128),
            "dskip": np.ascontiguousarray(dskip[sl].reshape(NCHH, 128).T),
            "wout": np.ascontiguousarray(
                np.concatenate([wout[sl], wout[slp]], axis=0)
            ).reshape(NCH, 128, D),
            "rhs": np.ascontiguousarray(rhs).reshape(KD, 128, D),
        }
        in_maps.append(m)

    nc = build_nc(L=L)
    res = run_bass_kernel_spmd(nc, in_maps, core_ids=list(range(8)))

    out = np.empty((B, L, D), np.float32)
    for c in range(8):
        beta, phi = divmod(c, 2)
        out[beta, phi * LH:(phi + 1) * LH, :] = res.results[c]["out_half"]
    return out


# version 16
# speedup vs baseline: 1.1046x; 1.1046x over previous
import numpy as np
import concourse.bass as bass
import concourse.mybir as mybir
from concourse.bass_utils import run_bass_kernel_spmd
from contextlib import ExitStack

F32 = mybir.dt.float32
BF16 = mybir.dt.bfloat16
AF = mybir.ActivationFunctionType
ALU = mybir.AluOpType
AX = mybir.AxisListType.X

D = 1024
DI = 2048
DIH = 1024
NST = 16
T = 512
NSUB = T // 128
KD = D // 128
NCH = DI // 128
NCHH = DIH // 128
RC = float(np.float32(1.5 * 2 ** 23))
EPS = 1e-5
I127 = float(np.float32(1.0 / 127.0))
GROUPS = [[0, 1], [2, 3], [4, 5], [6, 7]]


class Sem:
    def __init__(self, ctx, nc, name):
        self.h = ctx.enter_context(nc.semaphore(name))
        self.n = 0

    def add(self, k):
        self.n += k
        return self.n


class Guards:

    def __init__(self):
        self.g = {}

    def wait(self, eng, key):
        if key in self.g:
            s, v = self.g.pop(key)
            eng.wait_ge(s.h, v)

    def set(self, key, sem, val):
        self.g[key] = (sem, val)


def build_nc(L=8192, debug=False):
    nchunk = L // T
    nslot = nchunk // 2
    LH = L // 2

    nc = bass.Bass("TRN2", target_bir_lowering=False)
    dk = "ExternalOutput" if debug else "Internal"

    ap_x = nc.dram_tensor("x", [L, D], F32, kind="ExternalInput").ap()
    ap_xo = nc.dram_tensor("xown", [LH, D], F32, kind="ExternalInput").ap()
    ap_wix = nc.dram_tensor("wix", [KD, 128, DI], F32, kind="ExternalInput").ap()
    ap_wiz = nc.dram_tensor("wiz", [KD, 128, DIH], F32, kind="ExternalInput").ap()
    ap_cw = nc.dram_tensor("convw", [128, NCH * 4], F32, kind="ExternalInput").ap()
    ap_cb = nc.dram_tensor("convb", [128, NCH], F32, kind="ExternalInput").ap()
    ap_wxp = nc.dram_tensor("wxp", [NCH, 128, 128], F32, kind="ExternalInput").ap()
    ap_wdt = nc.dram_tensor("wdt", [128, DIH], F32, kind="ExternalInput").ap()
    ap_bdt = nc.dram_tensor("bdt", [128, NCHH], F32, kind="ExternalInput").ap()
    ap_al = nc.dram_tensor("alog", [128, 128], F32, kind="ExternalInput").ap()
    ap_ds = nc.dram_tensor("dskip", [128, NCHH], F32, kind="ExternalInput").ap()
    ap_wo = nc.dram_tensor("wout", [NCH, 128, D], F32, kind="ExternalInput").ap()
    ap_rhs = nc.dram_tensor("rhs", [KD, 128, D], F32, kind="ExternalInput").ap()

    ap_out = nc.dram_tensor("out_half", [LH, D], F32, kind="ExternalOutput").ap()

    d_delta = nc.dram_tensor("d_delta", [nchunk, 128, NCHH, T], F32, kind=dk).ap()
    d_u = nc.dram_tensor("d_u", [nchunk, 128, NCHH, T], F32, kind=dk).ap()
    d_sz = nc.dram_tensor("d_sz", [nchunk, 128, NCHH, T], F32, kind=dk).ap()
    d_bc = nc.dram_tensor("d_bc", [nchunk, 32, T], F32, kind=dk).ap()
    d_y = nc.dram_tensor("d_y", [nchunk, 128, NCHH * T], BF16, kind=dk).ap()
    d_qr = nc.dram_tensor("d_qr", [128, KD * D], BF16).ap()
    d_rs = nc.dram_tensor("d_rs", [D], F32).ap()
    d_y4p = nc.dram_tensor("d_y4p", [nslot, 128, NCHH * T], BF16).ap()
    d_yag = nc.dram_tensor("d_yag", [2, nslot, 128, NCHH * T], BF16).ap()
    d_yown = nc.dram_tensor("d_yown", [nslot, 128, NCHH * T], BF16).ap()
    d_ypeer = nc.dram_tensor("d_ypeer", [nslot, 128, NCHH * T], BF16).ap()

    ctx = ExitStack()
    with ctx:
        identb = ctx.enter_context(nc.sbuf_tensor("identb", [128, 128], BF16))
        identf = ctx.enter_context(nc.sbuf_tensor("identf", [128, 128], F32))
        a_sb = ctx.enter_context(nc.sbuf_tensor("a_sb", [128, 128], F32))
        cw_sb = ctx.enter_context(nc.sbuf_tensor("cw_sb", [128, NCH * 4], F32))
        cb_sb = ctx.enter_context(nc.sbuf_tensor("cb_sb", [128, NCH], F32))
        bdt_sb = ctx.enter_context(nc.sbuf_tensor("bdt_sb", [128, NCHH], F32))
        ds_sb = ctx.enter_context(nc.sbuf_tensor("ds_sb", [128, NCHH], F32))
        hcar = ctx.enter_context(nc.sbuf_tensor("hcar", [128, 128], F32))
        ones1 = ctx.enter_context(nc.sbuf_tensor("ones1", [1, 128], F32))
        rsb = ctx.enter_context(nc.sbuf_tensor("rsb", [128, D], F32))
        big8 = ctx.enter_context(nc.psum_tensor("big8", [128, 8 * T], F32))
        ps = [big8[:, i * T:(i + 1) * T] for i in range(8)]
        hps = big8[:, :].rearrange("p (a t) -> p a t", t=T)

        LDS = Sem(ctx, nc, "LDS")
        SPS = Sem(ctx, nc, "SPS")
        TPS = Sem(ctx, nc, "TPS")
        VES = Sem(ctx, nc, "VES")
        SCS = Sem(ctx, nc, "SCS")
        GPS = Sem(ctx, nc, "GPS")
        CCS = Sem(ctx, nc, "CCS")
        GPC = Sem(ctx, nc, "GPC")
        BCS = [Sem(ctx, nc, f"BCS{k}") for k in range(4)]
        LD2 = Sem(ctx, nc, "LD2")
        LD3 = Sem(ctx, nc, "LD3")
        STS = Sem(ctx, nc, "STS")

        block = ctx.enter_context(nc.Block())
        G = Guards()

        def on(eng, fn):
            getattr(block, eng)(fn)

        _dyn = {}

        def gdyn(g, key):
            if "phi" not in _dyn:
                _dyn["phi"] = g.partition_id() % 2
            phi = _dyn["phi"]
            if key not in _dyn:
                if key == "uoff":
                    _dyn[key] = phi * NCHH
                elif key == "own":
                    _dyn[key] = phi * nslot
                elif key == "peer":
                    _dyn[key] = (1 - phi) * nslot
                elif key == "peerag":
                    _dyn[key] = 1 - phi
            return _dyn[key]

        def qdma(eng, sem, **kw):
            eng.wait_ge(sem.h, sem.n)
            sem.n += 16
            return eng.dma_start(**kw).then_inc(sem.h, 16)

        RSQRT_MAGIC = 0x5F3759DF

        def emit_ln(xb, xn_out, st, sqscr, ld_wait):
            def _v1(vector):
                def vch(ins):
                    VES.n += 1
                    ins.then_inc(VES.h, 1)
                    vector.wait_ge(VES.h, VES.n)

                vector.wait_ge(LDS.h, ld_wait)
                for s in range(NSUB):
                    vector.tensor_reduce(
                        st["xsum"][:, s:s + 1], xb[:, s, :], AX, ALU.add)
                    vch(vector.tensor_tensor(
                        sqscr[:, :], xb[:, s, :], xb[:, s, :], ALU.mult))
                    vch(vector.tensor_reduce(
                        st["xsq"][:, s:s + 1], sqscr[:, :], AX, ALU.add))
                vch(vector.tensor_scalar(
                    st["mcol"][:, :], st["xsum"][:, :], 1.0 / D, None, ALU.mult))
                vch(vector.tensor_tensor(
                    st["sc1"][:, :], st["mcol"][:, :], st["mcol"][:, :],
                    ALU.mult))
                vch(vector.scalar_tensor_tensor(
                    st["varc"][:, :], st["xsq"][:, :], 1.0 / D,
                    st["sc1"][:, :], ALU.mult, ALU.subtract))
                vch(vector.tensor_scalar(
                    st["veps"][:, :], st["varc"][:, :], EPS, None, ALU.add))
                iv = st["veps"][:, :].bitcast(mybir.dt.int32)
                io = st["rstd"][:, :].bitcast(mybir.dt.int32)
                vch(vector.tensor_scalar(
                    io, iv, 1, None, ALU.arith_shift_right))
                vch(vector.tensor_scalar(
                    io, io, -1, RSQRT_MAGIC, ALU.mult, ALU.add))
                for _ in range(3):
                    vch(vector.tensor_tensor(
                        st["sc1"][:, :], st["rstd"][:, :], st["rstd"][:, :],
                        ALU.mult))
                    vch(vector.tensor_tensor(
                        st["sc2"][:, :], st["veps"][:, :], st["sc1"][:, :],
                        ALU.mult))
                    vch(vector.tensor_scalar(
                        st["sc2"][:, :], st["sc2"][:, :], -0.5, 1.5,
                        ALU.mult, ALU.add))
                    vch(vector.tensor_tensor(
                        st["rstd"][:, :], st["rstd"][:, :], st["sc2"][:, :],
                        ALU.mult))
                ins = None
                for s in range(NSUB):
                    ins = vector.tensor_scalar(
                        xn_out[:, s, :], xb[:, s, :], st["mcol"][:, s:s + 1],
                        st["rstd"][:, s:s + 1], ALU.subtract, ALU.mult)
                ins.then_inc(VES.h, 1)
            on("vector", _v1)
            VES.add(1)
            return VES.n

        s1 = ExitStack()
        wix_sb = s1.enter_context(nc.sbuf_tensor("wix_sb", [128, KD, DI], BF16))
        wiz_sb = s1.enter_context(nc.sbuf_tensor("wiz_sb", [128, KD, DIH], BF16))
        wxp_sb = s1.enter_context(nc.sbuf_tensor("wxp_sb", [128, NCH, 128], BF16))
        wdt_sb = s1.enter_context(nc.sbuf_tensor("wdt_sb", [128, DIH], BF16))
        diag_sb = s1.enter_context(
            nc.sbuf_tensor("diag_sb", [128, NCH * 4, 128], BF16))

        pro = ExitStack()
        stgA = pro.enter_context(nc.sbuf_tensor("stgA", [128, KD, D], F32))
        stgB = pro.enter_context(nc.sbuf_tensor("stgB", [128, KD, D], F32))
        qrT = pro.enter_context(nc.sbuf_tensor("qrT", [128, KD, D], BF16))
        qrS = pro.enter_context(nc.sbuf_tensor("qrS", [128, KD, D], BF16))
        ram8 = pro.enter_context(nc.sbuf_tensor("ram8", [128, KD], F32))
        rs8 = pro.enter_context(nc.sbuf_tensor("rs8", [128, KD], F32))
        rrs8 = pro.enter_context(nc.sbuf_tensor("rrs8", [128, KD], F32))
        qtmp = pro.enter_context(nc.sbuf_tensor("qtmp", [128, D], F32))

        def _sy0(sync):
            qdma(sync, LDS, out=cw_sb[:, :], in_=ap_cw)
            qdma(sync, LDS, out=cb_sb[:, :], in_=ap_cb)
            qdma(sync, LDS, out=bdt_sb[:, :], in_=ap_bdt)
            qdma(sync, LDS, out=ds_sb[:, :], in_=ap_ds)
            qdma(sync, LDS, out=a_sb[:, :], in_=ap_al)
            qdma(sync, LDS, out=stgA[:, :, :],
                 in_=ap_rhs.rearrange("k p e -> p k e"))
        on("sync", _sy0)
        ld_c = LDS.n

        def _gp0(gpsimd):
            for idt in (identb, identf):
                GPC.n += 1
                gpsimd.memset(idt[:, :], 0.0).then_inc(GPC.h, 1)
                gpsimd.wait_ge(GPC.h, GPC.n)
                GPC.n += 1
                gpsimd.affine_select(
                    out=idt[:, :], in_=idt[:, :],
                    compare_op=ALU.not_equal, fill=1.0, base=0,
                    pattern=[[-1, 128]], channel_multiplier=1
                ).then_inc(GPC.h, 1)
                gpsimd.wait_ge(GPC.h, GPC.n)
            GPC.n += 1
            gpsimd.memset(hcar[:, :], 0.0).then_inc(GPC.h, 1)
            GPC.n += 1
            gpsimd.memset(ones1[0:1, :], 1.0).then_inc(GPC.h, 1)
        on("gpsimd", _gp0)
        gp_id = GPC.n

        def _sc0(scalar):
            scalar.wait_ge(LDS.h, ld_c)
            scalar.activation(a_sb[:, :], a_sb[:, :], AF.Exp).then_inc(SCS.h, 1)
        sc_a = SCS.add(1)
        on("scalar", _sc0)

        def _v0(vector):
            vector.wait_ge(SCS.h, sc_a)
            vector.wait_ge(GPC.h, gp_id)
            vector.wait_ge(LDS.h, ld_c)
            vector.tensor_scalar(a_sb[:, :], a_sb[:, :], -1.0, None, ALU.mult)
            ins = None
            for jk in range(NCH * 4):
                ins = vector.tensor_scalar(
                    diag_sb[:, jk, :], identb[:, :], cw_sb[:, jk:jk + 1],
                    None, ALU.mult)
            ins.then_inc(VES.h, 1)
        ve_diag = VES.add(1)
        on("vector", _v0)

        def tr_block(src_ap, dst_fn, wait_fn=None, ident=None, bf=False):
            for idx in range(KD * KD):
                k, ec = divmod(idx, KD)
                bk = idx % 2

                def _pe(tensor, k=k, ec=ec, bk=bk, first=(idx == 0)):
                    if first and wait_fn is not None:
                        wait_fn(tensor)
                    G.wait(tensor, f"ps{bk}")
                    pv = (ps[bk][:, 0:64].bitcast(BF16) if bf
                          else ps[bk][:, 0:128])
                    tensor.transpose(
                        pv, src_ap(k, ec), ident[:, :]).then_inc(TPS.h, 1)
                tp = TPS.add(1)
                on("tensor", _pe)

                def _v(vector, k=k, ec=ec, bk=bk, tp=tp):
                    vector.wait_ge(TPS.h, tp)
                    pv = (ps[bk][:, 0:64].bitcast(BF16) if bf
                          else ps[bk][:, 0:128])
                    vector.tensor_copy(dst_fn(k, ec), pv).then_inc(VES.h, 1)
                ve = VES.add(1)
                on("vector", _v)
                G.set(f"ps{bk}", VES, ve)

        def _w_rhs(tensor):
            tensor.wait_ge(LDS.h, ld_c)
            tensor.wait_ge(VES.h, ve_diag)

        tr_block(lambda k, ec: stgA[:, k, ec * 128:(ec + 1) * 128],
                 lambda k, ec: stgB[:, ec, k * 128:(k + 1) * 128],
                 wait_fn=_w_rhs, ident=identf)
        ve_rT = VES.n

        def _v1(vector):
            def vch(ins):
                VES.n += 1
                ins.then_inc(VES.h, 1)
                vector.wait_ge(VES.h, VES.n)

            vector.wait_ge(VES.h, ve_rT)
            ins = None
            for ec in range(KD):
                ins = vector.tensor_reduce(
                    ram8[:, ec:ec + 1], stgB[:, ec, :], AX, ALU.max,
                    apply_absolute_value=True)
            vch(ins)
            vch(vector.tensor_scalar(
                rs8[:, :], ram8[:, :], 1e-12, I127, ALU.max, ALU.mult))
            vch(vector.reciprocal(rrs8[:, :], rs8[:, :]))
            ins = None
            for ec in range(KD):
                vch(vector.tensor_scalar(
                    qtmp[:, :], stgB[:, ec, :], rrs8[:, ec:ec + 1], None,
                    ALU.mult))
                ins = vector.tensor_scalar(
                    qrT[:, ec, :], qtmp[:, :], RC, -RC, ALU.add, ALU.add)
                if ec < KD - 1:
                    vch(ins)
            ins.then_inc(VES.h, 1)
        on("vector", _v1)
        ve_q = VES.add(1)

        def _sy1(sync):
            sync.wait_ge(VES.h, ve_q)
            for ec in range(KD):
                qdma(sync, SPS, out=d_rs[ec * 128:(ec + 1) * 128],
                     in_=rs8[:, ec:ec + 1])
        on("sync", _sy1)
        sp_rs = SPS.n

        def _sy2(sync):
            sync.wait_ge(SPS.h, sp_rs)
            qdma(sync, LDS, out=rsb[:, :], in_=d_rs.partition_broadcast(128))
        on("sync", _sy2)

        def _w_q(tensor):
            tensor.wait_ge(VES.h, ve_q)

        tr_block(lambda k, ec: qrT[:, ec, k * 128:(k + 1) * 128],
                 lambda k, ec: qrS[:, k, ec * 128:(ec + 1) * 128],
                 wait_fn=_w_q, ident=identb, bf=True)
        ve_qS = VES.n

        def _sy3(sync):
            sync.wait_ge(VES.h, ve_qS)
            qdma(sync, SPS, out=d_qr,
                 in_=qrS[:, :, :].rearrange("p k e -> p (k e)"))
        on("sync", _sy3)

        def ld_cast(src_ap, stg_view, dst_view, stg_guard_key):
            def _s(sync):
                G.wait(sync, stg_guard_key)
                qdma(sync, LDS, out=stg_view, in_=src_ap)
            on("sync", _s)
            ldn = LDS.n

            def _v(vector, ldn=ldn):
                vector.wait_ge(LDS.h, ldn)
                vector.tensor_copy(dst_view, stg_view).then_inc(VES.h, 1)
            ve = VES.add(1)
            on("vector", _v)
            G.set(stg_guard_key, VES, ve)

        sAf = stgA[:, :, :].rearrange("p k e -> p (k e)")
        sBf = stgB[:, :, :].rearrange("p k e -> p (k e)")
        G.set("stgA", VES, ve_rT)
        G.set("stgB", VES, ve_q)
        for h in range(2):
            ld_cast(ap_wix[:, :, h * 1024:(h + 1) * 1024].rearrange(
                        "k p c -> p k c"),
                    stgA[:, :, :],
                    wix_sb[:, :, h * 1024:(h + 1) * 1024], "stgA")
        ld_cast(ap_wiz.rearrange("k p c -> p k c"), stgB[:, :, :],
                wiz_sb[:, :, :], "stgB")
        ld_cast(ap_wxp.rearrange("j p c -> p j c"),
                sAf[:, 0:NCH * 128].rearrange("p (j c) -> p j c", c=128),
                wxp_sb[:, :, :], "stgA")
        ld_cast(ap_wdt, sBf[:, 0:DIH], wdt_sb[:, :], "stgB")
        G.g.pop("stgA", None)
        G.g.pop("stgB", None)

        ve_pro, sp_pro, ld_pro, tp_pro, sc_pro = VES.n, SPS.n, LDS.n, TPS.n, SCS.n

        def _bar(eng):
            eng.wait_ge(VES.h, ve_pro)
            eng.wait_ge(SPS.h, sp_pro)
            eng.wait_ge(LDS.h, ld_pro)
            eng.wait_ge(TPS.h, tp_pro)
            eng.wait_ge(SCS.h, sc_pro)
            eng.wait_ge(GPC.h, gp_id)
        for en in ("sync", "vector", "scalar", "tensor", "gpsimd"):
            on(en, _bar)
        pro.close()

        s1w = ExitStack()
        xsb = [s1w.enter_context(nc.sbuf_tensor(f"xsb{b}", [128, NSUB, D], F32))
               for b in range(2)]
        xnb = s1w.enter_context(nc.sbuf_tensor("xnb", [128, NSUB, D], BF16))
        xnT = s1w.enter_context(nc.sbuf_tensor("xnT", [128, KD, T], BF16))
        xinT = s1w.enter_context(nc.sbuf_tensor("xinT", [128, NCH, T + 3], BF16))
        ubf = s1w.enter_context(nc.sbuf_tensor("ubf", [128, NCH, T], BF16))
        xdT = s1w.enter_context(nc.sbuf_tensor("xdT", [128, T], BF16))
        bcst = [s1w.enter_context(nc.sbuf_tensor(f"bcst{b}", [128, T], F32))
                for b in range(2)]
        et8 = s1w.enter_context(nc.sbuf_tensor("et8", [128, NCHH, T], F32))
        szst = [s1w.enter_context(nc.sbuf_tensor(f"szst{b}", [128, T], F32))
                for b in range(2)]
        ufst = [s1w.enter_context(nc.sbuf_tensor(f"ufst{b}", [128, T], F32))
                for b in range(2)]
        sqscr = s1w.enter_context(nc.sbuf_tensor("sqscr", [128, D], F32))
        st4 = {n: s1w.enter_context(nc.sbuf_tensor("st_" + n, [128, NSUB], F32))
               for n in ("xsum", "xsq", "mcol", "varc", "veps", "rstd",
                         "sc1", "sc2")}

        def _gpz(gpsimd):
            GPC.n += 1
            gpsimd.memset(xinT[:, :, 0:3], 0.0).then_inc(GPC.h, 1)
        on("gpsimd", _gpz)
        gp_h0 = GPC.n

        xfree = {}
        szst_free = [0, 0]
        ufst_free = [0, 0]
        et_free = [0] * NCHH
        for i in range(nchunk):
            xb = xsb[i % 2]

            def _sx(sync, i=i):
                if i >= 2:
                    s, v = xfree[i - 2]
                    sync.wait_ge(s.h, v)
                qdma(sync, LDS, out=xsb[i % 2][:, :, :],
                     in_=ap_x[i * T:(i + 1) * T, :].rearrange(
                         "(s p) d -> p s d", p=128))
            on("sync", _sx)
            ld_x = LDS.n

            ve_xn1 = emit_ln(xb, xb, st4, sqscr, ld_x)

            def _vcast(vector, xb=xb, ve_xn1=ve_xn1):
                vector.wait_ge(VES.h, ve_xn1)
                vector.tensor_copy(
                    xnb[:, :, :].rearrange("p s d -> p (s d)"),
                    xb[:, :, :].rearrange("p s d -> p (s d)")
                ).then_inc(VES.h, 1)
            ve_xnb = VES.add(1)
            on("vector", _vcast)
            xfree[i] = (VES, ve_xnb)

            for j in range(KD):
                bk = j % 2

                def _pt(tensor, j=j, bk=bk, ve_xnb=ve_xnb, i=i):
                    if j == 0:
                        tensor.wait_ge(VES.h, ve_xnb)
                        if i == 0:
                            tensor.wait_ge(GPC.h, gp_h0)
                    G.wait(tensor, f"ps{bk}")
                    pv = ps[bk][:, 0:256].bitcast(BF16)
                    ins = None
                    for s in range(NSUB):
                        ins = tensor.transpose(
                            pv[:, s * 128:(s + 1) * 128],
                            xnb[:, s, j * 128:(j + 1) * 128], identb[:, :])
                    ins.then_inc(TPS.h, 1)
                tp = TPS.add(1)
                on("tensor", _pt)

                def _vt(vector, j=j, bk=bk, tp=tp):
                    vector.wait_ge(TPS.h, tp)
                    vector.tensor_copy(
                        xnT[:, j, :], ps[bk][:, 0:256].bitcast(BF16)
                    ).then_inc(VES.h, 1)
                ve = VES.add(1)
                on("vector", _vt)
                G.set(f"ps{bk}", VES, ve)
            ve_xnT = VES.n

            sc_zu = SCS.n
            sps_zu = SPS.n
            ve_xin_evac = {}
            tp_w = {}
            for m in range(NCH + NCHH):
                bk = 4 + m % 4

                def _pw(tensor, m=m, bk=bk, ve_xnT=ve_xnT):
                    if m == 0:
                        tensor.wait_ge(VES.h, ve_xnT)
                    G.wait(tensor, f"ps{bk}")
                    ins = None
                    for k in range(KD):
                        if m < NCH:
                            stat = wix_sb[:, k, m * 128:(m + 1) * 128]
                        else:
                            stat = wiz_sb[:, k, (m - NCH) * 128:(m - NCH + 1) * 128]
                        ins = tensor.matmul(
                            ps[bk][:, :], stat, xnT[:, k, :],
                            start=(k == 0), stop=(k == KD - 1))
                    ins.then_inc(TPS.h, 1)
                tp_w[m] = TPS.add(1)
                on("tensor", _pw)

                if m < NCH:
                    def _vw(vector, m=m, bk=bk, tp=tp_w[m]):
                        vector.wait_ge(TPS.h, tp)
                        vector.tensor_copy(
                            xinT[:, m, 3:T + 3], ps[bk][:, :]).then_inc(VES.h, 1)
                    ve = VES.add(1)
                    on("vector", _vw)
                    G.set(f"ps{bk}", VES, ve)
                    ve_xin_evac[m] = ve
                else:
                    G.set(f"ps{bk}", SCS, sc_zu + (m - NCH) + 1)

            tp_c = {}
            for j in range(NCH):
                bk = j % 2

                def _pc(tensor, j=j, bk=bk, ve_evj=ve_xin_evac[j]):
                    tensor.wait_ge(VES.h, ve_evj)
                    G.wait(tensor, f"ps{bk}")
                    ins = None
                    for k in range(4):
                        ins = tensor.matmul(
                            ps[bk][:, :], diag_sb[:, j * 4 + k, :],
                            xinT[:, j, k:k + T], start=(k == 0), stop=(k == 3))
                    ins.then_inc(TPS.h, 1)
                tp_c[j] = TPS.add(1)
                on("tensor", _pc)
                G.set(f"ps{bk}", SCS, sc_zu + NCHH + j + 1)

            def _szu(scalar, i=i, tp_w=dict(tp_w), tp_c=dict(tp_c),
                     sc_zu=sc_zu, sps_zu=sps_zu, szf=tuple(szst_free),
                     uff=tuple(ufst_free)):
                for zc in range(NCHH):
                    scalar.wait_ge(TPS.h, tp_w[NCH + zc])
                    if zc >= 2:
                        scalar.wait_ge(SPS.h, sps_zu + (zc - 1) * 16)
                    elif szf[zc % 2]:
                        scalar.wait_ge(SPS.h, szf[zc % 2])
                    SCS.n += 1
                    scalar.activation(
                        szst[zc % 2][:, :], ps[4 + zc % 4][:, :], AF.Silu
                    ).then_inc(SCS.h, 1)
                for j in range(NCH):
                    scalar.wait_ge(TPS.h, tp_c[j])
                    if j < NCHH:
                        if j >= 2:
                            scalar.wait_ge(SPS.h, sps_zu + 128 + (j - 1) * 16)
                        elif uff[j % 2]:
                            scalar.wait_ge(SPS.h, uff[j % 2])
                        scalar.activation(
                            ubf[:, j, :], ps[j % 2][:, :], AF.Silu,
                            bias=cb_sb[:, j:j + 1])
                        SCS.n += 1
                        scalar.activation(
                            ufst[j % 2][:, :], ps[j % 2][:, :], AF.Silu,
                            bias=cb_sb[:, j:j + 1]).then_inc(SCS.h, 1)
                    else:
                        SCS.n += 1
                        scalar.activation(
                            ubf[:, j, :], ps[j % 2][:, :], AF.Silu,
                            bias=cb_sb[:, j:j + 1]).then_inc(SCS.h, 1)
            on("scalar", _szu)
            assert SCS.n == sc_zu + NCHH + NCH

            for zc in range(NCHH):
                def _syz(sync, i=i, zc=zc, sc=sc_zu + zc + 1):
                    sync.wait_ge(SCS.h, sc)
                    qdma(sync, SPS, out=d_sz[i, :, zc, :],
                         in_=szst[zc % 2][:, :])
                on("sync", _syz)
            for j in range(NCHH):
                def _syu(sync, i=i, j=j, sc=sc_zu + NCHH + j + 1):
                    sync.wait_ge(SCS.h, sc)
                    qdma(sync, SPS, out=d_u[i, :, j, :], in_=ufst[j % 2][:, :])
                on("sync", _syu)
            assert SPS.n == sps_zu + 256
            szst_free[0] = sps_zu + 7 * 16
            szst_free[1] = sps_zu + 8 * 16
            ufst_free[0] = sps_zu + 128 + 7 * 16
            ufst_free[1] = sps_zu + 128 + 8 * 16
            sc_u_all = SCS.n

            def _pxd(tensor, sc_u_all=sc_u_all):
                tensor.wait_ge(SCS.h, sc_u_all)
                G.wait(tensor, "ps6")
                ins = None
                for k in range(NCH):
                    ins = tensor.matmul(
                        ps[6][:, :], wxp_sb[:, k, :], ubf[:, k, :],
                        start=(k == 0), stop=(k == NCH - 1))
                ins.then_inc(TPS.h, 1)
            tp_xd = TPS.add(1)
            on("tensor", _pxd)

            bcb = i % 2

            def _vxd(vector, bcb=bcb, tp_xd=tp_xd):
                vector.wait_ge(TPS.h, tp_xd)
                G.wait(vector, f"bcst{bcb}")
                vector.tensor_copy(xdT[:, :], ps[6][:, :])
                vector.tensor_copy(
                    bcst[bcb][64:96, :], ps[6][64:96, :]).then_inc(VES.h, 1)
            ve_xd = VES.add(1)
            on("vector", _vxd)
            G.set("ps6", VES, ve_xd)

            def _sybc(sync, i=i, bcb=bcb, ve_xd=ve_xd):
                sync.wait_ge(VES.h, ve_xd)
                qdma(sync, SPS, out=d_bc[i, :, :], in_=bcst[bcb][64:96, :])
            on("sync", _sybc)
            sp = SPS.n
            G.set(f"bcst{bcb}", SPS, sp)

            scd = SCS.n
            tp_d = {}
            for dh in range(NCHH):
                bk = 2 + dh % 2

                def _pd(tensor, dh=dh, bk=bk, ve_xd=ve_xd):
                    if dh == 0:
                        tensor.wait_ge(VES.h, ve_xd)
                    G.wait(tensor, f"ps{bk}")
                    tensor.matmul(
                        ps[bk][:, :], wdt_sb[:, dh * 128:(dh + 1) * 128],
                        xdT[:, :], start=True, stop=True).then_inc(TPS.h, 1)
                tp_d[dh] = TPS.add(1)
                on("tensor", _pd)
                G.set(f"ps{bk}", SCS, scd + dh + 1)

            def _sdel(scalar, i=i, tp_d=dict(tp_d), scd=scd,
                      etf=tuple(et_free)):
                for dh in range(NCHH):
                    scalar.wait_ge(TPS.h, tp_d[dh])
                    if etf[dh]:
                        scalar.wait_ge(SPS.h, etf[dh])
                    SCS.n += 1
                    scalar.activation(
                        et8[:, dh, :], ps[2 + dh % 2][:, :], AF.Exp,
                        bias=bdt_sb[:, dh:dh + 1]).then_inc(SCS.h, 1)
                scalar.wait_ge(SCS.h, scd + NCHH)
                for dh in range(NCHH):
                    SCS.n += 1
                    scalar.activation(
                        et8[:, dh, :], et8[:, dh, :], AF.Ln, bias=1.0
                    ).then_inc(SCS.h, 1)
            on("scalar", _sdel)
            assert SCS.n == scd + 2 * NCHH

            for dh in range(NCHH):
                def _syd(sync, i=i, dh=dh, lnv=scd + NCHH + dh + 1):
                    sync.wait_ge(SCS.h, lnv)
                    qdma(sync, SPS, out=d_delta[i, :, dh, :],
                         in_=et8[:, dh, :])
                on("sync", _syd)
                et_free[dh] = SPS.n

            tp_now = TPS.n

            ve_xin_last = ve_xin_evac[NCH - 1]

            def _vh(vector, tp_now=tp_now, ve_xin_last=ve_xin_last):
                vector.wait_ge(TPS.h, tp_now)
                vector.wait_ge(VES.h, ve_xin_last)
                vector.tensor_copy(
                    xinT[:, :, 0:3], xinT[:, :, T:T + 3]).then_inc(VES.h, 1)
            VES.add(1)
            on("vector", _vh)

        m1 = (VES.n, SPS.n, LDS.n, TPS.n, SCS.n, GPS.n)

        def _bar1(eng, m1=m1):
            eng.wait_ge(VES.h, m1[0])
            eng.wait_ge(SPS.h, m1[1])
            eng.wait_ge(LDS.h, m1[2])
            eng.wait_ge(TPS.h, m1[3])
            eng.wait_ge(SCS.h, m1[4])
            eng.wait_ge(GPS.h, m1[5])
        for en in ("sync", "vector", "scalar", "tensor", "gpsimd"):
            on(en, _bar1)
        s1w.close()
        s1.close()

        s2 = ExitStack()
        delta_a = s2.enter_context(nc.sbuf_tensor("delta_a", [128, NCHH * T], F32))
        u_a = s2.enter_context(nc.sbuf_tensor("u_a", [128, NCHH * T], F32))
        sz_a = s2.enter_context(nc.sbuf_tensor("sz_a", [128, NCHH * T], F32))
        dtu_a = s2.enter_context(nc.sbuf_tensor("dtu_a", [128, NCHH * T], F32))
        yacc = [s2.enter_context(nc.sbuf_tensor(f"yacc{b}", [128, NCHH * T], F32))
                for b in range(2)]
        yp_blk = [s2.enter_context(nc.sbuf_tensor(f"yp_blk{b}", [128, NCHH * T], F32))
                  for b in range(2)]
        dbu_b = s2.enter_context(nc.sbuf_tensor("dbu_b", [128, NCHH * T], F32))
        dAb = [s2.enter_context(nc.sbuf_tensor(f"dAb{b}", [128, T], F32))
               for b in range(4)]
        Bb = [s2.enter_context(nc.sbuf_tensor(f"Bb{b}", [128, T], F32))
              for b in range(2)]
        Cb = [s2.enter_context(nc.sbuf_tensor(f"Cb{b}", [128, T], F32))
              for b in range(2)]
        ybf = s2.enter_context(nc.sbuf_tensor("ybf", [128, NCHH * T], BF16))

        prev_vn_base = None
        prev_tts_last = None
        prev_tts_last2 = None

        def bc_load(i, n):
            nb = n % 2

            def _s(sync, i=i, n=n, nb=nb):
                G.wait(sync, f"Bb{nb}")
                qdma(sync, BCS[nb], out=Bb[nb][:, :],
                     in_=d_bc[i, n].unsqueeze(0).partition_broadcast(128))
                G.wait(sync, f"Cb{nb}")
                qdma(sync, BCS[2 + nb], out=Cb[nb][:, :],
                     in_=d_bc[i, 16 + n].unsqueeze(0).partition_broadcast(128))
            on("sync", _s)
            return (BCS[nb].n, BCS[2 + nb].n)

        for i in range(nchunk):
            def _sl(sync, i=i):
                G.wait(sync, "delta_a")
                qdma(sync, LDS, out=delta_a[:, :],
                     in_=d_delta[i].rearrange("p a t -> p (a t)"))
                G.wait(sync, "sz_a")
                qdma(sync, LD2, out=sz_a[:, :],
                     in_=d_sz[i].rearrange("p a t -> p (a t)"))
                G.wait(sync, "u_a")
                qdma(sync, LD3, out=u_a[:, :],
                     in_=d_u[i].rearrange("p a t -> p (a t)"))
            on("sync", _sl)
            ld_ds = LDS.n
            gp_u = LD3.n
            ld_sz = LD2.n

            def _vdtu(vector, ld_ds=ld_ds, gp_u=gp_u):
                vector.wait_ge(LDS.h, ld_ds)
                vector.wait_ge(LD3.h, gp_u)
                G.wait(vector, "dtu_a")
                vector.tensor_tensor(
                    dtu_a[:, :], delta_a[:, :], u_a[:, :], ALU.mult
                ).then_inc(VES.h, 1)
            ve_dtu = VES.add(1)
            on("vector", _vdtu)
            ve_dtu_g = [ve_dtu]

            if i == 0:
                bc_ready = [bc_load(0, 0), bc_load(0, 1)]

            gpc_carry_prev = GPC.n
            for n in range(NST):
                nb = n % 2
                vn_base = VES.n
                sc_bc = bc_ready[n]

                sc_base = SCS.n

                def _se(scalar, i=i, n=n, ld_ds=ld_ds, vn_base=vn_base,
                        prev_vn=prev_vn_base):
                    for dh in range(NCHH):
                        if dh == 0 and n == 0:
                            scalar.wait_ge(LDS.h, ld_ds)
                        if dh >= 4:
                            fa = vn_base + (dh - 4) + 2
                        elif prev_vn is not None:
                            fa = prev_vn + (dh + 4) + 2
                        else:
                            fa = 0
                        if fa:
                            scalar.wait_ge(VES.h, fa)
                        SCS.n += 1
                        scalar.activation(
                            dAb[dh % 4][:, :], delta_a[:, dh * T:(dh + 1) * T],
                            AF.Exp, scale=a_sb[:, dh * 16 + n:dh * 16 + n + 1]
                        ).then_inc(SCS.h, 1)
                on("scalar", _se)
                sc_base2 = sc_base

                def _vn(vector, i=i, n=n, nb=nb, vn_base=vn_base,
                        sc_bc=sc_bc, sc_base=sc_base2, ve_dtu=ve_dtu,
                        gpcw=gpc_carry_prev):
                    if n == 0:
                        vector.wait_ge(VES.h, ve_dtu)
                    vector.wait_ge(VES.h, vn_base)
                    vector.wait_ge(BCS[nb].h, sc_bc[0])
                    vector.wait_ge(BCS[2 + nb].h, sc_bc[1])
                    VES.n += 1
                    vector.tensor_tensor(
                        dbu_b[:, :].rearrange("p (a t) -> p a t", t=T),
                        dtu_a[:, :].rearrange("p (a t) -> p a t", t=T),
                        Bb[nb][:, :].unsqueeze(1).broadcast_to([128, NCHH, T]),
                        ALU.mult).then_inc(VES.h, 1)
                    vector.wait_ge(VES.h, VES.n)
                    for dh in range(NCHH):
                        vector.wait_ge(SCS.h, sc_base + dh + 1)
                        VES.n += 1
                        vector.tensor_tensor_scan(
                            hps[:, dh, :], dAb[dh % 4][:, :],
                            dbu_b[:, dh * T:(dh + 1) * T],
                            hcar[:, n * 8 + dh:n * 8 + dh + 1],
                            ALU.mult, ALU.add).then_inc(VES.h, 1)
                    vector.wait_ge(VES.h, VES.n)
                    vector.wait_ge(GPC.h, gpcw)
                    VES.n += 1
                    if n == 0:
                        vector.tensor_tensor(
                            yacc[0][:, :].rearrange("p (a t) -> p a t", t=T),
                            hps[:, :, :],
                            Cb[nb][:, :].unsqueeze(1).broadcast_to(
                                [128, NCHH, T]), ALU.mult).then_inc(VES.h, 1)
                    else:
                        vector.tensor_tensor(
                            yp_blk[nb][:, :].rearrange("p (a t) -> p a t", t=T),
                            hps[:, :, :],
                            Cb[nb][:, :].unsqueeze(1).broadcast_to(
                                [128, NCHH, T]), ALU.mult).then_inc(VES.h, 1)
                on("vector", _vn)
                tts_last = vn_base + 1 + NCHH
                ve_y = VES.n
                G.set(f"Bb{nb}", VES, vn_base + 1)
                G.set(f"Cb{nb}", VES, ve_y)

                def _sc(scalar, n=n, vn_base=vn_base):
                    for dh in range(NCHH):
                        scalar.wait_ge(VES.h, vn_base + 2 + dh)
                        SCS.n += 1
                        scalar.copy(
                            hcar[:, n * 8 + dh:n * 8 + dh + 1],
                            big8[:, dh * T + T - 1:dh * T + T]
                        ).then_inc(SCS.h, 1)
                on("scalar", _sc)

                gpc_carry_prev = GPC.n
                if n >= 2:
                    def _gp(gpsimd, pn=n - 1, gate=vn_base + 1 + NCHH):
                        gpsimd.wait_ge(VES.h, gate)
                        gpsimd.wait_ge(GPC.h, GPC.n)
                        GPC.n += 1
                        gpsimd.tensor_tensor(
                            yacc[pn % 2][:, :], yacc[1 - pn % 2][:, :],
                            yp_blk[pn % 2][:, :], ALU.add).then_inc(GPC.h, 1)
                    on("gpsimd", _gp)
                if n == NST - 1:
                    def _gpl(gpsimd, n=n, ve_y=ve_y):
                        gpsimd.wait_ge(VES.h, ve_y)
                        gpsimd.wait_ge(GPC.h, GPC.n)
                        GPC.n += 1
                        gpsimd.tensor_tensor(
                            yacc[n % 2][:, :], yacc[1 - n % 2][:, :],
                            yp_blk[n % 2][:, :], ALU.add).then_inc(GPC.h, 1)
                    on("gpsimd", _gpl)
                gpc_add_last = GPC.n

                if n + 2 < NST:
                    bc_ready.append(bc_load(i, n + 2))
                elif i + 1 < nchunk:
                    bc_ready.append(bc_load(i + 1, n + 2 - NST))
                else:
                    bc_ready.append(None)

                prev_vn_base = vn_base
            bc_ready = bc_ready[NST:]

            def _vyf(vector, i=i, ld_sz=ld_sz, gpl=gpc_add_last):
                vector.wait_ge(VES.h, VES.n)
                vector.wait_ge(GPC.h, gpl)
                vector.wait_ge(LD2.h, ld_sz)
                ins = None
                for dh in range(NCHH):
                    ins = vector.scalar_tensor_tensor(
                        yp_blk[0][:, dh * T:(dh + 1) * T],
                        u_a[:, dh * T:(dh + 1) * T],
                        ds_sb[:, dh:dh + 1],
                        yacc[(NST - 1) % 2][:, dh * T:(dh + 1) * T],
                        ALU.mult, ALU.add)
                VES.n += 1
                ins.then_inc(VES.h, 1)
                vector.wait_ge(VES.h, VES.n)
                G.wait(vector, "ybf")
                vector.tensor_tensor(
                    ybf[:, :], yp_blk[0][:, :], sz_a[:, :],
                    ALU.mult).then_inc(VES.h, 1)
            on("vector", _vyf)
            ve_yf = VES.add(1)
            G.set("delta_a", SCS, SCS.n)
            G.set("u_a", VES, ve_yf)
            G.set("sz_a", VES, ve_yf)
            G.set("dtu_a", VES, ve_yf)

            def _syy(sync, i=i, ve_yf=ve_yf):
                sync.wait_ge(VES.h, ve_yf)
                qdma(sync, SPS, out=d_y[i], in_=ybf[:, :])
            on("sync", _syy)
            sp_y = SPS.n
            G.set("ybf", SPS, sp_y)

        sp_all_y = SPS.n

        def _gx(gpsimd, sp_all_y=sp_all_y):
            gpsimd.wait_ge(SPS.h, sp_all_y)
            pb = gdyn(gpsimd, "peer")
            own = gdyn(gpsimd, "own")
            pag = gdyn(gpsimd, "peerag")
            qdma(gpsimd, GPS, out=d_y4p,
                 in_=d_y[bass.ds(pb, nslot)])
            qdma(gpsimd, GPS, out=d_yown,
                 in_=d_y[bass.ds(own, nslot)])
            gpsimd.wait_ge(GPS.h, GPS.n)
            CCS.n += 1
            gpsimd.collective_compute(
                "AllGather", ALU.bypass, replica_groups=GROUPS,
                ins=[d_y4p], outs=[d_yag]).then_inc(CCS.h, 1)
            gpsimd.wait_ge(CCS.h, CCS.n)
            qdma(gpsimd, GPS, out=d_ypeer,
                 in_=d_yag[bass.ds(pag, 1)].squeeze(0))
        on("gpsimd", _gx)
        gp_xch = GPS.n

        m2 = (VES.n, SPS.n, LDS.n, SCS.n, GPS.n)

        def _bar2(eng, m2=m2):
            eng.wait_ge(VES.h, m2[0])
            eng.wait_ge(SPS.h, m2[1])
            eng.wait_ge(LDS.h, m2[2])
            eng.wait_ge(SCS.h, m2[3])
            eng.wait_ge(GPS.h, m2[4])
        for en in ("sync", "vector", "scalar", "tensor", "gpsimd"):
            on(en, _bar2)
        s2.close()

        s3 = ExitStack()
        wout_sb = s3.enter_context(nc.sbuf_tensor("wout_sb", [128, NCH, D], BF16))
        qr_sb = s3.enter_context(nc.sbuf_tensor("qr_sb", [128, KD, D], BF16))
        xs3 = s3.enter_context(nc.sbuf_tensor("xs3", [128, NSUB, D], F32))
        xn3 = s3.enter_context(nc.sbuf_tensor("xn3", [128, NSUB, D], F32))
        q3 = s3.enter_context(nc.sbuf_tensor("q3", [128, NSUB, D], BF16))
        qT3 = s3.enter_context(nc.sbuf_tensor("qT3", [128, KD, T], BF16))
        ffn3 = s3.enter_context(nc.sbuf_tensor("ffn3", [128, NSUB, D], F32))
        comb3 = s3.enter_context(nc.sbuf_tensor("comb3", [128, NSUB, D], F32))
        ybf3 = s3.enter_context(nc.sbuf_tensor("ybf3", [128, NCH, T], BF16))
        outs3 = [s3.enter_context(nc.sbuf_tensor(f"out{b}", [128, D], F32))
                 for b in range(2)]
        qt3 = s3.enter_context(nc.sbuf_tensor("qt3", [128, D], F32))
        am3 = s3.enter_context(nc.sbuf_tensor("am3", [128, NSUB], F32))
        ls3 = s3.enter_context(nc.sbuf_tensor("ls3", [128, NSUB], F32))
        rls3 = s3.enter_context(nc.sbuf_tensor("rls3", [128, NSUB], F32))
        sq3 = s3.enter_context(nc.sbuf_tensor("sq3", [128, D], F32))
        st3 = {n: s3.enter_context(nc.sbuf_tensor("s3_" + n, [128, NSUB], F32))
               for n in ("xsum", "xsq", "mcol", "varc", "veps", "rstd",
                         "sc1", "sc2")}

        w3 = ExitStack()
        wstg = w3.enter_context(nc.sbuf_tensor("wstg", [128, KD * D], F32))
        for hh in range(2):
            def _sw(sync, hh=hh):
                if hh == 1:
                    G.wait(sync, "wstg")
                qdma(sync, LDS,
                     out=wstg[:, :].rearrange("p (j d) -> p j d", d=D),
                     in_=ap_wo[hh * KD:(hh + 1) * KD].rearrange(
                         "j p d -> p j d"))
            on("sync", _sw)
            ldw = LDS.n

            def _vw3(vector, hh=hh, ldw=ldw):
                vector.wait_ge(LDS.h, ldw)
                vector.tensor_copy(
                    wout_sb[:, hh * KD:(hh + 1) * KD, :],
                    wstg[:, :].rearrange("p (j d) -> p j d", d=D)
                ).then_inc(VES.h, 1)
            ve = VES.add(1)
            on("vector", _vw3)
            G.set("wstg", VES, ve)

        def _sq3(sync):
            qdma(sync, LDS,
                 out=qr_sb[:, :, :].rearrange("p k e -> p (k e)"), in_=d_qr)
        on("sync", _sq3)
        ld_qr = LDS.n

        ve_w3 = VES.n

        def _bar3(eng, ve_w3=ve_w3):
            eng.wait_ge(VES.h, ve_w3)
        for en in ("sync", "vector", "scalar", "tensor", "gpsimd"):
            on(en, _bar3)
        w3.close()

        for j in range(nslot):
            def _sx3(sync, j=j):
                G.wait(sync, "xs3")
                qdma(sync, LDS, out=xs3[:, :, :],
                     in_=ap_xo[j * T:(j + 1) * T, :].rearrange(
                         "(s p) d -> p s d", p=128))
            on("sync", _sx3)
            ld_x3 = LDS.n

            def _gy(sync, j=j):
                G.wait(sync, "ybf3")
                sync.wait_ge(GPS.h, gp_xch)
                qdma(sync, LDS,
                     out=ybf3[:, 0:NCHH, :].rearrange("p a t -> p (a t)"),
                     in_=d_yown[j])
                qdma(sync, LDS,
                     out=ybf3[:, NCHH:NCH, :].rearrange("p a t -> p (a t)"),
                     in_=d_ypeer[j])
            on("sync", _gy)
            gp_y = LDS.n

            ve_xn3 = emit_ln(xs3, xn3, st3, sq3, ld_x3)

            def _vq(vector, j=j, ve_xn3=ve_xn3):
                def vch(ins):
                    VES.n += 1
                    ins.then_inc(VES.h, 1)
                    vector.wait_ge(VES.h, VES.n)

                vector.wait_ge(VES.h, ve_xn3)
                ins = None
                for s in range(NSUB):
                    ins = vector.tensor_reduce(
                        am3[:, s:s + 1], xn3[:, s, :], AX, ALU.max,
                        apply_absolute_value=True)
                vch(ins)
                vch(vector.tensor_scalar(
                    ls3[:, :], am3[:, :], 1e-12, I127, ALU.max, ALU.mult))
                vch(vector.reciprocal(rls3[:, :], ls3[:, :]))
                ins = None
                for s in range(NSUB):
                    vch(vector.tensor_scalar(
                        qt3[:, :], xn3[:, s, :], rls3[:, s:s + 1], None,
                        ALU.mult))
                    ins = vector.tensor_scalar(
                        q3[:, s, :], qt3[:, :], RC, -RC, ALU.add, ALU.add)
                    if s < NSUB - 1:
                        vch(ins)
                ins.then_inc(VES.h, 1)
            on("vector", _vq)
            ve_q3 = VES.add(1)

            for k in range(KD):
                bk = k % 2

                def _pt3(tensor, k=k, bk=bk, ve_q3=ve_q3):
                    if k == 0:
                        tensor.wait_ge(VES.h, ve_q3)
                    G.wait(tensor, f"ps{bk}")
                    pv = ps[bk][:, 0:256].bitcast(BF16)
                    ins = None
                    for s in range(NSUB):
                        ins = tensor.transpose(
                            pv[:, s * 128:(s + 1) * 128],
                            q3[:, s, k * 128:(k + 1) * 128], identb[:, :])
                    ins.then_inc(TPS.h, 1)
                tp = TPS.add(1)
                on("tensor", _pt3)

                def _vt3(vector, k=k, bk=bk, tp=tp):
                    vector.wait_ge(TPS.h, tp)
                    vector.tensor_copy(
                        qT3[:, k, :], ps[bk][:, 0:256].bitcast(BF16)
                    ).then_inc(VES.h, 1)
                ve = VES.add(1)
                on("vector", _vt3)
                G.set(f"ps{bk}", VES, ve)
            ve_qT = VES.n

            for s in range(NSUB):
                def _pf(tensor, j=j, s=s, ve_qT=ve_qT, ld_qr=ld_qr):
                    if j == 0 and s == 0:
                        tensor.wait_ge(LDS.h, ld_qr)
                    if s == 0:
                        tensor.wait_ge(VES.h, ve_qT)
                    G.wait(tensor, "ps2")
                    G.wait(tensor, "ps3")
                    ins = None
                    for k in range(KD):
                        stat = qT3[:, k, s * 128:(s + 1) * 128]
                        tensor.matmul(
                            ps[2][:, :], stat, qr_sb[:, k, 0:T],
                            start=(k == 0), stop=(k == KD - 1))
                        ins = tensor.matmul(
                            ps[3][:, :], stat, qr_sb[:, k, T:D],
                            start=(k == 0), stop=(k == KD - 1))
                    ins.then_inc(TPS.h, 1)
                tp_f = TPS.add(1)
                on("tensor", _pf)

                def _sdq(scalar, s=s, tp_f=tp_f):
                    scalar.wait_ge(TPS.h, tp_f)
                    scalar.activation(
                        ffn3[:, s, 0:T], ps[2][:, :], AF.Copy,
                        scale=ls3[:, s:s + 1])
                    scalar.activation(
                        ffn3[:, s, T:D], ps[3][:, :], AF.Copy,
                        scale=ls3[:, s:s + 1]).then_inc(SCS.h, 1)
                sc_dq = SCS.add(1)
                on("scalar", _sdq)
                G.set("ps2", SCS, sc_dq)
                G.set("ps3", SCS, sc_dq)

                def _vdq(vector, s=s, sc_dq=sc_dq):
                    vector.wait_ge(SCS.h, sc_dq)
                    VES.n += 1
                    vector.tensor_tensor(
                        ffn3[:, s, :], ffn3[:, s, :], rsb[:, :], ALU.mult
                    ).then_inc(VES.h, 1)
                    vector.wait_ge(VES.h, VES.n)
                    vector.tensor_tensor(
                        comb3[:, s, :], xs3[:, s, :], ffn3[:, s, :], ALU.add
                    ).then_inc(VES.h, 1)
                on("vector", _vdq)
                ve_cmb = VES.add(1)

                def _po(tensor, j=j, s=s, gp_y=gp_y):
                    if s == 0:
                        tensor.wait_ge(LDS.h, gp_y)
                    G.wait(tensor, "ps4")
                    G.wait(tensor, "ps5")
                    ins = None
                    for jj in range(NCH):
                        stat = ybf3[:, jj, s * 128:(s + 1) * 128]
                        tensor.matmul(
                            ps[4][:, :], stat, wout_sb[:, jj, 0:T],
                            start=(jj == 0), stop=(jj == NCH - 1))
                        ins = tensor.matmul(
                            ps[5][:, :], stat, wout_sb[:, jj, T:D],
                            start=(jj == 0), stop=(jj == NCH - 1))
                    ins.then_inc(TPS.h, 1)
                tp_o = TPS.add(1)
                on("tensor", _po)

                ob = s % 2

                def _vo(vector, s=s, ob=ob, tp_o=tp_o, ve_cmb=ve_cmb):
                    vector.wait_ge(TPS.h, tp_o)
                    vector.wait_ge(VES.h, ve_cmb)
                    G.wait(vector, f"out{ob}")
                    vector.tensor_tensor(
                        outs3[ob][:, 0:T], comb3[:, s, 0:T], ps[4][:, :],
                        ALU.add)
                    vector.tensor_tensor(
                        outs3[ob][:, T:D], comb3[:, s, T:D], ps[5][:, :],
                        ALU.add).then_inc(VES.h, 1)
                ve_o = VES.add(1)
                on("vector", _vo)
                G.set("ps4", VES, ve_o)
                G.set("ps5", VES, ve_o)

                def _syo(sync, j=j, s=s, ob=ob, ve_o=ve_o):
                    sync.wait_ge(VES.h, ve_o)
                    qdma(sync, STS,
                         out=ap_out[j * T + s * 128:j * T + (s + 1) * 128, :],
                         in_=outs3[ob][:, :])
                on("sync", _syo)
                stn = STS.n
                G.set(f"out{ob}", STS, stn)

            G.set("xs3", VES, VES.n)
            G.set("ybf3", TPS, TPS.n)

        def _fin(sync):
            sync.wait_ge(STS.h, STS.n)
        on("sync", _fin)

        s3.close()
    return nc


def kernel(L0_attn_x, L0_attn_W_in, L0_attn_conv_w, L0_attn_conv_b,
           L0_attn_W_xproj, L0_attn_W_dt, L0_attn_b_dt, L0_attn_A_log,
           L0_attn_D_skip, L0_attn_W_out, L0_ffn_rhs):
    B, L, _ = L0_attn_x.shape
    LH = L // 2
    f32 = np.float32
    x_np = np.asarray(L0_attn_x, f32)
    w_in = np.asarray(L0_attn_W_in, f32)
    conv_w = np.asarray(L0_attn_conv_w, f32)
    conv_b = np.asarray(L0_attn_conv_b, f32)
    wxp = np.asarray(L0_attn_W_xproj, f32)
    wdt = np.asarray(L0_attn_W_dt, f32)
    bdt = np.asarray(L0_attn_b_dt, f32)
    alog = np.asarray(L0_attn_A_log, f32)
    dskip = np.asarray(L0_attn_D_skip, f32)
    wout = np.asarray(L0_attn_W_out, f32)
    rhs = np.asarray(L0_ffn_rhs, f32)

    wxp_pad = np.concatenate(
        [wxp, np.zeros((DI, 128 - wxp.shape[1]), f32)], axis=1)
    wdt_pad = np.concatenate([wdt, np.zeros((128 - wdt.shape[0], DI), f32)],
                             axis=0)

    in_maps = []
    for c in range(8):
        beta, phi = divmod(c, 2)
        sl = slice(phi * DIH, (phi + 1) * DIH)
        slp = slice((1 - phi) * DIH, (2 - phi) * DIH)
        xb = np.ascontiguousarray(x_np[beta])
        perm = np.concatenate([np.arange(phi * DIH, (phi + 1) * DIH),
                               np.arange((1 - phi) * DIH, (2 - phi) * DIH)])
        m = {
            "x": xb,
            "xown": np.ascontiguousarray(xb[phi * LH:(phi + 1) * LH]),
            "wix": np.ascontiguousarray(
                w_in[:, 0:DI][:, perm]).reshape(KD, 128, DI),
            "wiz": np.ascontiguousarray(
                w_in[:, DI + phi * DIH: DI + (phi + 1) * DIH]
            ).reshape(KD, 128, DIH),
            "convw": np.ascontiguousarray(
                conv_w[:, 0, :][perm].reshape(NCH, 128, 4).transpose(1, 0, 2)
            ).reshape(128, NCH * 4),
            "convb": np.ascontiguousarray(conv_b[perm].reshape(NCH, 128).T),
            "wxp": np.ascontiguousarray(
                wxp_pad[perm].reshape(NCH, 128, 128)),
            "wdt": np.ascontiguousarray(wdt_pad[:, sl]),
            "bdt": np.ascontiguousarray(bdt[sl].reshape(NCHH, 128).T),
            "alog": np.ascontiguousarray(
                alog[sl].reshape(NCHH, 128, NST).transpose(1, 0, 2)
            ).reshape(128, 128),
            "dskip": np.ascontiguousarray(dskip[sl].reshape(NCHH, 128).T),
            "wout": np.ascontiguousarray(
                np.concatenate([wout[sl], wout[slp]], axis=0)
            ).reshape(NCH, 128, D),
            "rhs": np.ascontiguousarray(rhs).reshape(KD, 128, D),
        }
        in_maps.append(m)

    nc = build_nc(L=L)
    res = run_bass_kernel_spmd(nc, in_maps, core_ids=list(range(8)))

    out = np.empty((B, L, D), np.float32)
    for c in range(8):
        beta, phi = divmod(c, 2)
        out[beta, phi * LH:(phi + 1) * LH, :] = res.results[c]["out_half"]
    return out


# version 17
# speedup vs baseline: 1.1103x; 1.0052x over previous
import numpy as np
import concourse.bass as bass
import concourse.mybir as mybir
from concourse.bass_utils import run_bass_kernel_spmd
from contextlib import ExitStack

F32 = mybir.dt.float32
BF16 = mybir.dt.bfloat16
AF = mybir.ActivationFunctionType
ALU = mybir.AluOpType
AX = mybir.AxisListType.X

D = 1024
DI = 2048
DIH = 1024
NST = 16
T = 512
NSUB = T // 128
KD = D // 128
NCH = DI // 128
NCHH = DIH // 128
RC = float(np.float32(1.5 * 2 ** 23))
EPS = 1e-5
I127 = float(np.float32(1.0 / 127.0))
GROUPS = [[0, 1], [2, 3], [4, 5], [6, 7]]


class Sem:
    def __init__(self, ctx, nc, name):
        self.h = ctx.enter_context(nc.semaphore(name))
        self.n = 0

    def add(self, k):
        self.n += k
        return self.n


class Guards:

    def __init__(self):
        self.g = {}

    def wait(self, eng, key):
        if key in self.g:
            s, v = self.g.pop(key)
            eng.wait_ge(s.h, v)

    def set(self, key, sem, val):
        self.g[key] = (sem, val)


def build_nc(L=8192, debug=False):
    nchunk = L // T
    nslot = nchunk // 2
    LH = L // 2

    nc = bass.Bass("TRN2", target_bir_lowering=False)
    dk = "ExternalOutput" if debug else "Internal"

    ap_x = nc.dram_tensor("x", [L, D], F32, kind="ExternalInput").ap()
    ap_xo = nc.dram_tensor("xown", [LH, D], F32, kind="ExternalInput").ap()
    ap_wix = nc.dram_tensor("wix", [KD, 128, DI], F32, kind="ExternalInput").ap()
    ap_wiz = nc.dram_tensor("wiz", [KD, 128, DIH], F32, kind="ExternalInput").ap()
    ap_cw = nc.dram_tensor("convw", [128, NCH * 4], F32, kind="ExternalInput").ap()
    ap_cb = nc.dram_tensor("convb", [128, NCH], F32, kind="ExternalInput").ap()
    ap_wxp = nc.dram_tensor("wxp", [NCH, 128, 128], F32, kind="ExternalInput").ap()
    ap_wdt = nc.dram_tensor("wdt", [128, DIH], F32, kind="ExternalInput").ap()
    ap_bdt = nc.dram_tensor("bdt", [128, NCHH], F32, kind="ExternalInput").ap()
    ap_al = nc.dram_tensor("alog", [128, 128], F32, kind="ExternalInput").ap()
    ap_ds = nc.dram_tensor("dskip", [128, NCHH], F32, kind="ExternalInput").ap()
    ap_wo = nc.dram_tensor("wout", [NCH, 128, D], F32, kind="ExternalInput").ap()
    ap_rhs = nc.dram_tensor("rhs", [KD, 128, D], F32, kind="ExternalInput").ap()

    ap_out = nc.dram_tensor("out_half", [LH, D], F32, kind="ExternalOutput").ap()

    d_delta = nc.dram_tensor("d_delta", [nchunk, 128, NCHH, T], F32, kind=dk).ap()
    d_u = nc.dram_tensor("d_u", [nchunk, 128, NCHH, T], F32, kind=dk).ap()
    d_sz = nc.dram_tensor("d_sz", [nchunk, 128, NCHH, T], F32, kind=dk).ap()
    d_bc = nc.dram_tensor("d_bc", [nchunk, 32, T], F32, kind=dk).ap()
    d_y = nc.dram_tensor("d_y", [nchunk, 128, NCHH * T], BF16, kind=dk).ap()
    d_qr = nc.dram_tensor("d_qr", [128, KD * D], BF16).ap()
    d_rs = nc.dram_tensor("d_rs", [D], F32).ap()
    d_y4p = nc.dram_tensor("d_y4p", [nslot, 128, NCHH * T], BF16).ap()
    d_yag = nc.dram_tensor("d_yag", [2, nslot, 128, NCHH * T], BF16).ap()
    d_yown = nc.dram_tensor("d_yown", [nslot, 128, NCHH * T], BF16).ap()
    d_ypeer = nc.dram_tensor("d_ypeer", [nslot, 128, NCHH * T], BF16).ap()

    ctx = ExitStack()
    with ctx:
        identb = ctx.enter_context(nc.sbuf_tensor("identb", [128, 128], BF16))
        identf = ctx.enter_context(nc.sbuf_tensor("identf", [128, 128], F32))
        a_sb = ctx.enter_context(nc.sbuf_tensor("a_sb", [128, 128], F32))
        cw_sb = ctx.enter_context(nc.sbuf_tensor("cw_sb", [128, NCH * 4], F32))
        cb_sb = ctx.enter_context(nc.sbuf_tensor("cb_sb", [128, NCH], F32))
        bdt_sb = ctx.enter_context(nc.sbuf_tensor("bdt_sb", [128, NCHH], F32))
        ds_sb = ctx.enter_context(nc.sbuf_tensor("ds_sb", [128, NCHH], F32))
        hcar = ctx.enter_context(nc.sbuf_tensor("hcar", [128, 128], F32))
        ones1 = ctx.enter_context(nc.sbuf_tensor("ones1", [1, 128], F32))
        rsb = ctx.enter_context(nc.sbuf_tensor("rsb", [128, D], F32))
        big8 = ctx.enter_context(nc.psum_tensor("big8", [128, 8 * T], F32))
        ps = [big8[:, i * T:(i + 1) * T] for i in range(8)]
        hps = big8[:, :].rearrange("p (a t) -> p a t", t=T)

        LDS = Sem(ctx, nc, "LDS")
        SPS = Sem(ctx, nc, "SPS")
        TPS = Sem(ctx, nc, "TPS")
        VES = Sem(ctx, nc, "VES")
        SCS = Sem(ctx, nc, "SCS")
        GPS = Sem(ctx, nc, "GPS")
        CCS = Sem(ctx, nc, "CCS")
        GPC = Sem(ctx, nc, "GPC")
        BCS = [Sem(ctx, nc, f"BCS{k}") for k in range(4)]
        LD2 = Sem(ctx, nc, "LD2")
        LD3 = Sem(ctx, nc, "LD3")
        STS = Sem(ctx, nc, "STS")

        block = ctx.enter_context(nc.Block())
        G = Guards()

        def on(eng, fn):
            getattr(block, eng)(fn)

        _dyn = {}

        def gdyn(g, key):
            if "phi" not in _dyn:
                _dyn["phi"] = g.partition_id() % 2
            phi = _dyn["phi"]
            if key not in _dyn:
                if key == "uoff":
                    _dyn[key] = phi * NCHH
                elif key == "own":
                    _dyn[key] = phi * nslot
                elif key == "peer":
                    _dyn[key] = (1 - phi) * nslot
                elif key == "peerag":
                    _dyn[key] = 1 - phi
            return _dyn[key]

        def qdma(eng, sem, **kw):
            eng.wait_ge(sem.h, sem.n)
            sem.n += 16
            return eng.dma_start(**kw).then_inc(sem.h, 16)

        RSQRT_MAGIC = 0x5F3759DF

        def emit_ln(xb, xn_out, st, sqscr, ld_wait):
            def _v1(vector):
                def vch(ins):
                    VES.n += 1
                    ins.then_inc(VES.h, 1)
                    vector.wait_ge(VES.h, VES.n)

                vector.wait_ge(LDS.h, ld_wait)
                for s in range(NSUB):
                    vector.tensor_reduce(
                        st["xsum"][:, s:s + 1], xb[:, s, :], AX, ALU.add)
                    vch(vector.tensor_tensor(
                        sqscr[:, :], xb[:, s, :], xb[:, s, :], ALU.mult))
                    vch(vector.tensor_reduce(
                        st["xsq"][:, s:s + 1], sqscr[:, :], AX, ALU.add))
                vch(vector.tensor_scalar(
                    st["mcol"][:, :], st["xsum"][:, :], 1.0 / D, None, ALU.mult))
                vch(vector.tensor_tensor(
                    st["sc1"][:, :], st["mcol"][:, :], st["mcol"][:, :],
                    ALU.mult))
                vch(vector.scalar_tensor_tensor(
                    st["varc"][:, :], st["xsq"][:, :], 1.0 / D,
                    st["sc1"][:, :], ALU.mult, ALU.subtract))
                vch(vector.tensor_scalar(
                    st["veps"][:, :], st["varc"][:, :], EPS, None, ALU.add))
                iv = st["veps"][:, :].bitcast(mybir.dt.int32)
                io = st["rstd"][:, :].bitcast(mybir.dt.int32)
                vch(vector.tensor_scalar(
                    io, iv, 1, None, ALU.arith_shift_right))
                vch(vector.tensor_scalar(
                    io, io, -1, RSQRT_MAGIC, ALU.mult, ALU.add))
                for _ in range(3):
                    vch(vector.tensor_tensor(
                        st["sc1"][:, :], st["rstd"][:, :], st["rstd"][:, :],
                        ALU.mult))
                    vch(vector.tensor_tensor(
                        st["sc2"][:, :], st["veps"][:, :], st["sc1"][:, :],
                        ALU.mult))
                    vch(vector.tensor_scalar(
                        st["sc2"][:, :], st["sc2"][:, :], -0.5, 1.5,
                        ALU.mult, ALU.add))
                    vch(vector.tensor_tensor(
                        st["rstd"][:, :], st["rstd"][:, :], st["sc2"][:, :],
                        ALU.mult))
                ins = None
                for s in range(NSUB):
                    ins = vector.tensor_scalar(
                        xn_out[:, s, :], xb[:, s, :], st["mcol"][:, s:s + 1],
                        st["rstd"][:, s:s + 1], ALU.subtract, ALU.mult)
                ins.then_inc(VES.h, 1)
            on("vector", _v1)
            VES.add(1)
            return VES.n

        s1 = ExitStack()
        wix_sb = s1.enter_context(nc.sbuf_tensor("wix_sb", [128, KD, DI], BF16))
        wiz_sb = s1.enter_context(nc.sbuf_tensor("wiz_sb", [128, KD, DIH], BF16))
        wxp_sb = s1.enter_context(nc.sbuf_tensor("wxp_sb", [128, NCH, 128], BF16))
        wdt_sb = s1.enter_context(nc.sbuf_tensor("wdt_sb", [128, DIH], BF16))
        diag_sb = s1.enter_context(
            nc.sbuf_tensor("diag_sb", [128, NCH * 4, 128], BF16))

        pro = ExitStack()
        stgA = pro.enter_context(nc.sbuf_tensor("stgA", [128, KD, D], F32))
        stgB = pro.enter_context(nc.sbuf_tensor("stgB", [128, KD, D], F32))
        qrT = pro.enter_context(nc.sbuf_tensor("qrT", [128, KD, D], BF16))
        qrS = pro.enter_context(nc.sbuf_tensor("qrS", [128, KD, D], BF16))
        ram8 = pro.enter_context(nc.sbuf_tensor("ram8", [128, KD], F32))
        rs8 = pro.enter_context(nc.sbuf_tensor("rs8", [128, KD], F32))
        rrs8 = pro.enter_context(nc.sbuf_tensor("rrs8", [128, KD], F32))
        qtmp = pro.enter_context(nc.sbuf_tensor("qtmp", [128, D], F32))

        def _sy0(sync):
            qdma(sync, LDS, out=cw_sb[:, :], in_=ap_cw)
            qdma(sync, LDS, out=cb_sb[:, :], in_=ap_cb)
            qdma(sync, LDS, out=bdt_sb[:, :], in_=ap_bdt)
            qdma(sync, LDS, out=ds_sb[:, :], in_=ap_ds)
            qdma(sync, LDS, out=a_sb[:, :], in_=ap_al)
            qdma(sync, LDS, out=stgA[:, :, :],
                 in_=ap_rhs.rearrange("k p e -> p k e"))
        on("sync", _sy0)
        ld_c = LDS.n

        def _gp0(gpsimd):
            for idt in (identb, identf):
                GPC.n += 1
                gpsimd.memset(idt[:, :], 0.0).then_inc(GPC.h, 1)
                gpsimd.wait_ge(GPC.h, GPC.n)
                GPC.n += 1
                gpsimd.affine_select(
                    out=idt[:, :], in_=idt[:, :],
                    compare_op=ALU.not_equal, fill=1.0, base=0,
                    pattern=[[-1, 128]], channel_multiplier=1
                ).then_inc(GPC.h, 1)
                gpsimd.wait_ge(GPC.h, GPC.n)
            GPC.n += 1
            gpsimd.memset(hcar[:, :], 0.0).then_inc(GPC.h, 1)
            GPC.n += 1
            gpsimd.memset(ones1[0:1, :], 1.0).then_inc(GPC.h, 1)
        on("gpsimd", _gp0)
        gp_id = GPC.n

        def _sc0(scalar):
            scalar.wait_ge(LDS.h, ld_c)
            scalar.activation(a_sb[:, :], a_sb[:, :], AF.Exp).then_inc(SCS.h, 1)
        sc_a = SCS.add(1)
        on("scalar", _sc0)

        def _v0(vector):
            vector.wait_ge(SCS.h, sc_a)
            vector.wait_ge(GPC.h, gp_id)
            vector.wait_ge(LDS.h, ld_c)
            vector.tensor_scalar(a_sb[:, :], a_sb[:, :], -1.0, None, ALU.mult)
            ins = None
            for jk in range(NCH * 4):
                ins = vector.tensor_scalar(
                    diag_sb[:, jk, :], identb[:, :], cw_sb[:, jk:jk + 1],
                    None, ALU.mult)
            ins.then_inc(VES.h, 1)
        ve_diag = VES.add(1)
        on("vector", _v0)

        def tr_block(src_ap, dst_fn, wait_fn=None, ident=None, bf=False):
            for idx in range(KD * KD):
                k, ec = divmod(idx, KD)
                bk = idx % 2

                def _pe(tensor, k=k, ec=ec, bk=bk, first=(idx == 0)):
                    if first and wait_fn is not None:
                        wait_fn(tensor)
                    G.wait(tensor, f"ps{bk}")
                    pv = (ps[bk][:, 0:64].bitcast(BF16) if bf
                          else ps[bk][:, 0:128])
                    tensor.transpose(
                        pv, src_ap(k, ec), ident[:, :]).then_inc(TPS.h, 1)
                tp = TPS.add(1)
                on("tensor", _pe)

                def _v(vector, k=k, ec=ec, bk=bk, tp=tp):
                    vector.wait_ge(TPS.h, tp)
                    pv = (ps[bk][:, 0:64].bitcast(BF16) if bf
                          else ps[bk][:, 0:128])
                    vector.tensor_copy(dst_fn(k, ec), pv).then_inc(VES.h, 1)
                ve = VES.add(1)
                on("vector", _v)
                G.set(f"ps{bk}", VES, ve)

        def _w_rhs(tensor):
            tensor.wait_ge(LDS.h, ld_c)
            tensor.wait_ge(VES.h, ve_diag)

        tr_block(lambda k, ec: stgA[:, k, ec * 128:(ec + 1) * 128],
                 lambda k, ec: stgB[:, ec, k * 128:(k + 1) * 128],
                 wait_fn=_w_rhs, ident=identf)
        ve_rT = VES.n

        def _v1(vector):
            def vch(ins):
                VES.n += 1
                ins.then_inc(VES.h, 1)
                vector.wait_ge(VES.h, VES.n)

            vector.wait_ge(VES.h, ve_rT)
            ins = None
            for ec in range(KD):
                ins = vector.tensor_reduce(
                    ram8[:, ec:ec + 1], stgB[:, ec, :], AX, ALU.max,
                    apply_absolute_value=True)
            vch(ins)
            vch(vector.tensor_scalar(
                rs8[:, :], ram8[:, :], 1e-12, I127, ALU.max, ALU.mult))
            vch(vector.reciprocal(rrs8[:, :], rs8[:, :]))
            ins = None
            for ec in range(KD):
                vch(vector.tensor_scalar(
                    qtmp[:, :], stgB[:, ec, :], rrs8[:, ec:ec + 1], None,
                    ALU.mult))
                ins = vector.tensor_scalar(
                    qrT[:, ec, :], qtmp[:, :], RC, -RC, ALU.add, ALU.add)
                if ec < KD - 1:
                    vch(ins)
            ins.then_inc(VES.h, 1)
        on("vector", _v1)
        ve_q = VES.add(1)

        def _sy1(sync):
            sync.wait_ge(VES.h, ve_q)
            for ec in range(KD):
                qdma(sync, SPS, out=d_rs[ec * 128:(ec + 1) * 128],
                     in_=rs8[:, ec:ec + 1])
        on("sync", _sy1)
        sp_rs = SPS.n

        def _sy2(sync):
            sync.wait_ge(SPS.h, sp_rs)
            qdma(sync, LDS, out=rsb[:, :], in_=d_rs.partition_broadcast(128))
        on("sync", _sy2)

        def _w_q(tensor):
            tensor.wait_ge(VES.h, ve_q)

        tr_block(lambda k, ec: qrT[:, ec, k * 128:(k + 1) * 128],
                 lambda k, ec: qrS[:, k, ec * 128:(ec + 1) * 128],
                 wait_fn=_w_q, ident=identb, bf=True)
        ve_qS = VES.n

        def _sy3(sync):
            sync.wait_ge(VES.h, ve_qS)
            qdma(sync, SPS, out=d_qr,
                 in_=qrS[:, :, :].rearrange("p k e -> p (k e)"))
        on("sync", _sy3)

        def ld_cast(src_ap, stg_view, dst_view, stg_guard_key):
            def _s(sync):
                G.wait(sync, stg_guard_key)
                qdma(sync, LDS, out=stg_view, in_=src_ap)
            on("sync", _s)
            ldn = LDS.n

            def _v(vector, ldn=ldn):
                vector.wait_ge(LDS.h, ldn)
                vector.tensor_copy(dst_view, stg_view).then_inc(VES.h, 1)
            ve = VES.add(1)
            on("vector", _v)
            G.set(stg_guard_key, VES, ve)

        sAf = stgA[:, :, :].rearrange("p k e -> p (k e)")
        sBf = stgB[:, :, :].rearrange("p k e -> p (k e)")
        G.set("stgA", VES, ve_rT)
        G.set("stgB", VES, ve_q)
        for h in range(2):
            ld_cast(ap_wix[:, :, h * 1024:(h + 1) * 1024].rearrange(
                        "k p c -> p k c"),
                    stgA[:, :, :],
                    wix_sb[:, :, h * 1024:(h + 1) * 1024], "stgA")
        ld_cast(ap_wiz.rearrange("k p c -> p k c"), stgB[:, :, :],
                wiz_sb[:, :, :], "stgB")
        ld_cast(ap_wxp.rearrange("j p c -> p j c"),
                sAf[:, 0:NCH * 128].rearrange("p (j c) -> p j c", c=128),
                wxp_sb[:, :, :], "stgA")
        ld_cast(ap_wdt, sBf[:, 0:DIH], wdt_sb[:, :], "stgB")
        G.g.pop("stgA", None)
        G.g.pop("stgB", None)

        ve_pro, sp_pro, ld_pro, tp_pro, sc_pro = VES.n, SPS.n, LDS.n, TPS.n, SCS.n

        def _bar(eng):
            eng.wait_ge(VES.h, ve_pro)
            eng.wait_ge(SPS.h, sp_pro)
            eng.wait_ge(LDS.h, ld_pro)
            eng.wait_ge(TPS.h, tp_pro)
            eng.wait_ge(SCS.h, sc_pro)
            eng.wait_ge(GPC.h, gp_id)
        for en in ("sync", "vector", "scalar", "tensor", "gpsimd"):
            on(en, _bar)
        pro.close()

        s1w = ExitStack()
        xsb = [s1w.enter_context(nc.sbuf_tensor(f"xsb{b}", [128, NSUB, D], F32))
               for b in range(2)]
        xnb = s1w.enter_context(nc.sbuf_tensor("xnb", [128, NSUB, D], BF16))
        xnT = s1w.enter_context(nc.sbuf_tensor("xnT", [128, KD, T], BF16))
        xinT = s1w.enter_context(nc.sbuf_tensor("xinT", [128, NCH, T + 3], BF16))
        ubf = s1w.enter_context(nc.sbuf_tensor("ubf", [128, NCH, T], BF16))
        xdT = s1w.enter_context(nc.sbuf_tensor("xdT", [128, T], BF16))
        bcst = [s1w.enter_context(nc.sbuf_tensor(f"bcst{b}", [128, T], F32))
                for b in range(2)]
        et8 = s1w.enter_context(nc.sbuf_tensor("et8", [128, NCHH, T], F32))
        szst = [s1w.enter_context(nc.sbuf_tensor(f"szst{b}", [128, T], F32))
                for b in range(2)]
        ufst = [s1w.enter_context(nc.sbuf_tensor(f"ufst{b}", [128, T], F32))
                for b in range(2)]
        sqscr = s1w.enter_context(nc.sbuf_tensor("sqscr", [128, D], F32))
        st4 = {n: s1w.enter_context(nc.sbuf_tensor("st_" + n, [128, NSUB], F32))
               for n in ("xsum", "xsq", "mcol", "varc", "veps", "rstd",
                         "sc1", "sc2")}

        def _gpz(gpsimd):
            GPC.n += 1
            gpsimd.memset(xinT[:, :, 0:3], 0.0).then_inc(GPC.h, 1)
        on("gpsimd", _gpz)
        gp_h0 = GPC.n

        xfree = {}
        szst_free = [0, 0]
        ufst_free = [0, 0]
        et_free = [0] * NCHH
        for i in range(nchunk):
            xb = xsb[i % 2]

            def _sx(sync, i=i):
                if i >= 2:
                    s, v = xfree[i - 2]
                    sync.wait_ge(s.h, v)
                qdma(sync, LDS, out=xsb[i % 2][:, :, :],
                     in_=ap_x[i * T:(i + 1) * T, :].rearrange(
                         "(s p) d -> p s d", p=128))
            on("sync", _sx)
            ld_x = LDS.n

            ve_xn1 = emit_ln(xb, xb, st4, sqscr, ld_x)

            def _vcast(vector, xb=xb, ve_xn1=ve_xn1):
                vector.wait_ge(VES.h, ve_xn1)
                vector.tensor_copy(
                    xnb[:, :, :].rearrange("p s d -> p (s d)"),
                    xb[:, :, :].rearrange("p s d -> p (s d)")
                ).then_inc(VES.h, 1)
            ve_xnb = VES.add(1)
            on("vector", _vcast)
            xfree[i] = (VES, ve_xnb)

            for j in range(KD):
                bk = j % 2

                def _pt(tensor, j=j, bk=bk, ve_xnb=ve_xnb, i=i):
                    if j == 0:
                        tensor.wait_ge(VES.h, ve_xnb)
                        if i == 0:
                            tensor.wait_ge(GPC.h, gp_h0)
                    G.wait(tensor, f"ps{bk}")
                    pv = ps[bk][:, 0:256].bitcast(BF16)
                    ins = None
                    for s in range(NSUB):
                        ins = tensor.transpose(
                            pv[:, s * 128:(s + 1) * 128],
                            xnb[:, s, j * 128:(j + 1) * 128], identb[:, :])
                    ins.then_inc(TPS.h, 1)
                tp = TPS.add(1)
                on("tensor", _pt)

                def _vt(vector, j=j, bk=bk, tp=tp):
                    vector.wait_ge(TPS.h, tp)
                    vector.tensor_copy(
                        xnT[:, j, :], ps[bk][:, 0:256].bitcast(BF16)
                    ).then_inc(VES.h, 1)
                ve = VES.add(1)
                on("vector", _vt)
                G.set(f"ps{bk}", VES, ve)
            ve_xnT = VES.n

            sc_zu = SCS.n
            sps_zu = SPS.n
            ve_xin_evac = {}
            tp_w = {}
            for m in range(NCH + NCHH):
                bk = 4 + m % 4

                def _pw(tensor, m=m, bk=bk, ve_xnT=ve_xnT):
                    if m == 0:
                        tensor.wait_ge(VES.h, ve_xnT)
                    G.wait(tensor, f"ps{bk}")
                    ins = None
                    for k in range(KD):
                        if m < NCH:
                            stat = wix_sb[:, k, m * 128:(m + 1) * 128]
                        else:
                            stat = wiz_sb[:, k, (m - NCH) * 128:(m - NCH + 1) * 128]
                        ins = tensor.matmul(
                            ps[bk][:, :], stat, xnT[:, k, :],
                            start=(k == 0), stop=(k == KD - 1))
                    ins.then_inc(TPS.h, 1)
                tp_w[m] = TPS.add(1)
                on("tensor", _pw)

                if m < NCH:
                    def _vw(vector, m=m, bk=bk, tp=tp_w[m]):
                        vector.wait_ge(TPS.h, tp)
                        vector.tensor_copy(
                            xinT[:, m, 3:T + 3], ps[bk][:, :]).then_inc(VES.h, 1)
                    ve = VES.add(1)
                    on("vector", _vw)
                    G.set(f"ps{bk}", VES, ve)
                    ve_xin_evac[m] = ve
                else:
                    G.set(f"ps{bk}", SCS, sc_zu + (m - NCH) + 1)

            tp_c = {}
            for j in range(NCH):
                bk = j % 2

                def _pc(tensor, j=j, bk=bk, ve_evj=ve_xin_evac[j]):
                    tensor.wait_ge(VES.h, ve_evj)
                    G.wait(tensor, f"ps{bk}")
                    ins = None
                    for k in range(4):
                        ins = tensor.matmul(
                            ps[bk][:, :], diag_sb[:, j * 4 + k, :],
                            xinT[:, j, k:k + T], start=(k == 0), stop=(k == 3))
                    ins.then_inc(TPS.h, 1)
                tp_c[j] = TPS.add(1)
                on("tensor", _pc)
                G.set(f"ps{bk}", SCS, sc_zu + NCHH + j + 1)

            def _szu(scalar, i=i, tp_w=dict(tp_w), tp_c=dict(tp_c),
                     sc_zu=sc_zu, sps_zu=sps_zu, szf=tuple(szst_free),
                     uff=tuple(ufst_free)):
                for zc in range(NCHH):
                    scalar.wait_ge(TPS.h, tp_w[NCH + zc])
                    if zc >= 2:
                        scalar.wait_ge(SPS.h, sps_zu + (zc - 1) * 16)
                    elif szf[zc % 2]:
                        scalar.wait_ge(SPS.h, szf[zc % 2])
                    SCS.n += 1
                    scalar.activation(
                        szst[zc % 2][:, :], ps[4 + zc % 4][:, :], AF.Silu
                    ).then_inc(SCS.h, 1)
                for j in range(NCH):
                    scalar.wait_ge(TPS.h, tp_c[j])
                    if j < NCHH:
                        if j >= 2:
                            scalar.wait_ge(SPS.h, sps_zu + 128 + (j - 1) * 16)
                        elif uff[j % 2]:
                            scalar.wait_ge(SPS.h, uff[j % 2])
                        scalar.activation(
                            ubf[:, j, :], ps[j % 2][:, :], AF.Silu,
                            bias=cb_sb[:, j:j + 1])
                        SCS.n += 1
                        scalar.activation(
                            ufst[j % 2][:, :], ps[j % 2][:, :], AF.Silu,
                            bias=cb_sb[:, j:j + 1]).then_inc(SCS.h, 1)
                    else:
                        SCS.n += 1
                        scalar.activation(
                            ubf[:, j, :], ps[j % 2][:, :], AF.Silu,
                            bias=cb_sb[:, j:j + 1]).then_inc(SCS.h, 1)
            on("scalar", _szu)
            assert SCS.n == sc_zu + NCHH + NCH

            for zc in range(NCHH):
                def _syz(sync, i=i, zc=zc, sc=sc_zu + zc + 1):
                    sync.wait_ge(SCS.h, sc)
                    qdma(sync, SPS, out=d_sz[i, :, zc, :],
                         in_=szst[zc % 2][:, :])
                on("sync", _syz)
            for j in range(NCHH):
                def _syu(sync, i=i, j=j, sc=sc_zu + NCHH + j + 1):
                    sync.wait_ge(SCS.h, sc)
                    qdma(sync, SPS, out=d_u[i, :, j, :], in_=ufst[j % 2][:, :])
                on("sync", _syu)
            assert SPS.n == sps_zu + 256
            szst_free[0] = sps_zu + 7 * 16
            szst_free[1] = sps_zu + 8 * 16
            ufst_free[0] = sps_zu + 128 + 7 * 16
            ufst_free[1] = sps_zu + 128 + 8 * 16
            sc_u_all = SCS.n

            def _pxd(tensor, sc_u_all=sc_u_all):
                tensor.wait_ge(SCS.h, sc_u_all)
                G.wait(tensor, "ps6")
                ins = None
                for k in range(NCH):
                    ins = tensor.matmul(
                        ps[6][:, :], wxp_sb[:, k, :], ubf[:, k, :],
                        start=(k == 0), stop=(k == NCH - 1))
                ins.then_inc(TPS.h, 1)
            tp_xd = TPS.add(1)
            on("tensor", _pxd)

            bcb = i % 2

            def _vxd(vector, bcb=bcb, tp_xd=tp_xd):
                vector.wait_ge(TPS.h, tp_xd)
                G.wait(vector, f"bcst{bcb}")
                vector.tensor_copy(xdT[:, :], ps[6][:, :])
                vector.tensor_copy(
                    bcst[bcb][64:96, :], ps[6][64:96, :]).then_inc(VES.h, 1)
            ve_xd = VES.add(1)
            on("vector", _vxd)
            G.set("ps6", VES, ve_xd)

            def _sybc(sync, i=i, bcb=bcb, ve_xd=ve_xd):
                sync.wait_ge(VES.h, ve_xd)
                qdma(sync, SPS, out=d_bc[i, :, :], in_=bcst[bcb][64:96, :])
            on("sync", _sybc)
            sp = SPS.n
            G.set(f"bcst{bcb}", SPS, sp)

            scd = SCS.n
            tp_d = {}
            for dh in range(NCHH):
                bk = 2 + dh % 2

                def _pd(tensor, dh=dh, bk=bk, ve_xd=ve_xd):
                    if dh == 0:
                        tensor.wait_ge(VES.h, ve_xd)
                    G.wait(tensor, f"ps{bk}")
                    tensor.matmul(
                        ps[bk][:, :], wdt_sb[:, dh * 128:(dh + 1) * 128],
                        xdT[:, :], start=True, stop=True).then_inc(TPS.h, 1)
                tp_d[dh] = TPS.add(1)
                on("tensor", _pd)
                G.set(f"ps{bk}", SCS, scd + dh + 1)

            def _sdel(scalar, i=i, tp_d=dict(tp_d), scd=scd,
                      etf=tuple(et_free)):
                for dh in range(NCHH):
                    scalar.wait_ge(TPS.h, tp_d[dh])
                    if etf[dh]:
                        scalar.wait_ge(SPS.h, etf[dh])
                    SCS.n += 1
                    scalar.activation(
                        et8[:, dh, :], ps[2 + dh % 2][:, :], AF.Exp,
                        bias=bdt_sb[:, dh:dh + 1]).then_inc(SCS.h, 1)
                scalar.wait_ge(SCS.h, scd + NCHH)
                for dh in range(NCHH):
                    SCS.n += 1
                    scalar.activation(
                        et8[:, dh, :], et8[:, dh, :], AF.Ln, bias=1.0
                    ).then_inc(SCS.h, 1)
            on("scalar", _sdel)
            assert SCS.n == scd + 2 * NCHH

            for dh in range(NCHH):
                def _syd(sync, i=i, dh=dh, lnv=scd + NCHH + dh + 1):
                    sync.wait_ge(SCS.h, lnv)
                    qdma(sync, SPS, out=d_delta[i, :, dh, :],
                         in_=et8[:, dh, :])
                on("sync", _syd)
                et_free[dh] = SPS.n

            tp_now = TPS.n

            ve_xin_last = ve_xin_evac[NCH - 1]

            def _vh(vector, tp_now=tp_now, ve_xin_last=ve_xin_last):
                vector.wait_ge(TPS.h, tp_now)
                vector.wait_ge(VES.h, ve_xin_last)
                vector.tensor_copy(
                    xinT[:, :, 0:3], xinT[:, :, T:T + 3]).then_inc(VES.h, 1)
            VES.add(1)
            on("vector", _vh)

        m1 = (VES.n, SPS.n, LDS.n, TPS.n, SCS.n, GPS.n)

        def _bar1(eng, m1=m1):
            eng.wait_ge(VES.h, m1[0])
            eng.wait_ge(SPS.h, m1[1])
            eng.wait_ge(LDS.h, m1[2])
            eng.wait_ge(TPS.h, m1[3])
            eng.wait_ge(SCS.h, m1[4])
            eng.wait_ge(GPS.h, m1[5])
        for en in ("sync", "vector", "scalar", "tensor", "gpsimd"):
            on(en, _bar1)
        s1w.close()
        s1.close()

        s2 = ExitStack()
        delta_a = s2.enter_context(nc.sbuf_tensor("delta_a", [128, NCHH * T], F32))
        u_a = s2.enter_context(nc.sbuf_tensor("u_a", [128, NCHH * T], F32))
        sz_a = s2.enter_context(nc.sbuf_tensor("sz_a", [128, NCHH * T], F32))
        dtu_a = s2.enter_context(nc.sbuf_tensor("dtu_a", [128, NCHH * T], F32))
        yacc = [s2.enter_context(nc.sbuf_tensor(f"yacc{b}", [128, NCHH * T], F32))
                for b in range(2)]
        yp_blk = [s2.enter_context(nc.sbuf_tensor(f"yp_blk{b}", [128, NCHH * T], F32))
                  for b in range(2)]
        dbu_b = s2.enter_context(nc.sbuf_tensor("dbu_b", [128, NCHH * T], F32))
        dAb = [s2.enter_context(nc.sbuf_tensor(f"dAb{b}", [128, T], F32))
               for b in range(4)]
        Bb = [s2.enter_context(nc.sbuf_tensor(f"Bb{b}", [128, T], F32))
              for b in range(2)]
        Cb = [s2.enter_context(nc.sbuf_tensor(f"Cb{b}", [128, T], F32))
              for b in range(2)]
        ybf = s2.enter_context(nc.sbuf_tensor("ybf", [128, NCHH * T], BF16))

        prev_vn_base = None
        prev_tts_last = None
        prev_tts_last2 = None

        def bc_load(i, n):
            nb = n % 2

            def _s(sync, i=i, n=n, nb=nb):
                G.wait(sync, f"Bb{nb}")
                qdma(sync, BCS[nb], out=Bb[nb][:, :],
                     in_=d_bc[i, n].unsqueeze(0).partition_broadcast(128))
                G.wait(sync, f"Cb{nb}")
                qdma(sync, BCS[2 + nb], out=Cb[nb][:, :],
                     in_=d_bc[i, 16 + n].unsqueeze(0).partition_broadcast(128))
            on("sync", _s)
            return (BCS[nb].n, BCS[2 + nb].n)

        for i in range(nchunk):
            def _sl(sync, i=i):
                G.wait(sync, "delta_a")
                qdma(sync, LDS, out=delta_a[:, :],
                     in_=d_delta[i].rearrange("p a t -> p (a t)"))
                G.wait(sync, "sz_a")
                qdma(sync, LD2, out=sz_a[:, :],
                     in_=d_sz[i].rearrange("p a t -> p (a t)"))
                G.wait(sync, "u_a")
                qdma(sync, LD3, out=u_a[:, :],
                     in_=d_u[i].rearrange("p a t -> p (a t)"))
            on("sync", _sl)
            ld_ds = LDS.n
            gp_u = LD3.n
            ld_sz = LD2.n

            def _vdtu(vector, ld_ds=ld_ds, gp_u=gp_u):
                vector.wait_ge(LDS.h, ld_ds)
                vector.wait_ge(LD3.h, gp_u)
                G.wait(vector, "dtu_a")
                vector.tensor_tensor(
                    dtu_a[:, :], delta_a[:, :], u_a[:, :], ALU.mult
                ).then_inc(VES.h, 1)
            ve_dtu = VES.add(1)
            on("vector", _vdtu)
            ve_dtu_g = [ve_dtu]

            if i == 0:
                bc_ready = [bc_load(0, 0), bc_load(0, 1)]

            gpc_carry_prev = GPC.n
            for n in range(NST):
                nb = n % 2
                vn_base = VES.n
                sc_bc = bc_ready[n]

                sc_base = SCS.n

                def _se(scalar, i=i, n=n, ld_ds=ld_ds, vn_base=vn_base,
                        prev_vn=prev_vn_base):
                    for dh in range(NCHH):
                        if dh == 0 and n == 0:
                            scalar.wait_ge(LDS.h, ld_ds)
                        if dh >= 4:
                            fa = vn_base + (dh - 4) + 2
                        elif prev_vn is not None:
                            fa = prev_vn + (dh + 4) + 2
                        else:
                            fa = 0
                        if fa:
                            scalar.wait_ge(VES.h, fa)
                        SCS.n += 1
                        scalar.activation(
                            dAb[dh % 4][:, :], delta_a[:, dh * T:(dh + 1) * T],
                            AF.Exp, scale=a_sb[:, dh * 16 + n:dh * 16 + n + 1]
                        ).then_inc(SCS.h, 1)
                on("scalar", _se)
                sc_base2 = sc_base

                def _vn(vector, i=i, n=n, nb=nb, vn_base=vn_base,
                        sc_bc=sc_bc, sc_base=sc_base2, ve_dtu=ve_dtu,
                        gpcw=gpc_carry_prev):
                    if n == 0:
                        vector.wait_ge(VES.h, ve_dtu)
                    vector.wait_ge(VES.h, vn_base)
                    vector.wait_ge(BCS[nb].h, sc_bc[0])
                    vector.wait_ge(BCS[2 + nb].h, sc_bc[1])
                    VES.n += 1
                    vector.tensor_tensor(
                        dbu_b[:, :].rearrange("p (a t) -> p a t", t=T),
                        dtu_a[:, :].rearrange("p (a t) -> p a t", t=T),
                        Bb[nb][:, :].unsqueeze(1).broadcast_to([128, NCHH, T]),
                        ALU.mult).then_inc(VES.h, 1)
                    vector.wait_ge(VES.h, VES.n)
                    for dh in range(NCHH):
                        vector.wait_ge(SCS.h, sc_base + dh + 1)
                        VES.n += 1
                        vector.tensor_tensor_scan(
                            hps[:, dh, :], dAb[dh % 4][:, :],
                            dbu_b[:, dh * T:(dh + 1) * T],
                            hcar[:, n * 8 + dh:n * 8 + dh + 1],
                            ALU.mult, ALU.add).then_inc(VES.h, 1)
                    vector.wait_ge(VES.h, VES.n)
                    vector.wait_ge(GPC.h, gpcw)
                    VES.n += 1
                    if n == 0:
                        vector.tensor_tensor(
                            yacc[0][:, :].rearrange("p (a t) -> p a t", t=T),
                            hps[:, :, :],
                            Cb[nb][:, :].unsqueeze(1).broadcast_to(
                                [128, NCHH, T]), ALU.mult).then_inc(VES.h, 1)
                    else:
                        vector.tensor_tensor(
                            yp_blk[nb][:, :].rearrange("p (a t) -> p a t", t=T),
                            hps[:, :, :],
                            Cb[nb][:, :].unsqueeze(1).broadcast_to(
                                [128, NCHH, T]), ALU.mult).then_inc(VES.h, 1)
                on("vector", _vn)
                tts_last = vn_base + 1 + NCHH
                ve_y = VES.n
                G.set(f"Bb{nb}", VES, vn_base + 1)
                G.set(f"Cb{nb}", VES, ve_y)

                def _sc(scalar, n=n, vn_base=vn_base):
                    for dh in range(NCHH):
                        scalar.wait_ge(VES.h, vn_base + 2 + dh)
                        SCS.n += 1
                        scalar.copy(
                            hcar[:, n * 8 + dh:n * 8 + dh + 1],
                            big8[:, dh * T + T - 1:dh * T + T]
                        ).then_inc(SCS.h, 1)
                on("scalar", _sc)

                gpc_carry_prev = GPC.n
                if n >= 2:
                    def _gp(gpsimd, pn=n - 1, gate=vn_base + 1 + NCHH):
                        gpsimd.wait_ge(VES.h, gate)
                        gpsimd.wait_ge(GPC.h, GPC.n)
                        GPC.n += 1
                        gpsimd.tensor_tensor(
                            yacc[pn % 2][:, :], yacc[1 - pn % 2][:, :],
                            yp_blk[pn % 2][:, :], ALU.add).then_inc(GPC.h, 1)
                    on("gpsimd", _gp)
                if n == NST - 1:
                    def _gpl(gpsimd, n=n, ve_y=ve_y):
                        gpsimd.wait_ge(VES.h, ve_y)
                        gpsimd.wait_ge(GPC.h, GPC.n)
                        GPC.n += 1
                        gpsimd.tensor_tensor(
                            yacc[n % 2][:, :], yacc[1 - n % 2][:, :],
                            yp_blk[n % 2][:, :], ALU.add).then_inc(GPC.h, 1)
                    on("gpsimd", _gpl)
                gpc_add_last = GPC.n

                if n + 2 < NST:
                    bc_ready.append(bc_load(i, n + 2))
                elif i + 1 < nchunk:
                    bc_ready.append(bc_load(i + 1, n + 2 - NST))
                else:
                    bc_ready.append(None)

                prev_vn_base = vn_base
            bc_ready = bc_ready[NST:]

            def _vyf(vector, i=i, ld_sz=ld_sz, gpl=gpc_add_last):
                vector.wait_ge(VES.h, VES.n)
                vector.wait_ge(GPC.h, gpl)
                vector.wait_ge(LD2.h, ld_sz)
                ins = None
                for dh in range(NCHH):
                    ins = vector.scalar_tensor_tensor(
                        yp_blk[0][:, dh * T:(dh + 1) * T],
                        u_a[:, dh * T:(dh + 1) * T],
                        ds_sb[:, dh:dh + 1],
                        yacc[(NST - 1) % 2][:, dh * T:(dh + 1) * T],
                        ALU.mult, ALU.add)
                VES.n += 1
                ins.then_inc(VES.h, 1)
                vector.wait_ge(VES.h, VES.n)
                G.wait(vector, "ybf")
                vector.tensor_tensor(
                    ybf[:, :], yp_blk[0][:, :], sz_a[:, :],
                    ALU.mult).then_inc(VES.h, 1)
            on("vector", _vyf)
            ve_yf = VES.add(1)
            G.set("delta_a", SCS, SCS.n)
            G.set("u_a", VES, ve_yf)
            G.set("sz_a", VES, ve_yf)
            G.set("dtu_a", VES, ve_yf)

            def _syy(sync, i=i, ve_yf=ve_yf):
                sync.wait_ge(VES.h, ve_yf)
                qdma(sync, SPS, out=d_y[i], in_=ybf[:, :])
            on("sync", _syy)
            sp_y = SPS.n
            G.set("ybf", SPS, sp_y)

        sp_all_y = SPS.n

        def _gx(gpsimd, sp_all_y=sp_all_y):
            gpsimd.wait_ge(SPS.h, sp_all_y)
            pb = gdyn(gpsimd, "peer")
            own = gdyn(gpsimd, "own")
            pag = gdyn(gpsimd, "peerag")
            qdma(gpsimd, GPS, out=d_y4p,
                 in_=d_y[bass.ds(pb, nslot)])
            qdma(gpsimd, GPS, out=d_yown,
                 in_=d_y[bass.ds(own, nslot)])
            gpsimd.wait_ge(GPS.h, GPS.n)
            CCS.n += 1
            gpsimd.collective_compute(
                "AllGather", ALU.bypass, replica_groups=GROUPS,
                ins=[d_y4p], outs=[d_yag]).then_inc(CCS.h, 1)
            gpsimd.wait_ge(CCS.h, CCS.n)
            qdma(gpsimd, GPS, out=d_ypeer,
                 in_=d_yag[bass.ds(pag, 1)].squeeze(0))
        gps_pre_xch = GPS.n
        on("gpsimd", _gx)
        gp_xch = GPS.n

        m2 = (VES.n, SPS.n, LDS.n, SCS.n, gps_pre_xch)

        def _bar2(eng, m2=m2):
            eng.wait_ge(VES.h, m2[0])
            eng.wait_ge(SPS.h, m2[1])
            eng.wait_ge(LDS.h, m2[2])
            eng.wait_ge(SCS.h, m2[3])
            eng.wait_ge(GPS.h, m2[4])
        for en in ("sync", "vector", "scalar", "tensor", "gpsimd"):
            on(en, _bar2)
        s2.close()

        s3 = ExitStack()
        wout_sb = s3.enter_context(nc.sbuf_tensor("wout_sb", [128, NCH, D], BF16))
        qr_sb = s3.enter_context(nc.sbuf_tensor("qr_sb", [128, KD, D], BF16))
        xs3 = s3.enter_context(nc.sbuf_tensor("xs3", [128, NSUB, D], F32))
        xn3 = s3.enter_context(nc.sbuf_tensor("xn3", [128, NSUB, D], F32))
        q3 = s3.enter_context(nc.sbuf_tensor("q3", [128, NSUB, D], BF16))
        qT3 = s3.enter_context(nc.sbuf_tensor("qT3", [128, KD, T], BF16))
        ffn3 = s3.enter_context(nc.sbuf_tensor("ffn3", [128, NSUB, D], F32))
        comb3 = s3.enter_context(nc.sbuf_tensor("comb3", [128, NSUB, D], F32))
        ybf3 = s3.enter_context(nc.sbuf_tensor("ybf3", [128, NCH, T], BF16))
        outs3 = [s3.enter_context(nc.sbuf_tensor(f"out{b}", [128, D], F32))
                 for b in range(2)]
        qt3 = s3.enter_context(nc.sbuf_tensor("qt3", [128, D], F32))
        am3 = s3.enter_context(nc.sbuf_tensor("am3", [128, NSUB], F32))
        ls3 = s3.enter_context(nc.sbuf_tensor("ls3", [128, NSUB], F32))
        rls3 = s3.enter_context(nc.sbuf_tensor("rls3", [128, NSUB], F32))
        sq3 = s3.enter_context(nc.sbuf_tensor("sq3", [128, D], F32))
        st3 = {n: s3.enter_context(nc.sbuf_tensor("s3_" + n, [128, NSUB], F32))
               for n in ("xsum", "xsq", "mcol", "varc", "veps", "rstd",
                         "sc1", "sc2")}

        w3 = ExitStack()
        wstg = w3.enter_context(nc.sbuf_tensor("wstg", [128, KD * D], F32))
        for hh in range(2):
            def _sw(sync, hh=hh):
                if hh == 1:
                    G.wait(sync, "wstg")
                qdma(sync, LDS,
                     out=wstg[:, :].rearrange("p (j d) -> p j d", d=D),
                     in_=ap_wo[hh * KD:(hh + 1) * KD].rearrange(
                         "j p d -> p j d"))
            on("sync", _sw)
            ldw = LDS.n

            def _vw3(vector, hh=hh, ldw=ldw):
                vector.wait_ge(LDS.h, ldw)
                vector.tensor_copy(
                    wout_sb[:, hh * KD:(hh + 1) * KD, :],
                    wstg[:, :].rearrange("p (j d) -> p j d", d=D)
                ).then_inc(VES.h, 1)
            ve = VES.add(1)
            on("vector", _vw3)
            G.set("wstg", VES, ve)

        def _sq3(sync):
            qdma(sync, LDS,
                 out=qr_sb[:, :, :].rearrange("p k e -> p (k e)"), in_=d_qr)
        on("sync", _sq3)
        ld_qr = LDS.n

        ve_w3 = VES.n

        def _bar3(eng, ve_w3=ve_w3):
            eng.wait_ge(VES.h, ve_w3)
        for en in ("sync", "vector", "scalar", "tensor", "gpsimd"):
            on(en, _bar3)
        w3.close()

        for j in range(nslot):
            def _sx3(sync, j=j):
                G.wait(sync, "xs3")
                qdma(sync, LDS, out=xs3[:, :, :],
                     in_=ap_xo[j * T:(j + 1) * T, :].rearrange(
                         "(s p) d -> p s d", p=128))
            on("sync", _sx3)
            ld_x3 = LDS.n

            def _gy(sync, j=j):
                G.wait(sync, "ybf3")
                sync.wait_ge(GPS.h, gp_xch)
                qdma(sync, LDS,
                     out=ybf3[:, 0:NCHH, :].rearrange("p a t -> p (a t)"),
                     in_=d_yown[j])
                qdma(sync, LDS,
                     out=ybf3[:, NCHH:NCH, :].rearrange("p a t -> p (a t)"),
                     in_=d_ypeer[j])
            on("sync", _gy)
            gp_y = LDS.n

            ve_xn3 = emit_ln(xs3, xn3, st3, sq3, ld_x3)

            def _vq(vector, j=j, ve_xn3=ve_xn3):
                def vch(ins):
                    VES.n += 1
                    ins.then_inc(VES.h, 1)
                    vector.wait_ge(VES.h, VES.n)

                vector.wait_ge(VES.h, ve_xn3)
                ins = None
                for s in range(NSUB):
                    ins = vector.tensor_reduce(
                        am3[:, s:s + 1], xn3[:, s, :], AX, ALU.max,
                        apply_absolute_value=True)
                vch(ins)
                vch(vector.tensor_scalar(
                    ls3[:, :], am3[:, :], 1e-12, I127, ALU.max, ALU.mult))
                vch(vector.reciprocal(rls3[:, :], ls3[:, :]))
                ins = None
                for s in range(NSUB):
                    vch(vector.tensor_scalar(
                        qt3[:, :], xn3[:, s, :], rls3[:, s:s + 1], None,
                        ALU.mult))
                    ins = vector.tensor_scalar(
                        q3[:, s, :], qt3[:, :], RC, -RC, ALU.add, ALU.add)
                    if s < NSUB - 1:
                        vch(ins)
                ins.then_inc(VES.h, 1)
            on("vector", _vq)
            ve_q3 = VES.add(1)

            for k in range(KD):
                bk = k % 2

                def _pt3(tensor, k=k, bk=bk, ve_q3=ve_q3):
                    if k == 0:
                        tensor.wait_ge(VES.h, ve_q3)
                    G.wait(tensor, f"ps{bk}")
                    pv = ps[bk][:, 0:256].bitcast(BF16)
                    ins = None
                    for s in range(NSUB):
                        ins = tensor.transpose(
                            pv[:, s * 128:(s + 1) * 128],
                            q3[:, s, k * 128:(k + 1) * 128], identb[:, :])
                    ins.then_inc(TPS.h, 1)
                tp = TPS.add(1)
                on("tensor", _pt3)

                def _vt3(vector, k=k, bk=bk, tp=tp):
                    vector.wait_ge(TPS.h, tp)
                    vector.tensor_copy(
                        qT3[:, k, :], ps[bk][:, 0:256].bitcast(BF16)
                    ).then_inc(VES.h, 1)
                ve = VES.add(1)
                on("vector", _vt3)
                G.set(f"ps{bk}", VES, ve)
            ve_qT = VES.n

            for s in range(NSUB):
                def _pf(tensor, j=j, s=s, ve_qT=ve_qT, ld_qr=ld_qr):
                    if j == 0 and s == 0:
                        tensor.wait_ge(LDS.h, ld_qr)
                    if s == 0:
                        tensor.wait_ge(VES.h, ve_qT)
                    G.wait(tensor, "ps2")
                    G.wait(tensor, "ps3")
                    ins = None
                    for k in range(KD):
                        stat = qT3[:, k, s * 128:(s + 1) * 128]
                        tensor.matmul(
                            ps[2][:, :], stat, qr_sb[:, k, 0:T],
                            start=(k == 0), stop=(k == KD - 1))
                        ins = tensor.matmul(
                            ps[3][:, :], stat, qr_sb[:, k, T:D],
                            start=(k == 0), stop=(k == KD - 1))
                    ins.then_inc(TPS.h, 1)
                tp_f = TPS.add(1)
                on("tensor", _pf)

                def _sdq(scalar, s=s, tp_f=tp_f):
                    scalar.wait_ge(TPS.h, tp_f)
                    scalar.activation(
                        ffn3[:, s, 0:T], ps[2][:, :], AF.Copy,
                        scale=ls3[:, s:s + 1])
                    scalar.activation(
                        ffn3[:, s, T:D], ps[3][:, :], AF.Copy,
                        scale=ls3[:, s:s + 1]).then_inc(SCS.h, 1)
                sc_dq = SCS.add(1)
                on("scalar", _sdq)
                G.set("ps2", SCS, sc_dq)
                G.set("ps3", SCS, sc_dq)

                def _vdq(vector, s=s, sc_dq=sc_dq):
                    vector.wait_ge(SCS.h, sc_dq)
                    VES.n += 1
                    vector.tensor_tensor(
                        ffn3[:, s, :], ffn3[:, s, :], rsb[:, :], ALU.mult
                    ).then_inc(VES.h, 1)
                    vector.wait_ge(VES.h, VES.n)
                    vector.tensor_tensor(
                        comb3[:, s, :], xs3[:, s, :], ffn3[:, s, :], ALU.add
                    ).then_inc(VES.h, 1)
                on("vector", _vdq)
                ve_cmb = VES.add(1)

                def _po(tensor, j=j, s=s, gp_y=gp_y):
                    if s == 0:
                        tensor.wait_ge(LDS.h, gp_y)
                    G.wait(tensor, "ps4")
                    G.wait(tensor, "ps5")
                    ins = None
                    for jj in range(NCH):
                        stat = ybf3[:, jj, s * 128:(s + 1) * 128]
                        tensor.matmul(
                            ps[4][:, :], stat, wout_sb[:, jj, 0:T],
                            start=(jj == 0), stop=(jj == NCH - 1))
                        ins = tensor.matmul(
                            ps[5][:, :], stat, wout_sb[:, jj, T:D],
                            start=(jj == 0), stop=(jj == NCH - 1))
                    ins.then_inc(TPS.h, 1)
                tp_o = TPS.add(1)
                on("tensor", _po)

                ob = s % 2

                def _vo(vector, s=s, ob=ob, tp_o=tp_o, ve_cmb=ve_cmb):
                    vector.wait_ge(TPS.h, tp_o)
                    vector.wait_ge(VES.h, ve_cmb)
                    G.wait(vector, f"out{ob}")
                    vector.tensor_tensor(
                        outs3[ob][:, 0:T], comb3[:, s, 0:T], ps[4][:, :],
                        ALU.add)
                    vector.tensor_tensor(
                        outs3[ob][:, T:D], comb3[:, s, T:D], ps[5][:, :],
                        ALU.add).then_inc(VES.h, 1)
                ve_o = VES.add(1)
                on("vector", _vo)
                G.set("ps4", VES, ve_o)
                G.set("ps5", VES, ve_o)

                def _syo(sync, j=j, s=s, ob=ob, ve_o=ve_o):
                    sync.wait_ge(VES.h, ve_o)
                    qdma(sync, STS,
                         out=ap_out[j * T + s * 128:j * T + (s + 1) * 128, :],
                         in_=outs3[ob][:, :])
                on("sync", _syo)
                stn = STS.n
                G.set(f"out{ob}", STS, stn)

            G.set("xs3", VES, VES.n)
            G.set("ybf3", TPS, TPS.n)

        def _fin(sync):
            sync.wait_ge(STS.h, STS.n)
        on("sync", _fin)

        s3.close()
    return nc


def kernel(L0_attn_x, L0_attn_W_in, L0_attn_conv_w, L0_attn_conv_b,
           L0_attn_W_xproj, L0_attn_W_dt, L0_attn_b_dt, L0_attn_A_log,
           L0_attn_D_skip, L0_attn_W_out, L0_ffn_rhs):
    B, L, _ = L0_attn_x.shape
    LH = L // 2
    f32 = np.float32
    x_np = np.asarray(L0_attn_x, f32)
    w_in = np.asarray(L0_attn_W_in, f32)
    conv_w = np.asarray(L0_attn_conv_w, f32)
    conv_b = np.asarray(L0_attn_conv_b, f32)
    wxp = np.asarray(L0_attn_W_xproj, f32)
    wdt = np.asarray(L0_attn_W_dt, f32)
    bdt = np.asarray(L0_attn_b_dt, f32)
    alog = np.asarray(L0_attn_A_log, f32)
    dskip = np.asarray(L0_attn_D_skip, f32)
    wout = np.asarray(L0_attn_W_out, f32)
    rhs = np.asarray(L0_ffn_rhs, f32)

    wxp_pad = np.concatenate(
        [wxp, np.zeros((DI, 128 - wxp.shape[1]), f32)], axis=1)
    wdt_pad = np.concatenate([wdt, np.zeros((128 - wdt.shape[0], DI), f32)],
                             axis=0)

    in_maps = []
    for c in range(8):
        beta, phi = divmod(c, 2)
        sl = slice(phi * DIH, (phi + 1) * DIH)
        slp = slice((1 - phi) * DIH, (2 - phi) * DIH)
        xb = np.ascontiguousarray(x_np[beta])
        perm = np.concatenate([np.arange(phi * DIH, (phi + 1) * DIH),
                               np.arange((1 - phi) * DIH, (2 - phi) * DIH)])
        m = {
            "x": xb,
            "xown": np.ascontiguousarray(xb[phi * LH:(phi + 1) * LH]),
            "wix": np.ascontiguousarray(
                w_in[:, 0:DI][:, perm]).reshape(KD, 128, DI),
            "wiz": np.ascontiguousarray(
                w_in[:, DI + phi * DIH: DI + (phi + 1) * DIH]
            ).reshape(KD, 128, DIH),
            "convw": np.ascontiguousarray(
                conv_w[:, 0, :][perm].reshape(NCH, 128, 4).transpose(1, 0, 2)
            ).reshape(128, NCH * 4),
            "convb": np.ascontiguousarray(conv_b[perm].reshape(NCH, 128).T),
            "wxp": np.ascontiguousarray(
                wxp_pad[perm].reshape(NCH, 128, 128)),
            "wdt": np.ascontiguousarray(wdt_pad[:, sl]),
            "bdt": np.ascontiguousarray(bdt[sl].reshape(NCHH, 128).T),
            "alog": np.ascontiguousarray(
                alog[sl].reshape(NCHH, 128, NST).transpose(1, 0, 2)
            ).reshape(128, 128),
            "dskip": np.ascontiguousarray(dskip[sl].reshape(NCHH, 128).T),
            "wout": np.ascontiguousarray(
                np.concatenate([wout[sl], wout[slp]], axis=0)
            ).reshape(NCH, 128, D),
            "rhs": np.ascontiguousarray(rhs).reshape(KD, 128, D),
        }
        in_maps.append(m)

    nc = build_nc(L=L)
    res = run_bass_kernel_spmd(nc, in_maps, core_ids=list(range(8)))

    out = np.empty((B, L, D), np.float32)
    for c in range(8):
        beta, phi = divmod(c, 2)
        out[beta, phi * LH:(phi + 1) * LH, :] = res.results[c]["out_half"]
    return out


# version 18
# speedup vs baseline: 1.1573x; 1.0423x over previous
import numpy as np
import concourse.bass as bass
import concourse.mybir as mybir
from concourse.bass_utils import run_bass_kernel_spmd
from contextlib import ExitStack

F32 = mybir.dt.float32
BF16 = mybir.dt.bfloat16
AF = mybir.ActivationFunctionType
ALU = mybir.AluOpType
AX = mybir.AxisListType.X

D = 1024
DI = 2048
DIH = 1024
NST = 16
T = 512
NSUB = T // 128
KD = D // 128
NCH = DI // 128
NCHH = DIH // 128
RC = float(np.float32(1.5 * 2 ** 23))
EPS = 1e-5
I127 = float(np.float32(1.0 / 127.0))
GROUPS = [[0, 1], [2, 3], [4, 5], [6, 7]]


class Sem:
    def __init__(self, ctx, nc, name):
        self.h = ctx.enter_context(nc.semaphore(name))
        self.n = 0

    def add(self, k):
        self.n += k
        return self.n


class Guards:

    def __init__(self):
        self.g = {}

    def wait(self, eng, key):
        if key in self.g:
            s, v = self.g.pop(key)
            eng.wait_ge(s.h, v)

    def set(self, key, sem, val):
        self.g[key] = (sem, val)


def build_nc(L=8192, debug=False):
    nchunk = L // T
    nslot = nchunk // 2
    LH = L // 2

    nc = bass.Bass("TRN2", target_bir_lowering=False)
    dk = "ExternalOutput" if debug else "Internal"

    ap_x = nc.dram_tensor("x", [L, D], F32, kind="ExternalInput").ap()
    ap_xo = nc.dram_tensor("xown", [LH, D], F32, kind="ExternalInput").ap()
    ap_wix = nc.dram_tensor("wix", [KD, 128, DI], F32, kind="ExternalInput").ap()
    ap_wiz = nc.dram_tensor("wiz", [KD, 128, DIH], F32, kind="ExternalInput").ap()
    ap_cw = nc.dram_tensor("convw", [128, NCH * 4], F32, kind="ExternalInput").ap()
    ap_cb = nc.dram_tensor("convb", [128, NCH], F32, kind="ExternalInput").ap()
    ap_wxp = nc.dram_tensor("wxp", [NCH, 128, 128], F32, kind="ExternalInput").ap()
    ap_wdt = nc.dram_tensor("wdt", [128, DIH], F32, kind="ExternalInput").ap()
    ap_bdt = nc.dram_tensor("bdt", [128, NCHH], F32, kind="ExternalInput").ap()
    ap_al = nc.dram_tensor("alog", [128, 128], F32, kind="ExternalInput").ap()
    ap_ds = nc.dram_tensor("dskip", [128, NCHH], F32, kind="ExternalInput").ap()
    ap_wo = nc.dram_tensor("wout", [NCH, 128, D], F32, kind="ExternalInput").ap()
    ap_rhs = nc.dram_tensor("rhs", [KD, 128, D], F32, kind="ExternalInput").ap()

    ap_out = nc.dram_tensor("out_half", [LH, D], F32, kind="ExternalOutput").ap()

    d_delta = nc.dram_tensor("d_delta", [nchunk, 128, NCHH, T], F32, kind=dk).ap()
    d_u = nc.dram_tensor("d_u", [nchunk, 128, NCHH, T], F32, kind=dk).ap()
    d_sz = nc.dram_tensor("d_sz", [nchunk, 128, NCHH, T], F32, kind=dk).ap()
    d_bc = nc.dram_tensor("d_bc", [nchunk, 32, T], F32, kind=dk).ap()
    d_y = nc.dram_tensor("d_y", [nchunk, 128, NCHH * T], BF16, kind=dk).ap()
    d_qr = nc.dram_tensor("d_qr", [128, KD * D], BF16).ap()
    d_rs = nc.dram_tensor("d_rs", [D], F32).ap()
    d_y4p = nc.dram_tensor("d_y4p", [nslot, 128, NCHH * T], BF16).ap()
    d_yag = nc.dram_tensor("d_yag", [2, nslot, 128, NCHH * T], BF16).ap()
    d_yown = nc.dram_tensor("d_yown", [nslot, 128, NCHH * T], BF16).ap()
    d_ypeer = nc.dram_tensor("d_ypeer", [nslot, 128, NCHH * T], BF16).ap()

    ctx = ExitStack()
    with ctx:
        identb = ctx.enter_context(nc.sbuf_tensor("identb", [128, 128], BF16))
        identf = ctx.enter_context(nc.sbuf_tensor("identf", [128, 128], F32))
        a_sb = ctx.enter_context(nc.sbuf_tensor("a_sb", [128, 128], F32))
        cw_sb = ctx.enter_context(nc.sbuf_tensor("cw_sb", [128, NCH * 4], F32))
        cb_sb = ctx.enter_context(nc.sbuf_tensor("cb_sb", [128, NCH], F32))
        bdt_sb = ctx.enter_context(nc.sbuf_tensor("bdt_sb", [128, NCHH], F32))
        ds_sb = ctx.enter_context(nc.sbuf_tensor("ds_sb", [128, NCHH], F32))
        hcar = ctx.enter_context(nc.sbuf_tensor("hcar", [128, 128], F32))
        ones1 = ctx.enter_context(nc.sbuf_tensor("ones1", [1, 128], F32))
        rsb = ctx.enter_context(nc.sbuf_tensor("rsb", [128, D], F32))
        big8 = ctx.enter_context(nc.psum_tensor("big8", [128, 8 * T], F32))
        ps = [big8[:, i * T:(i + 1) * T] for i in range(8)]
        hps = big8[:, :].rearrange("p (a t) -> p a t", t=T)

        LDS = Sem(ctx, nc, "LDS")
        SPS = Sem(ctx, nc, "SPS")
        TPS = Sem(ctx, nc, "TPS")
        VES = Sem(ctx, nc, "VES")
        SCS = Sem(ctx, nc, "SCS")
        GPS = Sem(ctx, nc, "GPS")
        CCS = Sem(ctx, nc, "CCS")
        GPC = Sem(ctx, nc, "GPC")
        BCS = [Sem(ctx, nc, f"BCS{k}") for k in range(4)]
        LD2 = Sem(ctx, nc, "LD2")
        LD3 = Sem(ctx, nc, "LD3")
        STS = Sem(ctx, nc, "STS")

        block = ctx.enter_context(nc.Block())
        G = Guards()

        def on(eng, fn):
            getattr(block, eng)(fn)

        _dyn = {}

        def gdyn(g, key):
            if "phi" not in _dyn:
                _dyn["phi"] = g.partition_id() % 2
            phi = _dyn["phi"]
            if key not in _dyn:
                if key == "uoff":
                    _dyn[key] = phi * NCHH
                elif key == "own":
                    _dyn[key] = phi * nslot
                elif key == "peer":
                    _dyn[key] = (1 - phi) * nslot
                elif key == "peerag":
                    _dyn[key] = 1 - phi
            return _dyn[key]

        def qdma(eng, sem, **kw):
            eng.wait_ge(sem.h, sem.n)
            sem.n += 16
            return eng.dma_start(**kw).then_inc(sem.h, 16)

        RSQRT_MAGIC = 0x5F3759DF

        def emit_ln(xb, xn_out, st, sqscr, ld_wait):
            def _v1(vector):
                def vch(ins):
                    VES.n += 1
                    ins.then_inc(VES.h, 1)
                    vector.wait_ge(VES.h, VES.n)

                vector.wait_ge(LDS.h, ld_wait)
                for s in range(NSUB):
                    vector.tensor_reduce(
                        st["xsum"][:, s:s + 1], xb[:, s, :], AX, ALU.add)
                    vch(vector.tensor_tensor(
                        sqscr[:, :], xb[:, s, :], xb[:, s, :], ALU.mult))
                    vch(vector.tensor_reduce(
                        st["xsq"][:, s:s + 1], sqscr[:, :], AX, ALU.add))
                vch(vector.tensor_scalar(
                    st["mcol"][:, :], st["xsum"][:, :], 1.0 / D, None, ALU.mult))
                vch(vector.tensor_tensor(
                    st["sc1"][:, :], st["mcol"][:, :], st["mcol"][:, :],
                    ALU.mult))
                vch(vector.scalar_tensor_tensor(
                    st["varc"][:, :], st["xsq"][:, :], 1.0 / D,
                    st["sc1"][:, :], ALU.mult, ALU.subtract))
                vch(vector.tensor_scalar(
                    st["veps"][:, :], st["varc"][:, :], EPS, None, ALU.add))
                iv = st["veps"][:, :].bitcast(mybir.dt.int32)
                io = st["rstd"][:, :].bitcast(mybir.dt.int32)
                vch(vector.tensor_scalar(
                    io, iv, 1, None, ALU.arith_shift_right))
                vch(vector.tensor_scalar(
                    io, io, -1, RSQRT_MAGIC, ALU.mult, ALU.add))
                for _ in range(3):
                    vch(vector.tensor_tensor(
                        st["sc1"][:, :], st["rstd"][:, :], st["rstd"][:, :],
                        ALU.mult))
                    vch(vector.tensor_tensor(
                        st["sc2"][:, :], st["veps"][:, :], st["sc1"][:, :],
                        ALU.mult))
                    vch(vector.tensor_scalar(
                        st["sc2"][:, :], st["sc2"][:, :], -0.5, 1.5,
                        ALU.mult, ALU.add))
                    vch(vector.tensor_tensor(
                        st["rstd"][:, :], st["rstd"][:, :], st["sc2"][:, :],
                        ALU.mult))
                ins = None
                for s in range(NSUB):
                    ins = vector.tensor_scalar(
                        xn_out[:, s, :], xb[:, s, :], st["mcol"][:, s:s + 1],
                        st["rstd"][:, s:s + 1], ALU.subtract, ALU.mult)
                ins.then_inc(VES.h, 1)
            on("vector", _v1)
            VES.add(1)
            return VES.n

        s1 = ExitStack()
        wix_sb = s1.enter_context(nc.sbuf_tensor("wix_sb", [128, KD, DI], BF16))
        wiz_sb = s1.enter_context(nc.sbuf_tensor("wiz_sb", [128, KD, DIH], BF16))
        wxp_sb = s1.enter_context(nc.sbuf_tensor("wxp_sb", [128, NCH, 128], BF16))
        wdt_sb = s1.enter_context(nc.sbuf_tensor("wdt_sb", [128, DIH], BF16))
        diag_sb = s1.enter_context(
            nc.sbuf_tensor("diag_sb", [128, NCH * 4, 128], BF16))

        pro = ExitStack()
        stgA = pro.enter_context(nc.sbuf_tensor("stgA", [128, KD, D], F32))
        stgB = pro.enter_context(nc.sbuf_tensor("stgB", [128, KD, D], F32))
        qrT = pro.enter_context(nc.sbuf_tensor("qrT", [128, KD, D], BF16))
        qrS = pro.enter_context(nc.sbuf_tensor("qrS", [128, KD, D], BF16))
        ram8 = pro.enter_context(nc.sbuf_tensor("ram8", [128, KD], F32))
        rs8 = pro.enter_context(nc.sbuf_tensor("rs8", [128, KD], F32))
        rrs8 = pro.enter_context(nc.sbuf_tensor("rrs8", [128, KD], F32))
        qtmp = pro.enter_context(nc.sbuf_tensor("qtmp", [128, D], F32))

        def _sy0(sync):
            qdma(sync, LDS, out=cw_sb[:, :], in_=ap_cw)
            qdma(sync, LDS, out=cb_sb[:, :], in_=ap_cb)
            qdma(sync, LDS, out=bdt_sb[:, :], in_=ap_bdt)
            qdma(sync, LDS, out=ds_sb[:, :], in_=ap_ds)
            qdma(sync, LDS, out=a_sb[:, :], in_=ap_al)
            qdma(sync, LDS, out=stgA[:, :, :],
                 in_=ap_rhs.rearrange("k p e -> p k e"))
        on("sync", _sy0)
        ld_c = LDS.n

        def _gp0(gpsimd):
            for idt in (identb, identf):
                GPC.n += 1
                gpsimd.memset(idt[:, :], 0.0).then_inc(GPC.h, 1)
                gpsimd.wait_ge(GPC.h, GPC.n)
                GPC.n += 1
                gpsimd.affine_select(
                    out=idt[:, :], in_=idt[:, :],
                    compare_op=ALU.not_equal, fill=1.0, base=0,
                    pattern=[[-1, 128]], channel_multiplier=1
                ).then_inc(GPC.h, 1)
                gpsimd.wait_ge(GPC.h, GPC.n)
            GPC.n += 1
            gpsimd.memset(hcar[:, :], 0.0).then_inc(GPC.h, 1)
            GPC.n += 1
            gpsimd.memset(ones1[0:1, :], 1.0).then_inc(GPC.h, 1)
        on("gpsimd", _gp0)
        gp_id = GPC.n

        def _sc0(scalar):
            scalar.wait_ge(LDS.h, ld_c)
            scalar.activation(a_sb[:, :], a_sb[:, :], AF.Exp).then_inc(SCS.h, 1)
        sc_a = SCS.add(1)
        on("scalar", _sc0)

        def _v0(vector):
            vector.wait_ge(SCS.h, sc_a)
            vector.wait_ge(GPC.h, gp_id)
            vector.wait_ge(LDS.h, ld_c)
            vector.tensor_scalar(a_sb[:, :], a_sb[:, :], -1.0, None, ALU.mult)
            ins = None
            for jk in range(NCH * 4):
                ins = vector.tensor_scalar(
                    diag_sb[:, jk, :], identb[:, :], cw_sb[:, jk:jk + 1],
                    None, ALU.mult)
            ins.then_inc(VES.h, 1)
        ve_diag = VES.add(1)
        on("vector", _v0)

        def tr_block(src_ap, dst_fn, wait_fn=None, ident=None, bf=False):
            for idx in range(KD * KD):
                k, ec = divmod(idx, KD)
                bk = idx % 2

                def _pe(tensor, k=k, ec=ec, bk=bk, first=(idx == 0)):
                    if first and wait_fn is not None:
                        wait_fn(tensor)
                    G.wait(tensor, f"ps{bk}")
                    pv = (ps[bk][:, 0:64].bitcast(BF16) if bf
                          else ps[bk][:, 0:128])
                    tensor.transpose(
                        pv, src_ap(k, ec), ident[:, :]).then_inc(TPS.h, 1)
                tp = TPS.add(1)
                on("tensor", _pe)

                def _v(vector, k=k, ec=ec, bk=bk, tp=tp):
                    vector.wait_ge(TPS.h, tp)
                    pv = (ps[bk][:, 0:64].bitcast(BF16) if bf
                          else ps[bk][:, 0:128])
                    vector.tensor_copy(dst_fn(k, ec), pv).then_inc(VES.h, 1)
                ve = VES.add(1)
                on("vector", _v)
                G.set(f"ps{bk}", VES, ve)

        def _w_rhs(tensor):
            tensor.wait_ge(LDS.h, ld_c)
            tensor.wait_ge(VES.h, ve_diag)

        tr_block(lambda k, ec: stgA[:, k, ec * 128:(ec + 1) * 128],
                 lambda k, ec: stgB[:, ec, k * 128:(k + 1) * 128],
                 wait_fn=_w_rhs, ident=identf)
        ve_rT = VES.n

        def _v1(vector):
            def vch(ins):
                VES.n += 1
                ins.then_inc(VES.h, 1)
                vector.wait_ge(VES.h, VES.n)

            vector.wait_ge(VES.h, ve_rT)
            ins = None
            for ec in range(KD):
                ins = vector.tensor_reduce(
                    ram8[:, ec:ec + 1], stgB[:, ec, :], AX, ALU.max,
                    apply_absolute_value=True)
            vch(ins)
            vch(vector.tensor_scalar(
                rs8[:, :], ram8[:, :], 1e-12, I127, ALU.max, ALU.mult))
            vch(vector.reciprocal(rrs8[:, :], rs8[:, :]))
            ins = None
            for ec in range(KD):
                vch(vector.tensor_scalar(
                    qtmp[:, :], stgB[:, ec, :], rrs8[:, ec:ec + 1], None,
                    ALU.mult))
                ins = vector.tensor_scalar(
                    qrT[:, ec, :], qtmp[:, :], RC, -RC, ALU.add, ALU.add)
                if ec < KD - 1:
                    vch(ins)
            ins.then_inc(VES.h, 1)
        on("vector", _v1)
        ve_q = VES.add(1)

        def _sy1(sync):
            sync.wait_ge(VES.h, ve_q)
            for ec in range(KD):
                qdma(sync, SPS, out=d_rs[ec * 128:(ec + 1) * 128],
                     in_=rs8[:, ec:ec + 1])
        on("sync", _sy1)
        sp_rs = SPS.n

        def _sy2(sync):
            sync.wait_ge(SPS.h, sp_rs)
            qdma(sync, LDS, out=rsb[:, :], in_=d_rs.partition_broadcast(128))
        on("sync", _sy2)

        def _w_q(tensor):
            tensor.wait_ge(VES.h, ve_q)

        tr_block(lambda k, ec: qrT[:, ec, k * 128:(k + 1) * 128],
                 lambda k, ec: qrS[:, k, ec * 128:(ec + 1) * 128],
                 wait_fn=_w_q, ident=identb, bf=True)
        ve_qS = VES.n

        def _sy3(sync):
            sync.wait_ge(VES.h, ve_qS)
            qdma(sync, SPS, out=d_qr,
                 in_=qrS[:, :, :].rearrange("p k e -> p (k e)"))
        on("sync", _sy3)

        def ld_cast(src_ap, stg_view, dst_view, stg_guard_key):
            def _s(sync):
                G.wait(sync, stg_guard_key)
                qdma(sync, LDS, out=stg_view, in_=src_ap)
            on("sync", _s)
            ldn = LDS.n

            def _v(vector, ldn=ldn):
                vector.wait_ge(LDS.h, ldn)
                vector.tensor_copy(dst_view, stg_view).then_inc(VES.h, 1)
            ve = VES.add(1)
            on("vector", _v)
            G.set(stg_guard_key, VES, ve)

        sAf = stgA[:, :, :].rearrange("p k e -> p (k e)")
        sBf = stgB[:, :, :].rearrange("p k e -> p (k e)")
        G.set("stgA", VES, ve_rT)
        G.set("stgB", VES, ve_q)
        for h in range(2):
            ld_cast(ap_wix[:, :, h * 1024:(h + 1) * 1024].rearrange(
                        "k p c -> p k c"),
                    stgA[:, :, :],
                    wix_sb[:, :, h * 1024:(h + 1) * 1024], "stgA")
        ld_cast(ap_wiz.rearrange("k p c -> p k c"), stgB[:, :, :],
                wiz_sb[:, :, :], "stgB")
        ld_cast(ap_wxp.rearrange("j p c -> p j c"),
                sAf[:, 0:NCH * 128].rearrange("p (j c) -> p j c", c=128),
                wxp_sb[:, :, :], "stgA")
        ld_cast(ap_wdt, sBf[:, 0:DIH], wdt_sb[:, :], "stgB")
        G.g.pop("stgA", None)
        G.g.pop("stgB", None)

        ve_pro, sp_pro, ld_pro, tp_pro, sc_pro = VES.n, SPS.n, LDS.n, TPS.n, SCS.n

        def _bar(eng):
            eng.wait_ge(VES.h, ve_pro)
            eng.wait_ge(SPS.h, sp_pro)
            eng.wait_ge(LDS.h, ld_pro)
            eng.wait_ge(TPS.h, tp_pro)
            eng.wait_ge(SCS.h, sc_pro)
            eng.wait_ge(GPC.h, gp_id)
        for en in ("sync", "vector", "scalar", "tensor", "gpsimd"):
            on(en, _bar)
        pro.close()

        s1w = ExitStack()
        xsb = [s1w.enter_context(nc.sbuf_tensor(f"xsb{b}", [128, NSUB, D], F32))
               for b in range(2)]
        xnb = s1w.enter_context(nc.sbuf_tensor("xnb", [128, NSUB, D], BF16))
        xnT = s1w.enter_context(nc.sbuf_tensor("xnT", [128, KD, T], BF16))
        xinT = s1w.enter_context(nc.sbuf_tensor("xinT", [128, NCH, T + 3], BF16))
        ubf = s1w.enter_context(nc.sbuf_tensor("ubf", [128, NCH, T], BF16))
        xdT = s1w.enter_context(nc.sbuf_tensor("xdT", [128, T], BF16))
        bcst = [s1w.enter_context(nc.sbuf_tensor(f"bcst{b}", [128, T], F32))
                for b in range(2)]
        et8 = s1w.enter_context(nc.sbuf_tensor("et8", [128, NCHH, T], F32))
        szst = [s1w.enter_context(nc.sbuf_tensor(f"szst{b}", [128, T], F32))
                for b in range(2)]
        ufst = [s1w.enter_context(nc.sbuf_tensor(f"ufst{b}", [128, T], F32))
                for b in range(2)]
        sqscr = s1w.enter_context(nc.sbuf_tensor("sqscr", [128, D], F32))
        st4 = {n: s1w.enter_context(nc.sbuf_tensor("st_" + n, [128, NSUB], F32))
               for n in ("xsum", "xsq", "mcol", "varc", "veps", "rstd",
                         "sc1", "sc2")}

        def _gpz(gpsimd):
            GPC.n += 1
            gpsimd.memset(xinT[:, :, 0:3], 0.0).then_inc(GPC.h, 1)
        on("gpsimd", _gpz)
        gp_h0 = GPC.n

        xfree = {}
        szst_free = [0, 0]
        ufst_free = [0, 0]
        et_free = [0] * NCHH
        xld = {}

        def emit_sx(i2):
            def _sx(gpsimd, i2=i2):
                if i2 >= 2:
                    s, v = xfree[i2 - 2]
                    gpsimd.wait_ge(s.h, v)
                qdma(gpsimd, LDS, out=xsb[i2 % 2][:, :, :],
                     in_=ap_x[i2 * T:(i2 + 1) * T, :].rearrange(
                         "(s p) d -> p s d", p=128))
            on("gpsimd", _sx)
            xld[i2] = LDS.n

        emit_sx(0)
        emit_sx(1)
        for i in range(nchunk):
            xb = xsb[i % 2]
            ld_x = xld[i]

            ve_xn1 = emit_ln(xb, xb, st4, sqscr, ld_x)

            def _vcast(vector, xb=xb, ve_xn1=ve_xn1):
                vector.wait_ge(VES.h, ve_xn1)
                vector.tensor_copy(
                    xnb[:, :, :].rearrange("p s d -> p (s d)"),
                    xb[:, :, :].rearrange("p s d -> p (s d)")
                ).then_inc(VES.h, 1)
            ve_xnb = VES.add(1)
            on("vector", _vcast)
            xfree[i] = (VES, ve_xnb)

            for j in range(KD):
                bk = j % 2

                def _pt(tensor, j=j, bk=bk, ve_xnb=ve_xnb, i=i):
                    if j == 0:
                        tensor.wait_ge(VES.h, ve_xnb)
                        if i == 0:
                            tensor.wait_ge(GPC.h, gp_h0)
                    G.wait(tensor, f"ps{bk}")
                    pv = ps[bk][:, 0:256].bitcast(BF16)
                    ins = None
                    for s in range(NSUB):
                        ins = tensor.transpose(
                            pv[:, s * 128:(s + 1) * 128],
                            xnb[:, s, j * 128:(j + 1) * 128], identb[:, :])
                    ins.then_inc(TPS.h, 1)
                tp = TPS.add(1)
                on("tensor", _pt)

                def _vt(vector, j=j, bk=bk, tp=tp):
                    vector.wait_ge(TPS.h, tp)
                    vector.tensor_copy(
                        xnT[:, j, :], ps[bk][:, 0:256].bitcast(BF16)
                    ).then_inc(VES.h, 1)
                ve = VES.add(1)
                on("vector", _vt)
                G.set(f"ps{bk}", VES, ve)
            ve_xnT = VES.n

            sc_zu = SCS.n
            sps_zu = SPS.n
            ve_xin_evac = {}
            tp_w = {}
            for m in range(NCH + NCHH):
                bk = 4 + m % 4

                def _pw(tensor, m=m, bk=bk, ve_xnT=ve_xnT):
                    if m == 0:
                        tensor.wait_ge(VES.h, ve_xnT)
                    G.wait(tensor, f"ps{bk}")
                    ins = None
                    for k in range(KD):
                        if m < NCH:
                            stat = wix_sb[:, k, m * 128:(m + 1) * 128]
                        else:
                            stat = wiz_sb[:, k, (m - NCH) * 128:(m - NCH + 1) * 128]
                        ins = tensor.matmul(
                            ps[bk][:, :], stat, xnT[:, k, :],
                            start=(k == 0), stop=(k == KD - 1))
                    ins.then_inc(TPS.h, 1)
                tp_w[m] = TPS.add(1)
                on("tensor", _pw)

                if m < NCH:
                    def _vw(vector, m=m, bk=bk, tp=tp_w[m]):
                        vector.wait_ge(TPS.h, tp)
                        vector.tensor_copy(
                            xinT[:, m, 3:T + 3], ps[bk][:, :]).then_inc(VES.h, 1)
                    ve = VES.add(1)
                    on("vector", _vw)
                    G.set(f"ps{bk}", VES, ve)
                    ve_xin_evac[m] = ve
                else:
                    G.set(f"ps{bk}", SCS, sc_zu + (m - NCH) + 1)

            tp_c = {}
            for j in range(NCH):
                bk = j % 2

                def _pc(tensor, j=j, bk=bk, ve_evj=ve_xin_evac[j]):
                    tensor.wait_ge(VES.h, ve_evj)
                    G.wait(tensor, f"ps{bk}")
                    ins = None
                    for k in range(4):
                        ins = tensor.matmul(
                            ps[bk][:, :], diag_sb[:, j * 4 + k, :],
                            xinT[:, j, k:k + T], start=(k == 0), stop=(k == 3))
                    ins.then_inc(TPS.h, 1)
                tp_c[j] = TPS.add(1)
                on("tensor", _pc)
                G.set(f"ps{bk}", SCS, sc_zu + NCHH + j + 1)

            def _szu(scalar, i=i, tp_w=dict(tp_w), tp_c=dict(tp_c),
                     sc_zu=sc_zu, sps_zu=sps_zu, szf=tuple(szst_free),
                     uff=tuple(ufst_free)):
                for zc in range(NCHH):
                    scalar.wait_ge(TPS.h, tp_w[NCH + zc])
                    if zc >= 2:
                        scalar.wait_ge(SPS.h, sps_zu + (zc - 1) * 16)
                    elif szf[zc % 2]:
                        scalar.wait_ge(SPS.h, szf[zc % 2])
                    SCS.n += 1
                    scalar.activation(
                        szst[zc % 2][:, :], ps[4 + zc % 4][:, :], AF.Silu
                    ).then_inc(SCS.h, 1)
                for j in range(NCH):
                    scalar.wait_ge(TPS.h, tp_c[j])
                    if j < NCHH:
                        if j >= 2:
                            scalar.wait_ge(SPS.h, sps_zu + 128 + (j - 1) * 16)
                        elif uff[j % 2]:
                            scalar.wait_ge(SPS.h, uff[j % 2])
                        scalar.activation(
                            ubf[:, j, :], ps[j % 2][:, :], AF.Silu,
                            bias=cb_sb[:, j:j + 1])
                        SCS.n += 1
                        scalar.activation(
                            ufst[j % 2][:, :], ps[j % 2][:, :], AF.Silu,
                            bias=cb_sb[:, j:j + 1]).then_inc(SCS.h, 1)
                    else:
                        SCS.n += 1
                        scalar.activation(
                            ubf[:, j, :], ps[j % 2][:, :], AF.Silu,
                            bias=cb_sb[:, j:j + 1]).then_inc(SCS.h, 1)
            on("scalar", _szu)
            assert SCS.n == sc_zu + NCHH + NCH

            for zc in range(NCHH):
                def _syz(sync, i=i, zc=zc, sc=sc_zu + zc + 1):
                    sync.wait_ge(SCS.h, sc)
                    qdma(sync, SPS, out=d_sz[i, :, zc, :],
                         in_=szst[zc % 2][:, :])
                on("sync", _syz)
            for j in range(NCHH):
                def _syu(sync, i=i, j=j, sc=sc_zu + NCHH + j + 1):
                    sync.wait_ge(SCS.h, sc)
                    qdma(sync, SPS, out=d_u[i, :, j, :], in_=ufst[j % 2][:, :])
                on("sync", _syu)
            assert SPS.n == sps_zu + 256
            szst_free[0] = sps_zu + 7 * 16
            szst_free[1] = sps_zu + 8 * 16
            ufst_free[0] = sps_zu + 128 + 7 * 16
            ufst_free[1] = sps_zu + 128 + 8 * 16
            sc_u_all = SCS.n

            def _pxd(tensor, sc_u_all=sc_u_all):
                tensor.wait_ge(SCS.h, sc_u_all)
                G.wait(tensor, "ps6")
                ins = None
                for k in range(NCH):
                    ins = tensor.matmul(
                        ps[6][:, :], wxp_sb[:, k, :], ubf[:, k, :],
                        start=(k == 0), stop=(k == NCH - 1))
                ins.then_inc(TPS.h, 1)
            tp_xd = TPS.add(1)
            on("tensor", _pxd)

            bcb = i % 2

            def _vxd(vector, bcb=bcb, tp_xd=tp_xd):
                vector.wait_ge(TPS.h, tp_xd)
                G.wait(vector, f"bcst{bcb}")
                vector.tensor_copy(xdT[:, :], ps[6][:, :])
                vector.tensor_copy(
                    bcst[bcb][64:96, :], ps[6][64:96, :]).then_inc(VES.h, 1)
            ve_xd = VES.add(1)
            on("vector", _vxd)
            G.set("ps6", VES, ve_xd)

            def _sybc(sync, i=i, bcb=bcb, ve_xd=ve_xd):
                sync.wait_ge(VES.h, ve_xd)
                qdma(sync, SPS, out=d_bc[i, :, :], in_=bcst[bcb][64:96, :])
            on("sync", _sybc)
            sp = SPS.n
            G.set(f"bcst{bcb}", SPS, sp)

            scd = SCS.n
            tp_d = {}
            for dh in range(NCHH):
                bk = 2 + dh % 2

                def _pd(tensor, dh=dh, bk=bk, ve_xd=ve_xd):
                    if dh == 0:
                        tensor.wait_ge(VES.h, ve_xd)
                    G.wait(tensor, f"ps{bk}")
                    tensor.matmul(
                        ps[bk][:, :], wdt_sb[:, dh * 128:(dh + 1) * 128],
                        xdT[:, :], start=True, stop=True).then_inc(TPS.h, 1)
                tp_d[dh] = TPS.add(1)
                on("tensor", _pd)
                G.set(f"ps{bk}", SCS, scd + dh + 1)

            def _sdel(scalar, i=i, tp_d=dict(tp_d), scd=scd,
                      etf=tuple(et_free)):
                for dh in range(NCHH):
                    scalar.wait_ge(TPS.h, tp_d[dh])
                    if etf[dh]:
                        scalar.wait_ge(SPS.h, etf[dh])
                    SCS.n += 1
                    scalar.activation(
                        et8[:, dh, :], ps[2 + dh % 2][:, :], AF.Exp,
                        bias=bdt_sb[:, dh:dh + 1]).then_inc(SCS.h, 1)
                scalar.wait_ge(SCS.h, scd + NCHH)
                for dh in range(NCHH):
                    SCS.n += 1
                    scalar.activation(
                        et8[:, dh, :], et8[:, dh, :], AF.Ln, bias=1.0
                    ).then_inc(SCS.h, 1)
            on("scalar", _sdel)
            assert SCS.n == scd + 2 * NCHH

            for dh in range(NCHH):
                def _syd(sync, i=i, dh=dh, lnv=scd + NCHH + dh + 1):
                    sync.wait_ge(SCS.h, lnv)
                    qdma(sync, SPS, out=d_delta[i, :, dh, :],
                         in_=et8[:, dh, :])
                on("sync", _syd)
                et_free[dh] = SPS.n

            tp_now = TPS.n

            ve_xin_last = ve_xin_evac[NCH - 1]

            def _vh(vector, tp_now=tp_now, ve_xin_last=ve_xin_last):
                vector.wait_ge(TPS.h, tp_now)
                vector.wait_ge(VES.h, ve_xin_last)
                vector.tensor_copy(
                    xinT[:, :, 0:3], xinT[:, :, T:T + 3]).then_inc(VES.h, 1)
            VES.add(1)
            on("vector", _vh)

            if i + 2 < nchunk:
                emit_sx(i + 2)

        m1 = (VES.n, SPS.n, LDS.n, TPS.n, SCS.n, GPS.n)

        def _bar1(eng, m1=m1):
            eng.wait_ge(VES.h, m1[0])
            eng.wait_ge(SPS.h, m1[1])
            eng.wait_ge(LDS.h, m1[2])
            eng.wait_ge(TPS.h, m1[3])
            eng.wait_ge(SCS.h, m1[4])
            eng.wait_ge(GPS.h, m1[5])
        for en in ("sync", "vector", "scalar", "tensor", "gpsimd"):
            on(en, _bar1)
        s1w.close()
        s1.close()

        s2 = ExitStack()
        delta_a = s2.enter_context(nc.sbuf_tensor("delta_a", [128, NCHH * T], F32))
        u_a = s2.enter_context(nc.sbuf_tensor("u_a", [128, NCHH * T], F32))
        sz_a = s2.enter_context(nc.sbuf_tensor("sz_a", [128, NCHH * T], F32))
        dtu_a = s2.enter_context(nc.sbuf_tensor("dtu_a", [128, NCHH * T], F32))
        yacc = [s2.enter_context(nc.sbuf_tensor(f"yacc{b}", [128, NCHH * T], F32))
                for b in range(2)]
        yp_blk = [s2.enter_context(nc.sbuf_tensor(f"yp_blk{b}", [128, NCHH * T], F32))
                  for b in range(2)]
        dbu_b = s2.enter_context(nc.sbuf_tensor("dbu_b", [128, NCHH * T], F32))
        dAb = [s2.enter_context(nc.sbuf_tensor(f"dAb{b}", [128, T], F32))
               for b in range(4)]
        Bb = [s2.enter_context(nc.sbuf_tensor(f"Bb{b}", [128, T], F32))
              for b in range(2)]
        Cb = [s2.enter_context(nc.sbuf_tensor(f"Cb{b}", [128, T], F32))
              for b in range(2)]
        ybf = s2.enter_context(nc.sbuf_tensor("ybf", [128, NCHH * T], BF16))

        prev_vn_base = None
        prev_tts_last = None
        prev_tts_last2 = None

        def bc_load(i, n):
            nb = n % 2

            def _s(sync, i=i, n=n, nb=nb):
                G.wait(sync, f"Bb{nb}")
                qdma(sync, BCS[nb], out=Bb[nb][:, :],
                     in_=d_bc[i, n].unsqueeze(0).partition_broadcast(128))
                G.wait(sync, f"Cb{nb}")
                qdma(sync, BCS[2 + nb], out=Cb[nb][:, :],
                     in_=d_bc[i, 16 + n].unsqueeze(0).partition_broadcast(128))
            on("sync", _s)
            return (BCS[nb].n, BCS[2 + nb].n)

        for i in range(nchunk):
            def _sl(sync, i=i):
                G.wait(sync, "delta_a")
                qdma(sync, LDS, out=delta_a[:, :],
                     in_=d_delta[i].rearrange("p a t -> p (a t)"))
                G.wait(sync, "sz_a")
                qdma(sync, LD2, out=sz_a[:, :],
                     in_=d_sz[i].rearrange("p a t -> p (a t)"))
                G.wait(sync, "u_a")
                qdma(sync, LD3, out=u_a[:, :],
                     in_=d_u[i].rearrange("p a t -> p (a t)"))
            on("sync", _sl)
            ld_ds = LDS.n
            gp_u = LD3.n
            ld_sz = LD2.n

            def _vdtu(vector, ld_ds=ld_ds, gp_u=gp_u):
                vector.wait_ge(LDS.h, ld_ds)
                vector.wait_ge(LD3.h, gp_u)
                G.wait(vector, "dtu_a")
                vector.tensor_tensor(
                    dtu_a[:, :], delta_a[:, :], u_a[:, :], ALU.mult
                ).then_inc(VES.h, 1)
            ve_dtu = VES.add(1)
            on("vector", _vdtu)
            ve_dtu_g = [ve_dtu]

            if i == 0:
                bc_ready = [bc_load(0, 0), bc_load(0, 1)]

            gpc_carry_prev = GPC.n
            for n in range(NST):
                nb = n % 2
                vn_base = VES.n
                sc_bc = bc_ready[n]

                sc_base = SCS.n

                def _se(scalar, i=i, n=n, ld_ds=ld_ds, vn_base=vn_base,
                        prev_vn=prev_vn_base):
                    for dh in range(NCHH):
                        if dh == 0 and n == 0:
                            scalar.wait_ge(LDS.h, ld_ds)
                        if dh >= 4:
                            fa = vn_base + (dh - 4) + 2
                        elif prev_vn is not None:
                            fa = prev_vn + (dh + 4) + 2
                        else:
                            fa = 0
                        if fa:
                            scalar.wait_ge(VES.h, fa)
                        SCS.n += 1
                        scalar.activation(
                            dAb[dh % 4][:, :], delta_a[:, dh * T:(dh + 1) * T],
                            AF.Exp, scale=a_sb[:, dh * 16 + n:dh * 16 + n + 1]
                        ).then_inc(SCS.h, 1)
                on("scalar", _se)
                sc_base2 = sc_base

                def _vn(vector, i=i, n=n, nb=nb, vn_base=vn_base,
                        sc_bc=sc_bc, sc_base=sc_base2, ve_dtu=ve_dtu,
                        gpcw=gpc_carry_prev):
                    if n == 0:
                        vector.wait_ge(VES.h, ve_dtu)
                    vector.wait_ge(VES.h, vn_base)
                    vector.wait_ge(BCS[nb].h, sc_bc[0])
                    vector.wait_ge(BCS[2 + nb].h, sc_bc[1])
                    VES.n += 1
                    vector.tensor_tensor(
                        dbu_b[:, :].rearrange("p (a t) -> p a t", t=T),
                        dtu_a[:, :].rearrange("p (a t) -> p a t", t=T),
                        Bb[nb][:, :].unsqueeze(1).broadcast_to([128, NCHH, T]),
                        ALU.mult).then_inc(VES.h, 1)
                    vector.wait_ge(VES.h, VES.n)
                    for dh in range(NCHH):
                        vector.wait_ge(SCS.h, sc_base + dh + 1)
                        VES.n += 1
                        vector.tensor_tensor_scan(
                            hps[:, dh, :], dAb[dh % 4][:, :],
                            dbu_b[:, dh * T:(dh + 1) * T],
                            hcar[:, n * 8 + dh:n * 8 + dh + 1],
                            ALU.mult, ALU.add).then_inc(VES.h, 1)
                    vector.wait_ge(VES.h, VES.n)
                    vector.wait_ge(GPC.h, gpcw)
                    VES.n += 1
                    if n == 0:
                        vector.tensor_tensor(
                            yacc[0][:, :].rearrange("p (a t) -> p a t", t=T),
                            hps[:, :, :],
                            Cb[nb][:, :].unsqueeze(1).broadcast_to(
                                [128, NCHH, T]), ALU.mult).then_inc(VES.h, 1)
                    else:
                        vector.tensor_tensor(
                            yp_blk[nb][:, :].rearrange("p (a t) -> p a t", t=T),
                            hps[:, :, :],
                            Cb[nb][:, :].unsqueeze(1).broadcast_to(
                                [128, NCHH, T]), ALU.mult).then_inc(VES.h, 1)
                on("vector", _vn)
                tts_last = vn_base + 1 + NCHH
                ve_y = VES.n
                G.set(f"Bb{nb}", VES, vn_base + 1)
                G.set(f"Cb{nb}", VES, ve_y)

                def _sc(scalar, n=n, vn_base=vn_base):
                    for dh in range(NCHH):
                        scalar.wait_ge(VES.h, vn_base + 2 + dh)
                        SCS.n += 1
                        scalar.copy(
                            hcar[:, n * 8 + dh:n * 8 + dh + 1],
                            big8[:, dh * T + T - 1:dh * T + T]
                        ).then_inc(SCS.h, 1)
                on("scalar", _sc)

                gpc_carry_prev = GPC.n
                if n >= 2:
                    def _gp(gpsimd, pn=n - 1, gate=vn_base + 1 + NCHH):
                        gpsimd.wait_ge(VES.h, gate)
                        gpsimd.wait_ge(GPC.h, GPC.n)
                        GPC.n += 1
                        gpsimd.tensor_tensor(
                            yacc[pn % 2][:, :], yacc[1 - pn % 2][:, :],
                            yp_blk[pn % 2][:, :], ALU.add).then_inc(GPC.h, 1)
                    on("gpsimd", _gp)
                if n == NST - 1:
                    def _gpl(gpsimd, n=n, ve_y=ve_y):
                        gpsimd.wait_ge(VES.h, ve_y)
                        gpsimd.wait_ge(GPC.h, GPC.n)
                        GPC.n += 1
                        gpsimd.tensor_tensor(
                            yacc[n % 2][:, :], yacc[1 - n % 2][:, :],
                            yp_blk[n % 2][:, :], ALU.add).then_inc(GPC.h, 1)
                    on("gpsimd", _gpl)
                gpc_add_last = GPC.n

                if n + 2 < NST:
                    bc_ready.append(bc_load(i, n + 2))
                elif i + 1 < nchunk:
                    bc_ready.append(bc_load(i + 1, n + 2 - NST))
                else:
                    bc_ready.append(None)

                prev_vn_base = vn_base
            bc_ready = bc_ready[NST:]

            def _vyf(vector, i=i, ld_sz=ld_sz, gpl=gpc_add_last):
                vector.wait_ge(VES.h, VES.n)
                vector.wait_ge(GPC.h, gpl)
                vector.wait_ge(LD2.h, ld_sz)
                ins = None
                for dh in range(NCHH):
                    ins = vector.scalar_tensor_tensor(
                        yp_blk[0][:, dh * T:(dh + 1) * T],
                        u_a[:, dh * T:(dh + 1) * T],
                        ds_sb[:, dh:dh + 1],
                        yacc[(NST - 1) % 2][:, dh * T:(dh + 1) * T],
                        ALU.mult, ALU.add)
                VES.n += 1
                ins.then_inc(VES.h, 1)
                vector.wait_ge(VES.h, VES.n)
                G.wait(vector, "ybf")
                vector.tensor_tensor(
                    ybf[:, :], yp_blk[0][:, :], sz_a[:, :],
                    ALU.mult).then_inc(VES.h, 1)
            on("vector", _vyf)
            ve_yf = VES.add(1)
            G.set("delta_a", SCS, SCS.n)
            G.set("u_a", VES, ve_yf)
            G.set("sz_a", VES, ve_yf)
            G.set("dtu_a", VES, ve_yf)

            def _syy(sync, i=i, ve_yf=ve_yf):
                sync.wait_ge(VES.h, ve_yf)
                qdma(sync, SPS, out=d_y[i], in_=ybf[:, :])
            on("sync", _syy)
            sp_y = SPS.n
            G.set("ybf", SPS, sp_y)

        sp_all_y = SPS.n

        def _gx(gpsimd, sp_all_y=sp_all_y):
            gpsimd.wait_ge(SPS.h, sp_all_y)
            pb = gdyn(gpsimd, "peer")
            own = gdyn(gpsimd, "own")
            pag = gdyn(gpsimd, "peerag")
            qdma(gpsimd, GPS, out=d_y4p,
                 in_=d_y[bass.ds(pb, nslot)])
            qdma(gpsimd, GPS, out=d_yown,
                 in_=d_y[bass.ds(own, nslot)])
            gpsimd.wait_ge(GPS.h, GPS.n)
            CCS.n += 1
            gpsimd.collective_compute(
                "AllGather", ALU.bypass, replica_groups=GROUPS,
                ins=[d_y4p], outs=[d_yag]).then_inc(CCS.h, 1)
            gpsimd.wait_ge(CCS.h, CCS.n)
            qdma(gpsimd, GPS, out=d_ypeer,
                 in_=d_yag[bass.ds(pag, 1)].squeeze(0))
        gps_pre_xch = GPS.n
        on("gpsimd", _gx)
        gp_xch = GPS.n

        m2 = (VES.n, SPS.n, LDS.n, SCS.n, gps_pre_xch)

        def _bar2(eng, m2=m2):
            eng.wait_ge(VES.h, m2[0])
            eng.wait_ge(SPS.h, m2[1])
            eng.wait_ge(LDS.h, m2[2])
            eng.wait_ge(SCS.h, m2[3])
            eng.wait_ge(GPS.h, m2[4])
        for en in ("sync", "vector", "scalar", "tensor", "gpsimd"):
            on(en, _bar2)
        s2.close()

        s3 = ExitStack()
        wout_sb = s3.enter_context(nc.sbuf_tensor("wout_sb", [128, NCH, D], BF16))
        qr_sb = s3.enter_context(nc.sbuf_tensor("qr_sb", [128, KD, D], BF16))
        xs3 = s3.enter_context(nc.sbuf_tensor("xs3", [128, NSUB, D], F32))
        xn3 = s3.enter_context(nc.sbuf_tensor("xn3", [128, NSUB, D], F32))
        q3 = s3.enter_context(nc.sbuf_tensor("q3", [128, NSUB, D], BF16))
        qT3 = s3.enter_context(nc.sbuf_tensor("qT3", [128, KD, T], BF16))
        ffn3 = s3.enter_context(nc.sbuf_tensor("ffn3", [128, NSUB, D], F32))
        comb3 = s3.enter_context(nc.sbuf_tensor("comb3", [128, NSUB, D], F32))
        ybf3 = s3.enter_context(nc.sbuf_tensor("ybf3", [128, NCH, T], BF16))
        outs3 = [s3.enter_context(nc.sbuf_tensor(f"out{b}", [128, D], F32))
                 for b in range(2)]
        qt3 = s3.enter_context(nc.sbuf_tensor("qt3", [128, D], F32))
        am3 = s3.enter_context(nc.sbuf_tensor("am3", [128, NSUB], F32))
        ls3 = s3.enter_context(nc.sbuf_tensor("ls3", [128, NSUB], F32))
        rls3 = s3.enter_context(nc.sbuf_tensor("rls3", [128, NSUB], F32))
        sq3 = s3.enter_context(nc.sbuf_tensor("sq3", [128, D], F32))
        st3 = {n: s3.enter_context(nc.sbuf_tensor("s3_" + n, [128, NSUB], F32))
               for n in ("xsum", "xsq", "mcol", "varc", "veps", "rstd",
                         "sc1", "sc2")}

        w3 = ExitStack()
        wstg = w3.enter_context(nc.sbuf_tensor("wstg", [128, KD * D], F32))
        for hh in range(2):
            def _sw(sync, hh=hh):
                if hh == 1:
                    G.wait(sync, "wstg")
                qdma(sync, LDS,
                     out=wstg[:, :].rearrange("p (j d) -> p j d", d=D),
                     in_=ap_wo[hh * KD:(hh + 1) * KD].rearrange(
                         "j p d -> p j d"))
            on("sync", _sw)
            ldw = LDS.n

            def _vw3(vector, hh=hh, ldw=ldw):
                vector.wait_ge(LDS.h, ldw)
                vector.tensor_copy(
                    wout_sb[:, hh * KD:(hh + 1) * KD, :],
                    wstg[:, :].rearrange("p (j d) -> p j d", d=D)
                ).then_inc(VES.h, 1)
            ve = VES.add(1)
            on("vector", _vw3)
            G.set("wstg", VES, ve)

        def _sq3(sync):
            qdma(sync, LDS,
                 out=qr_sb[:, :, :].rearrange("p k e -> p (k e)"), in_=d_qr)
        on("sync", _sq3)
        ld_qr = LDS.n

        ve_w3 = VES.n

        def _bar3(eng, ve_w3=ve_w3):
            eng.wait_ge(VES.h, ve_w3)
        for en in ("sync", "vector", "scalar", "tensor", "gpsimd"):
            on(en, _bar3)
        w3.close()

        for j in range(nslot):
            def _sx3(sync, j=j):
                G.wait(sync, "xs3")
                qdma(sync, LDS, out=xs3[:, :, :],
                     in_=ap_xo[j * T:(j + 1) * T, :].rearrange(
                         "(s p) d -> p s d", p=128))
            on("sync", _sx3)
            ld_x3 = LDS.n

            def _gy(sync, j=j):
                G.wait(sync, "ybf3")
                sync.wait_ge(GPS.h, gp_xch)
                qdma(sync, LDS,
                     out=ybf3[:, 0:NCHH, :].rearrange("p a t -> p (a t)"),
                     in_=d_yown[j])
                qdma(sync, LDS,
                     out=ybf3[:, NCHH:NCH, :].rearrange("p a t -> p (a t)"),
                     in_=d_ypeer[j])
            on("sync", _gy)
            gp_y = LDS.n

            ve_xn3 = emit_ln(xs3, xn3, st3, sq3, ld_x3)

            def _vq(vector, j=j, ve_xn3=ve_xn3):
                def vch(ins):
                    VES.n += 1
                    ins.then_inc(VES.h, 1)
                    vector.wait_ge(VES.h, VES.n)

                vector.wait_ge(VES.h, ve_xn3)
                ins = None
                for s in range(NSUB):
                    ins = vector.tensor_reduce(
                        am3[:, s:s + 1], xn3[:, s, :], AX, ALU.max,
                        apply_absolute_value=True)
                vch(ins)
                vch(vector.tensor_scalar(
                    ls3[:, :], am3[:, :], 1e-12, I127, ALU.max, ALU.mult))
                vch(vector.reciprocal(rls3[:, :], ls3[:, :]))
                ins = None
                for s in range(NSUB):
                    vch(vector.tensor_scalar(
                        qt3[:, :], xn3[:, s, :], rls3[:, s:s + 1], None,
                        ALU.mult))
                    ins = vector.tensor_scalar(
                        q3[:, s, :], qt3[:, :], RC, -RC, ALU.add, ALU.add)
                    if s < NSUB - 1:
                        vch(ins)
                ins.then_inc(VES.h, 1)
            on("vector", _vq)
            ve_q3 = VES.add(1)

            for k in range(KD):
                bk = k % 2

                def _pt3(tensor, k=k, bk=bk, ve_q3=ve_q3):
                    if k == 0:
                        tensor.wait_ge(VES.h, ve_q3)
                    G.wait(tensor, f"ps{bk}")
                    pv = ps[bk][:, 0:256].bitcast(BF16)
                    ins = None
                    for s in range(NSUB):
                        ins = tensor.transpose(
                            pv[:, s * 128:(s + 1) * 128],
                            q3[:, s, k * 128:(k + 1) * 128], identb[:, :])
                    ins.then_inc(TPS.h, 1)
                tp = TPS.add(1)
                on("tensor", _pt3)

                def _vt3(vector, k=k, bk=bk, tp=tp):
                    vector.wait_ge(TPS.h, tp)
                    vector.tensor_copy(
                        qT3[:, k, :], ps[bk][:, 0:256].bitcast(BF16)
                    ).then_inc(VES.h, 1)
                ve = VES.add(1)
                on("vector", _vt3)
                G.set(f"ps{bk}", VES, ve)
            ve_qT = VES.n

            for s in range(NSUB):
                def _pf(tensor, j=j, s=s, ve_qT=ve_qT, ld_qr=ld_qr):
                    if j == 0 and s == 0:
                        tensor.wait_ge(LDS.h, ld_qr)
                    if s == 0:
                        tensor.wait_ge(VES.h, ve_qT)
                    G.wait(tensor, "ps2")
                    G.wait(tensor, "ps3")
                    ins = None
                    for k in range(KD):
                        stat = qT3[:, k, s * 128:(s + 1) * 128]
                        tensor.matmul(
                            ps[2][:, :], stat, qr_sb[:, k, 0:T],
                            start=(k == 0), stop=(k == KD - 1))
                        ins = tensor.matmul(
                            ps[3][:, :], stat, qr_sb[:, k, T:D],
                            start=(k == 0), stop=(k == KD - 1))
                    ins.then_inc(TPS.h, 1)
                tp_f = TPS.add(1)
                on("tensor", _pf)

                def _sdq(scalar, s=s, tp_f=tp_f):
                    scalar.wait_ge(TPS.h, tp_f)
                    scalar.activation(
                        ffn3[:, s, 0:T], ps[2][:, :], AF.Copy,
                        scale=ls3[:, s:s + 1])
                    scalar.activation(
                        ffn3[:, s, T:D], ps[3][:, :], AF.Copy,
                        scale=ls3[:, s:s + 1]).then_inc(SCS.h, 1)
                sc_dq = SCS.add(1)
                on("scalar", _sdq)
                G.set("ps2", SCS, sc_dq)
                G.set("ps3", SCS, sc_dq)

                def _vdq(vector, s=s, sc_dq=sc_dq):
                    vector.wait_ge(SCS.h, sc_dq)
                    VES.n += 1
                    vector.tensor_tensor(
                        ffn3[:, s, :], ffn3[:, s, :], rsb[:, :], ALU.mult
                    ).then_inc(VES.h, 1)
                    vector.wait_ge(VES.h, VES.n)
                    vector.tensor_tensor(
                        comb3[:, s, :], xs3[:, s, :], ffn3[:, s, :], ALU.add
                    ).then_inc(VES.h, 1)
                on("vector", _vdq)
                ve_cmb = VES.add(1)

                def _po(tensor, j=j, s=s, gp_y=gp_y):
                    if s == 0:
                        tensor.wait_ge(LDS.h, gp_y)
                    G.wait(tensor, "ps4")
                    G.wait(tensor, "ps5")
                    ins = None
                    for jj in range(NCH):
                        stat = ybf3[:, jj, s * 128:(s + 1) * 128]
                        tensor.matmul(
                            ps[4][:, :], stat, wout_sb[:, jj, 0:T],
                            start=(jj == 0), stop=(jj == NCH - 1))
                        ins = tensor.matmul(
                            ps[5][:, :], stat, wout_sb[:, jj, T:D],
                            start=(jj == 0), stop=(jj == NCH - 1))
                    ins.then_inc(TPS.h, 1)
                tp_o = TPS.add(1)
                on("tensor", _po)

                ob = s % 2

                def _vo(vector, s=s, ob=ob, tp_o=tp_o, ve_cmb=ve_cmb):
                    vector.wait_ge(TPS.h, tp_o)
                    vector.wait_ge(VES.h, ve_cmb)
                    G.wait(vector, f"out{ob}")
                    vector.tensor_tensor(
                        outs3[ob][:, 0:T], comb3[:, s, 0:T], ps[4][:, :],
                        ALU.add)
                    vector.tensor_tensor(
                        outs3[ob][:, T:D], comb3[:, s, T:D], ps[5][:, :],
                        ALU.add).then_inc(VES.h, 1)
                ve_o = VES.add(1)
                on("vector", _vo)
                G.set("ps4", VES, ve_o)
                G.set("ps5", VES, ve_o)

                def _syo(sync, j=j, s=s, ob=ob, ve_o=ve_o):
                    sync.wait_ge(VES.h, ve_o)
                    qdma(sync, STS,
                         out=ap_out[j * T + s * 128:j * T + (s + 1) * 128, :],
                         in_=outs3[ob][:, :])
                on("sync", _syo)
                stn = STS.n
                G.set(f"out{ob}", STS, stn)

            G.set("xs3", VES, VES.n)
            G.set("ybf3", TPS, TPS.n)

        def _fin(sync):
            sync.wait_ge(STS.h, STS.n)
        on("sync", _fin)

        s3.close()
    return nc


def kernel(L0_attn_x, L0_attn_W_in, L0_attn_conv_w, L0_attn_conv_b,
           L0_attn_W_xproj, L0_attn_W_dt, L0_attn_b_dt, L0_attn_A_log,
           L0_attn_D_skip, L0_attn_W_out, L0_ffn_rhs):
    B, L, _ = L0_attn_x.shape
    LH = L // 2
    f32 = np.float32
    x_np = np.asarray(L0_attn_x, f32)
    w_in = np.asarray(L0_attn_W_in, f32)
    conv_w = np.asarray(L0_attn_conv_w, f32)
    conv_b = np.asarray(L0_attn_conv_b, f32)
    wxp = np.asarray(L0_attn_W_xproj, f32)
    wdt = np.asarray(L0_attn_W_dt, f32)
    bdt = np.asarray(L0_attn_b_dt, f32)
    alog = np.asarray(L0_attn_A_log, f32)
    dskip = np.asarray(L0_attn_D_skip, f32)
    wout = np.asarray(L0_attn_W_out, f32)
    rhs = np.asarray(L0_ffn_rhs, f32)

    wxp_pad = np.concatenate(
        [wxp, np.zeros((DI, 128 - wxp.shape[1]), f32)], axis=1)
    wdt_pad = np.concatenate([wdt, np.zeros((128 - wdt.shape[0], DI), f32)],
                             axis=0)

    in_maps = []
    for c in range(8):
        beta, phi = divmod(c, 2)
        sl = slice(phi * DIH, (phi + 1) * DIH)
        slp = slice((1 - phi) * DIH, (2 - phi) * DIH)
        xb = np.ascontiguousarray(x_np[beta])
        perm = np.concatenate([np.arange(phi * DIH, (phi + 1) * DIH),
                               np.arange((1 - phi) * DIH, (2 - phi) * DIH)])
        m = {
            "x": xb,
            "xown": np.ascontiguousarray(xb[phi * LH:(phi + 1) * LH]),
            "wix": np.ascontiguousarray(
                w_in[:, 0:DI][:, perm]).reshape(KD, 128, DI),
            "wiz": np.ascontiguousarray(
                w_in[:, DI + phi * DIH: DI + (phi + 1) * DIH]
            ).reshape(KD, 128, DIH),
            "convw": np.ascontiguousarray(
                conv_w[:, 0, :][perm].reshape(NCH, 128, 4).transpose(1, 0, 2)
            ).reshape(128, NCH * 4),
            "convb": np.ascontiguousarray(conv_b[perm].reshape(NCH, 128).T),
            "wxp": np.ascontiguousarray(
                wxp_pad[perm].reshape(NCH, 128, 128)),
            "wdt": np.ascontiguousarray(wdt_pad[:, sl]),
            "bdt": np.ascontiguousarray(bdt[sl].reshape(NCHH, 128).T),
            "alog": np.ascontiguousarray(
                alog[sl].reshape(NCHH, 128, NST).transpose(1, 0, 2)
            ).reshape(128, 128),
            "dskip": np.ascontiguousarray(dskip[sl].reshape(NCHH, 128).T),
            "wout": np.ascontiguousarray(
                np.concatenate([wout[sl], wout[slp]], axis=0)
            ).reshape(NCH, 128, D),
            "rhs": np.ascontiguousarray(rhs).reshape(KD, 128, D),
        }
        in_maps.append(m)

    nc = build_nc(L=L)
    res = run_bass_kernel_spmd(nc, in_maps, core_ids=list(range(8)))

    out = np.empty((B, L, D), np.float32)
    for c in range(8):
        beta, phi = divmod(c, 2)
        out[beta, phi * LH:(phi + 1) * LH, :] = res.results[c]["out_half"]
    return out


# version 19
# speedup vs baseline: 1.1645x; 1.0062x over previous
import numpy as np
import concourse.bass as bass
import concourse.mybir as mybir
from concourse.bass_utils import run_bass_kernel_spmd
from contextlib import ExitStack

F32 = mybir.dt.float32
BF16 = mybir.dt.bfloat16
AF = mybir.ActivationFunctionType
ALU = mybir.AluOpType
AX = mybir.AxisListType.X

D = 1024
DI = 2048
DIH = 1024
NST = 16
T = 512
NSUB = T // 128
KD = D // 128
NCH = DI // 128
NCHH = DIH // 128
RC = float(np.float32(1.5 * 2 ** 23))
EPS = 1e-5
I127 = float(np.float32(1.0 / 127.0))
GROUPS = [[0, 1], [2, 3], [4, 5], [6, 7]]


class Sem:
    def __init__(self, ctx, nc, name):
        self.h = ctx.enter_context(nc.semaphore(name))
        self.n = 0

    def add(self, k):
        self.n += k
        return self.n


class Guards:

    def __init__(self):
        self.g = {}

    def wait(self, eng, key):
        if key in self.g:
            s, v = self.g.pop(key)
            eng.wait_ge(s.h, v)

    def set(self, key, sem, val):
        self.g[key] = (sem, val)


def build_nc(L=8192, debug=False):
    nchunk = L // T
    nslot = nchunk // 2
    LH = L // 2

    nc = bass.Bass("TRN2", target_bir_lowering=False)
    dk = "ExternalOutput" if debug else "Internal"

    ap_x = nc.dram_tensor("x", [L, D], F32, kind="ExternalInput").ap()
    ap_xo = nc.dram_tensor("xown", [LH, D], F32, kind="ExternalInput").ap()
    ap_wix = nc.dram_tensor("wix", [KD, 128, DI], F32, kind="ExternalInput").ap()
    ap_wiz = nc.dram_tensor("wiz", [KD, 128, DIH], F32, kind="ExternalInput").ap()
    ap_cw = nc.dram_tensor("convw", [128, NCH * 4], F32, kind="ExternalInput").ap()
    ap_cb = nc.dram_tensor("convb", [128, NCH], F32, kind="ExternalInput").ap()
    ap_wxp = nc.dram_tensor("wxp", [NCH, 128, 128], F32, kind="ExternalInput").ap()
    ap_wdt = nc.dram_tensor("wdt", [128, DIH], F32, kind="ExternalInput").ap()
    ap_bdt = nc.dram_tensor("bdt", [128, NCHH], F32, kind="ExternalInput").ap()
    ap_al = nc.dram_tensor("alog", [128, 128], F32, kind="ExternalInput").ap()
    ap_ds = nc.dram_tensor("dskip", [128, NCHH], F32, kind="ExternalInput").ap()
    ap_wo = nc.dram_tensor("wout", [NCH, 128, D], F32, kind="ExternalInput").ap()
    ap_rhs = nc.dram_tensor("rhs", [KD, 128, D], F32, kind="ExternalInput").ap()

    ap_out = nc.dram_tensor("out_half", [LH, D], F32, kind="ExternalOutput").ap()

    d_delta = nc.dram_tensor("d_delta", [nchunk, 128, NCHH, T], F32, kind=dk).ap()
    d_u = nc.dram_tensor("d_u", [nchunk, 128, NCHH, T], F32, kind=dk).ap()
    d_sz = nc.dram_tensor("d_sz", [nchunk, 128, NCHH, T], F32, kind=dk).ap()
    d_bc = nc.dram_tensor("d_bc", [nchunk, 32, T], F32, kind=dk).ap()
    d_y = nc.dram_tensor("d_y", [nchunk, 128, NCHH * T], BF16, kind=dk).ap()
    d_qr = nc.dram_tensor("d_qr", [128, KD * D], BF16).ap()
    d_rs = nc.dram_tensor("d_rs", [D], F32).ap()
    d_y4p = nc.dram_tensor("d_y4p", [nslot, 128, NCHH * T], BF16).ap()
    d_yag = nc.dram_tensor("d_yag", [2, nslot, 128, NCHH * T], BF16).ap()
    d_yown = nc.dram_tensor("d_yown", [nslot, 128, NCHH * T], BF16).ap()
    d_ypeer = nc.dram_tensor("d_ypeer", [nslot, 128, NCHH * T], BF16).ap()

    ctx = ExitStack()
    with ctx:
        identb = ctx.enter_context(nc.sbuf_tensor("identb", [128, 128], BF16))
        identf = ctx.enter_context(nc.sbuf_tensor("identf", [128, 128], F32))
        a_sb = ctx.enter_context(nc.sbuf_tensor("a_sb", [128, 128], F32))
        cw_sb = ctx.enter_context(nc.sbuf_tensor("cw_sb", [128, NCH * 4], F32))
        cb_sb = ctx.enter_context(nc.sbuf_tensor("cb_sb", [128, NCH], F32))
        bdt_sb = ctx.enter_context(nc.sbuf_tensor("bdt_sb", [128, NCHH], F32))
        ds_sb = ctx.enter_context(nc.sbuf_tensor("ds_sb", [128, NCHH], F32))
        hcar = ctx.enter_context(nc.sbuf_tensor("hcar", [128, 128], F32))
        ones1 = ctx.enter_context(nc.sbuf_tensor("ones1", [1, 128], F32))
        rsb = ctx.enter_context(nc.sbuf_tensor("rsb", [128, D], F32))
        big8 = ctx.enter_context(nc.psum_tensor("big8", [128, 8 * T], F32))
        ps = [big8[:, i * T:(i + 1) * T] for i in range(8)]
        hps = big8[:, :].rearrange("p (a t) -> p a t", t=T)

        LDS = Sem(ctx, nc, "LDS")
        SPS = Sem(ctx, nc, "SPS")
        TPS = Sem(ctx, nc, "TPS")
        VES = Sem(ctx, nc, "VES")
        SCS = Sem(ctx, nc, "SCS")
        GPS = Sem(ctx, nc, "GPS")
        CCS = Sem(ctx, nc, "CCS")
        GPC = Sem(ctx, nc, "GPC")
        BCS = [Sem(ctx, nc, f"BCS{k}") for k in range(4)]
        LD2 = Sem(ctx, nc, "LD2")
        LD3 = Sem(ctx, nc, "LD3")
        STS = Sem(ctx, nc, "STS")

        block = ctx.enter_context(nc.Block())
        G = Guards()

        def on(eng, fn):
            getattr(block, eng)(fn)

        _dyn = {}

        def gdyn(g, key):
            if "phi" not in _dyn:
                _dyn["phi"] = g.partition_id() % 2
            phi = _dyn["phi"]
            if key not in _dyn:
                if key == "uoff":
                    _dyn[key] = phi * NCHH
                elif key == "own":
                    _dyn[key] = phi * nslot
                elif key == "peer":
                    _dyn[key] = (1 - phi) * nslot
                elif key == "peerag":
                    _dyn[key] = 1 - phi
            return _dyn[key]

        def qdma(eng, sem, **kw):
            eng.wait_ge(sem.h, sem.n)
            sem.n += 16
            return eng.dma_start(**kw).then_inc(sem.h, 16)

        RSQRT_MAGIC = 0x5F3759DF

        def emit_ln(xb, xn_out, st, sqscr, ld_wait):
            def _v1(vector):
                def vch(ins):
                    VES.n += 1
                    ins.then_inc(VES.h, 1)
                    vector.wait_ge(VES.h, VES.n)

                vector.wait_ge(LDS.h, ld_wait)
                for s in range(NSUB):
                    vector.tensor_reduce(
                        st["xsum"][:, s:s + 1], xb[:, s, :], AX, ALU.add)
                    vch(vector.tensor_tensor(
                        sqscr[:, :], xb[:, s, :], xb[:, s, :], ALU.mult))
                    vch(vector.tensor_reduce(
                        st["xsq"][:, s:s + 1], sqscr[:, :], AX, ALU.add))
                vch(vector.tensor_scalar(
                    st["mcol"][:, :], st["xsum"][:, :], 1.0 / D, None, ALU.mult))
                vch(vector.tensor_tensor(
                    st["sc1"][:, :], st["mcol"][:, :], st["mcol"][:, :],
                    ALU.mult))
                vch(vector.scalar_tensor_tensor(
                    st["varc"][:, :], st["xsq"][:, :], 1.0 / D,
                    st["sc1"][:, :], ALU.mult, ALU.subtract))
                vch(vector.tensor_scalar(
                    st["veps"][:, :], st["varc"][:, :], EPS, None, ALU.add))
                iv = st["veps"][:, :].bitcast(mybir.dt.int32)
                io = st["rstd"][:, :].bitcast(mybir.dt.int32)
                vch(vector.tensor_scalar(
                    io, iv, 1, None, ALU.arith_shift_right))
                vch(vector.tensor_scalar(
                    io, io, -1, RSQRT_MAGIC, ALU.mult, ALU.add))
                for _ in range(3):
                    vch(vector.tensor_tensor(
                        st["sc1"][:, :], st["rstd"][:, :], st["rstd"][:, :],
                        ALU.mult))
                    vch(vector.tensor_tensor(
                        st["sc2"][:, :], st["veps"][:, :], st["sc1"][:, :],
                        ALU.mult))
                    vch(vector.tensor_scalar(
                        st["sc2"][:, :], st["sc2"][:, :], -0.5, 1.5,
                        ALU.mult, ALU.add))
                    vch(vector.tensor_tensor(
                        st["rstd"][:, :], st["rstd"][:, :], st["sc2"][:, :],
                        ALU.mult))
                ins = None
                for s in range(NSUB):
                    ins = vector.tensor_scalar(
                        xn_out[:, s, :], xb[:, s, :], st["mcol"][:, s:s + 1],
                        st["rstd"][:, s:s + 1], ALU.subtract, ALU.mult)
                ins.then_inc(VES.h, 1)
            on("vector", _v1)
            VES.add(1)
            return VES.n

        s1 = ExitStack()
        wix_sb = s1.enter_context(nc.sbuf_tensor("wix_sb", [128, KD, DI], BF16))
        wiz_sb = s1.enter_context(nc.sbuf_tensor("wiz_sb", [128, KD, DIH], BF16))
        wxp_sb = s1.enter_context(nc.sbuf_tensor("wxp_sb", [128, NCH, 128], BF16))
        wdt_sb = s1.enter_context(nc.sbuf_tensor("wdt_sb", [128, DIH], BF16))
        diag_sb = s1.enter_context(
            nc.sbuf_tensor("diag_sb", [128, NCH * 4, 128], BF16))

        pro = ExitStack()
        stgA = pro.enter_context(nc.sbuf_tensor("stgA", [128, KD, D], F32))
        stgB = pro.enter_context(nc.sbuf_tensor("stgB", [128, KD, D], F32))
        qrT = pro.enter_context(nc.sbuf_tensor("qrT", [128, KD, D], BF16))
        qrS = pro.enter_context(nc.sbuf_tensor("qrS", [128, KD, D], BF16))
        ram8 = pro.enter_context(nc.sbuf_tensor("ram8", [128, KD], F32))
        rs8 = pro.enter_context(nc.sbuf_tensor("rs8", [128, KD], F32))
        rrs8 = pro.enter_context(nc.sbuf_tensor("rrs8", [128, KD], F32))
        qtmp = pro.enter_context(nc.sbuf_tensor("qtmp", [128, D], F32))

        def _sy0(sync):
            qdma(sync, LDS, out=cw_sb[:, :], in_=ap_cw)
            qdma(sync, LDS, out=cb_sb[:, :], in_=ap_cb)
            qdma(sync, LDS, out=bdt_sb[:, :], in_=ap_bdt)
            qdma(sync, LDS, out=ds_sb[:, :], in_=ap_ds)
            qdma(sync, LDS, out=a_sb[:, :], in_=ap_al)
            qdma(sync, LDS, out=stgA[:, :, :],
                 in_=ap_rhs.rearrange("k p e -> p k e"))
        on("sync", _sy0)
        ld_c = LDS.n

        def _gp0(gpsimd):
            for idt in (identb, identf):
                GPC.n += 1
                gpsimd.memset(idt[:, :], 0.0).then_inc(GPC.h, 1)
                gpsimd.wait_ge(GPC.h, GPC.n)
                GPC.n += 1
                gpsimd.affine_select(
                    out=idt[:, :], in_=idt[:, :],
                    compare_op=ALU.not_equal, fill=1.0, base=0,
                    pattern=[[-1, 128]], channel_multiplier=1
                ).then_inc(GPC.h, 1)
                gpsimd.wait_ge(GPC.h, GPC.n)
            GPC.n += 1
            gpsimd.memset(hcar[:, :], 0.0).then_inc(GPC.h, 1)
            GPC.n += 1
            gpsimd.memset(ones1[0:1, :], 1.0).then_inc(GPC.h, 1)
        on("gpsimd", _gp0)
        gp_id = GPC.n

        def _sc0(scalar):
            scalar.wait_ge(LDS.h, ld_c)
            scalar.activation(a_sb[:, :], a_sb[:, :], AF.Exp).then_inc(SCS.h, 1)
        sc_a = SCS.add(1)
        on("scalar", _sc0)

        def _v0(vector):
            vector.wait_ge(SCS.h, sc_a)
            vector.wait_ge(GPC.h, gp_id)
            vector.wait_ge(LDS.h, ld_c)
            vector.tensor_scalar(a_sb[:, :], a_sb[:, :], -1.0, None, ALU.mult)
            ins = None
            for jk in range(NCH * 4):
                ins = vector.tensor_scalar(
                    diag_sb[:, jk, :], identb[:, :], cw_sb[:, jk:jk + 1],
                    None, ALU.mult)
            ins.then_inc(VES.h, 1)
        ve_diag = VES.add(1)
        on("vector", _v0)

        def tr_block(src_ap, dst_fn, wait_fn=None, ident=None, bf=False):
            for idx in range(KD * KD):
                k, ec = divmod(idx, KD)
                bk = idx % 2

                def _pe(tensor, k=k, ec=ec, bk=bk, first=(idx == 0)):
                    if first and wait_fn is not None:
                        wait_fn(tensor)
                    G.wait(tensor, f"ps{bk}")
                    pv = (ps[bk][:, 0:64].bitcast(BF16) if bf
                          else ps[bk][:, 0:128])
                    tensor.transpose(
                        pv, src_ap(k, ec), ident[:, :]).then_inc(TPS.h, 1)
                tp = TPS.add(1)
                on("tensor", _pe)

                def _v(vector, k=k, ec=ec, bk=bk, tp=tp):
                    vector.wait_ge(TPS.h, tp)
                    pv = (ps[bk][:, 0:64].bitcast(BF16) if bf
                          else ps[bk][:, 0:128])
                    vector.tensor_copy(dst_fn(k, ec), pv).then_inc(VES.h, 1)
                ve = VES.add(1)
                on("vector", _v)
                G.set(f"ps{bk}", VES, ve)

        def _w_rhs(tensor):
            tensor.wait_ge(LDS.h, ld_c)
            tensor.wait_ge(VES.h, ve_diag)

        tr_block(lambda k, ec: stgA[:, k, ec * 128:(ec + 1) * 128],
                 lambda k, ec: stgB[:, ec, k * 128:(k + 1) * 128],
                 wait_fn=_w_rhs, ident=identf)
        ve_rT = VES.n

        def _v1(vector):
            def vch(ins):
                VES.n += 1
                ins.then_inc(VES.h, 1)
                vector.wait_ge(VES.h, VES.n)

            vector.wait_ge(VES.h, ve_rT)
            ins = None
            for ec in range(KD):
                ins = vector.tensor_reduce(
                    ram8[:, ec:ec + 1], stgB[:, ec, :], AX, ALU.max,
                    apply_absolute_value=True)
            vch(ins)
            vch(vector.tensor_scalar(
                rs8[:, :], ram8[:, :], 1e-12, I127, ALU.max, ALU.mult))
            vch(vector.reciprocal(rrs8[:, :], rs8[:, :]))
            ins = None
            for ec in range(KD):
                vch(vector.tensor_scalar(
                    qtmp[:, :], stgB[:, ec, :], rrs8[:, ec:ec + 1], None,
                    ALU.mult))
                ins = vector.tensor_scalar(
                    qrT[:, ec, :], qtmp[:, :], RC, -RC, ALU.add, ALU.add)
                if ec < KD - 1:
                    vch(ins)
            ins.then_inc(VES.h, 1)
        on("vector", _v1)
        ve_q = VES.add(1)

        def _sy1(sync):
            sync.wait_ge(VES.h, ve_q)
            for ec in range(KD):
                qdma(sync, SPS, out=d_rs[ec * 128:(ec + 1) * 128],
                     in_=rs8[:, ec:ec + 1])
        on("sync", _sy1)
        sp_rs = SPS.n

        def _sy2(sync):
            sync.wait_ge(SPS.h, sp_rs)
            qdma(sync, LDS, out=rsb[:, :], in_=d_rs.partition_broadcast(128))
        on("sync", _sy2)

        def _w_q(tensor):
            tensor.wait_ge(VES.h, ve_q)

        tr_block(lambda k, ec: qrT[:, ec, k * 128:(k + 1) * 128],
                 lambda k, ec: qrS[:, k, ec * 128:(ec + 1) * 128],
                 wait_fn=_w_q, ident=identb, bf=True)
        ve_qS = VES.n

        def _sy3(sync):
            sync.wait_ge(VES.h, ve_qS)
            qdma(sync, SPS, out=d_qr,
                 in_=qrS[:, :, :].rearrange("p k e -> p (k e)"))
        on("sync", _sy3)

        def ld_cast(src_ap, stg_view, dst_view, stg_guard_key):
            def _s(sync):
                G.wait(sync, stg_guard_key)
                qdma(sync, LDS, out=stg_view, in_=src_ap)
            on("sync", _s)
            ldn = LDS.n

            def _v(vector, ldn=ldn):
                vector.wait_ge(LDS.h, ldn)
                vector.tensor_copy(dst_view, stg_view).then_inc(VES.h, 1)
            ve = VES.add(1)
            on("vector", _v)
            G.set(stg_guard_key, VES, ve)

        sAf = stgA[:, :, :].rearrange("p k e -> p (k e)")
        sBf = stgB[:, :, :].rearrange("p k e -> p (k e)")
        G.set("stgA", VES, ve_rT)
        G.set("stgB", VES, ve_q)
        for h in range(2):
            ld_cast(ap_wix[:, :, h * 1024:(h + 1) * 1024].rearrange(
                        "k p c -> p k c"),
                    stgA[:, :, :],
                    wix_sb[:, :, h * 1024:(h + 1) * 1024], "stgA")
        ld_cast(ap_wiz.rearrange("k p c -> p k c"), stgB[:, :, :],
                wiz_sb[:, :, :], "stgB")
        ld_cast(ap_wxp.rearrange("j p c -> p j c"),
                sAf[:, 0:NCH * 128].rearrange("p (j c) -> p j c", c=128),
                wxp_sb[:, :, :], "stgA")
        ld_cast(ap_wdt, sBf[:, 0:DIH], wdt_sb[:, :], "stgB")
        G.g.pop("stgA", None)
        G.g.pop("stgB", None)

        ve_pro, sp_pro, ld_pro, tp_pro, sc_pro = VES.n, SPS.n, LDS.n, TPS.n, SCS.n

        def _bar(eng):
            eng.wait_ge(VES.h, ve_pro)
            eng.wait_ge(SPS.h, sp_pro)
            eng.wait_ge(LDS.h, ld_pro)
            eng.wait_ge(TPS.h, tp_pro)
            eng.wait_ge(SCS.h, sc_pro)
            eng.wait_ge(GPC.h, gp_id)
        for en in ("sync", "vector", "scalar", "tensor", "gpsimd"):
            on(en, _bar)
        pro.close()

        s1w = ExitStack()
        xsb = [s1w.enter_context(nc.sbuf_tensor(f"xsb{b}", [128, NSUB, D], F32))
               for b in range(2)]
        xnb = s1w.enter_context(nc.sbuf_tensor("xnb", [128, NSUB, D], BF16))
        xnT = s1w.enter_context(nc.sbuf_tensor("xnT", [128, KD, T], BF16))
        xinT = s1w.enter_context(nc.sbuf_tensor("xinT", [128, NCH, T + 3], BF16))
        ubf = s1w.enter_context(nc.sbuf_tensor("ubf", [128, NCH, T], BF16))
        xdT = s1w.enter_context(nc.sbuf_tensor("xdT", [128, T], BF16))
        bcst = [s1w.enter_context(nc.sbuf_tensor(f"bcst{b}", [128, T], F32))
                for b in range(2)]
        et8 = s1w.enter_context(nc.sbuf_tensor("et8", [128, NCHH, T], F32))
        szst = [s1w.enter_context(nc.sbuf_tensor(f"szst{b}", [128, T], F32))
                for b in range(2)]
        ufst = [s1w.enter_context(nc.sbuf_tensor(f"ufst{b}", [128, T], F32))
                for b in range(2)]
        sqscr = s1w.enter_context(nc.sbuf_tensor("sqscr", [128, D], F32))
        st4 = {n: s1w.enter_context(nc.sbuf_tensor("st_" + n, [128, NSUB], F32))
               for n in ("xsum", "xsq", "mcol", "varc", "veps", "rstd",
                         "sc1", "sc2")}

        def _gpz(gpsimd):
            GPC.n += 1
            gpsimd.memset(xinT[:, :, 0:3], 0.0).then_inc(GPC.h, 1)
        on("gpsimd", _gpz)
        gp_h0 = GPC.n

        xfree = {}
        szst_free = [0, 0]
        ufst_free = [0, 0]
        et_free = [0] * NCHH
        xld = {}

        def emit_sx(i2):
            def _sx(gpsimd, i2=i2):
                if i2 >= 2:
                    s, v = xfree[i2 - 2]
                    gpsimd.wait_ge(s.h, v)
                qdma(gpsimd, LDS, out=xsb[i2 % 2][:, :, :],
                     in_=ap_x[i2 * T:(i2 + 1) * T, :].rearrange(
                         "(s p) d -> p s d", p=128))
            on("gpsimd", _sx)
            xld[i2] = LDS.n

        emit_sx(0)
        emit_sx(1)
        for i in range(nchunk):
            xb = xsb[i % 2]
            ld_x = xld[i]

            ve_xn1 = emit_ln(xb, xb, st4, sqscr, ld_x)

            def _vcast(vector, xb=xb, ve_xn1=ve_xn1):
                vector.wait_ge(VES.h, ve_xn1)
                vector.tensor_copy(
                    xnb[:, :, :].rearrange("p s d -> p (s d)"),
                    xb[:, :, :].rearrange("p s d -> p (s d)")
                ).then_inc(VES.h, 1)
            ve_xnb = VES.add(1)
            on("vector", _vcast)
            xfree[i] = (VES, ve_xnb)

            for j in range(KD):
                bk = j % 2

                def _pt(tensor, j=j, bk=bk, ve_xnb=ve_xnb, i=i):
                    if j == 0:
                        tensor.wait_ge(VES.h, ve_xnb)
                        if i == 0:
                            tensor.wait_ge(GPC.h, gp_h0)
                    G.wait(tensor, f"ps{bk}")
                    pv = ps[bk][:, 0:256].bitcast(BF16)
                    ins = None
                    for s in range(NSUB):
                        ins = tensor.transpose(
                            pv[:, s * 128:(s + 1) * 128],
                            xnb[:, s, j * 128:(j + 1) * 128], identb[:, :])
                    ins.then_inc(TPS.h, 1)
                tp = TPS.add(1)
                on("tensor", _pt)

                def _vt(vector, j=j, bk=bk, tp=tp):
                    vector.wait_ge(TPS.h, tp)
                    vector.tensor_copy(
                        xnT[:, j, :], ps[bk][:, 0:256].bitcast(BF16)
                    ).then_inc(VES.h, 1)
                ve = VES.add(1)
                on("vector", _vt)
                G.set(f"ps{bk}", VES, ve)
            ve_xnT = VES.n

            sc_zu = SCS.n
            sps_zu = SPS.n
            ve_xin_evac = {}
            tp_w = {}
            for m in range(NCH + NCHH):
                bk = 4 + m % 4

                def _pw(tensor, m=m, bk=bk, ve_xnT=ve_xnT):
                    if m == 0:
                        tensor.wait_ge(VES.h, ve_xnT)
                    G.wait(tensor, f"ps{bk}")
                    ins = None
                    for k in range(KD):
                        if m < NCH:
                            stat = wix_sb[:, k, m * 128:(m + 1) * 128]
                        else:
                            stat = wiz_sb[:, k, (m - NCH) * 128:(m - NCH + 1) * 128]
                        ins = tensor.matmul(
                            ps[bk][:, :], stat, xnT[:, k, :],
                            start=(k == 0), stop=(k == KD - 1))
                    ins.then_inc(TPS.h, 1)
                tp_w[m] = TPS.add(1)
                on("tensor", _pw)

                if m < NCH:
                    def _vw(vector, m=m, bk=bk, tp=tp_w[m]):
                        vector.wait_ge(TPS.h, tp)
                        vector.tensor_copy(
                            xinT[:, m, 3:T + 3], ps[bk][:, :]).then_inc(VES.h, 1)
                    ve = VES.add(1)
                    on("vector", _vw)
                    G.set(f"ps{bk}", VES, ve)
                    ve_xin_evac[m] = ve
                else:
                    G.set(f"ps{bk}", SCS, sc_zu + (m - NCH) + 1)

            tp_c = {}
            for j in range(NCH):
                bk = j % 2

                def _pc(tensor, j=j, bk=bk, ve_evj=ve_xin_evac[j]):
                    tensor.wait_ge(VES.h, ve_evj)
                    G.wait(tensor, f"ps{bk}")
                    ins = None
                    for k in range(4):
                        ins = tensor.matmul(
                            ps[bk][:, :], diag_sb[:, j * 4 + k, :],
                            xinT[:, j, k:k + T], start=(k == 0), stop=(k == 3))
                    ins.then_inc(TPS.h, 1)
                tp_c[j] = TPS.add(1)
                on("tensor", _pc)
                G.set(f"ps{bk}", SCS, sc_zu + NCHH + j + 1)

            def _szu(scalar, i=i, tp_w=dict(tp_w), tp_c=dict(tp_c),
                     sc_zu=sc_zu, sps_zu=sps_zu, szf=tuple(szst_free),
                     uff=tuple(ufst_free)):
                for zc in range(NCHH):
                    scalar.wait_ge(TPS.h, tp_w[NCH + zc])
                    if zc >= 2:
                        scalar.wait_ge(SPS.h, sps_zu + (zc - 1) * 16)
                    elif szf[zc % 2]:
                        scalar.wait_ge(SPS.h, szf[zc % 2])
                    SCS.n += 1
                    scalar.activation(
                        szst[zc % 2][:, :], ps[4 + zc % 4][:, :], AF.Silu
                    ).then_inc(SCS.h, 1)
                for j in range(NCH):
                    scalar.wait_ge(TPS.h, tp_c[j])
                    if j < NCHH:
                        if j >= 2:
                            scalar.wait_ge(SPS.h, sps_zu + 128 + (j - 1) * 16)
                        elif uff[j % 2]:
                            scalar.wait_ge(SPS.h, uff[j % 2])
                        scalar.activation(
                            ubf[:, j, :], ps[j % 2][:, :], AF.Silu,
                            bias=cb_sb[:, j:j + 1])
                        SCS.n += 1
                        scalar.activation(
                            ufst[j % 2][:, :], ps[j % 2][:, :], AF.Silu,
                            bias=cb_sb[:, j:j + 1]).then_inc(SCS.h, 1)
                    else:
                        SCS.n += 1
                        scalar.activation(
                            ubf[:, j, :], ps[j % 2][:, :], AF.Silu,
                            bias=cb_sb[:, j:j + 1]).then_inc(SCS.h, 1)
            on("scalar", _szu)
            assert SCS.n == sc_zu + NCHH + NCH

            for zc in range(NCHH):
                def _syz(sync, i=i, zc=zc, sc=sc_zu + zc + 1):
                    sync.wait_ge(SCS.h, sc)
                    qdma(sync, SPS, out=d_sz[i, :, zc, :],
                         in_=szst[zc % 2][:, :])
                on("sync", _syz)
            for j in range(NCHH):
                def _syu(sync, i=i, j=j, sc=sc_zu + NCHH + j + 1):
                    sync.wait_ge(SCS.h, sc)
                    qdma(sync, SPS, out=d_u[i, :, j, :], in_=ufst[j % 2][:, :])
                on("sync", _syu)
            assert SPS.n == sps_zu + 256
            szst_free[0] = sps_zu + 7 * 16
            szst_free[1] = sps_zu + 8 * 16
            ufst_free[0] = sps_zu + 128 + 7 * 16
            ufst_free[1] = sps_zu + 128 + 8 * 16
            sc_u_all = SCS.n

            def _pxd(tensor, sc_u_all=sc_u_all):
                tensor.wait_ge(SCS.h, sc_u_all)
                G.wait(tensor, "ps6")
                ins = None
                for k in range(NCH):
                    ins = tensor.matmul(
                        ps[6][:, :], wxp_sb[:, k, :], ubf[:, k, :],
                        start=(k == 0), stop=(k == NCH - 1))
                ins.then_inc(TPS.h, 1)
            tp_xd = TPS.add(1)
            on("tensor", _pxd)

            bcb = i % 2

            def _vxd(vector, bcb=bcb, tp_xd=tp_xd):
                vector.wait_ge(TPS.h, tp_xd)
                G.wait(vector, f"bcst{bcb}")
                vector.tensor_copy(xdT[:, :], ps[6][:, :])
                vector.tensor_copy(
                    bcst[bcb][64:96, :], ps[6][64:96, :]).then_inc(VES.h, 1)
            ve_xd = VES.add(1)
            on("vector", _vxd)
            G.set("ps6", VES, ve_xd)

            def _sybc(sync, i=i, bcb=bcb, ve_xd=ve_xd):
                sync.wait_ge(VES.h, ve_xd)
                qdma(sync, SPS, out=d_bc[i, :, :], in_=bcst[bcb][64:96, :])
            on("sync", _sybc)
            sp = SPS.n
            G.set(f"bcst{bcb}", SPS, sp)

            scd = SCS.n
            tp_d = {}
            for dh in range(NCHH):
                bk = 2 + dh % 2

                def _pd(tensor, dh=dh, bk=bk, ve_xd=ve_xd):
                    if dh == 0:
                        tensor.wait_ge(VES.h, ve_xd)
                    G.wait(tensor, f"ps{bk}")
                    tensor.matmul(
                        ps[bk][:, :], wdt_sb[:, dh * 128:(dh + 1) * 128],
                        xdT[:, :], start=True, stop=True).then_inc(TPS.h, 1)
                tp_d[dh] = TPS.add(1)
                on("tensor", _pd)
                G.set(f"ps{bk}", SCS, scd + dh + 1)

            def _sdel(scalar, i=i, tp_d=dict(tp_d), scd=scd,
                      etf=tuple(et_free)):
                for dh in range(NCHH):
                    scalar.wait_ge(TPS.h, tp_d[dh])
                    if etf[dh]:
                        scalar.wait_ge(SPS.h, etf[dh])
                    SCS.n += 1
                    scalar.activation(
                        et8[:, dh, :], ps[2 + dh % 2][:, :], AF.Exp,
                        bias=bdt_sb[:, dh:dh + 1]).then_inc(SCS.h, 1)
                scalar.wait_ge(SCS.h, scd + NCHH)
                for dh in range(NCHH):
                    SCS.n += 1
                    scalar.activation(
                        et8[:, dh, :], et8[:, dh, :], AF.Ln, bias=1.0
                    ).then_inc(SCS.h, 1)
            on("scalar", _sdel)
            assert SCS.n == scd + 2 * NCHH

            for dh in range(NCHH):
                def _syd(sync, i=i, dh=dh, lnv=scd + NCHH + dh + 1):
                    sync.wait_ge(SCS.h, lnv)
                    qdma(sync, SPS, out=d_delta[i, :, dh, :],
                         in_=et8[:, dh, :])
                on("sync", _syd)
                et_free[dh] = SPS.n

            tp_now = TPS.n

            ve_xin_last = ve_xin_evac[NCH - 1]

            def _vh(vector, tp_now=tp_now, ve_xin_last=ve_xin_last):
                vector.wait_ge(TPS.h, tp_now)
                vector.wait_ge(VES.h, ve_xin_last)
                vector.tensor_copy(
                    xinT[:, :, 0:3], xinT[:, :, T:T + 3]).then_inc(VES.h, 1)
            VES.add(1)
            on("vector", _vh)

            if i + 2 < nchunk:
                emit_sx(i + 2)

        m1 = (VES.n, SPS.n, LDS.n, TPS.n, SCS.n, GPS.n)

        def _bar1(eng, m1=m1):
            eng.wait_ge(VES.h, m1[0])
            eng.wait_ge(SPS.h, m1[1])
            eng.wait_ge(LDS.h, m1[2])
            eng.wait_ge(TPS.h, m1[3])
            eng.wait_ge(SCS.h, m1[4])
            eng.wait_ge(GPS.h, m1[5])
        for en in ("sync", "vector", "scalar", "tensor", "gpsimd"):
            on(en, _bar1)
        s1w.close()
        s1.close()

        s2 = ExitStack()
        delta_a = s2.enter_context(nc.sbuf_tensor("delta_a", [128, NCHH * T], F32))
        u_ab = [s2.enter_context(nc.sbuf_tensor(f"u_a{b}", [128, NCHH * T], F32))
                for b in range(2)]
        sz_a = s2.enter_context(nc.sbuf_tensor("sz_a", [128, NCHH * T], F32))
        dtu_a = s2.enter_context(nc.sbuf_tensor("dtu_a", [128, NCHH * T], F32))
        yacc = [s2.enter_context(nc.sbuf_tensor(f"yacc{b}", [128, NCHH * T], F32))
                for b in range(2)]
        yp_blk = [s2.enter_context(nc.sbuf_tensor(f"yp_blk{b}", [128, NCHH * T], F32))
                  for b in range(2)]
        dbu_b = s2.enter_context(nc.sbuf_tensor("dbu_b", [128, NCHH * T], F32))
        dAb = [s2.enter_context(nc.sbuf_tensor(f"dAb{b}", [128, T], F32))
               for b in range(4)]
        Bb = [s2.enter_context(nc.sbuf_tensor(f"Bb{b}", [128, T], F32))
              for b in range(2)]
        Cb = [s2.enter_context(nc.sbuf_tensor(f"Cb{b}", [128, T], F32))
              for b in range(2)]
        ybf = s2.enter_context(nc.sbuf_tensor("ybf", [128, NCHH * T], BF16))

        prev_vn_base = None
        prev_tts_last = None
        prev_tts_last2 = None

        def bc_load(i, n):
            nb = n % 2

            def _s(sync, i=i, n=n, nb=nb):
                G.wait(sync, f"Bb{nb}")
                qdma(sync, BCS[nb], out=Bb[nb][:, :],
                     in_=d_bc[i, n].unsqueeze(0).partition_broadcast(128))
                G.wait(sync, f"Cb{nb}")
                qdma(sync, BCS[2 + nb], out=Cb[nb][:, :],
                     in_=d_bc[i, 16 + n].unsqueeze(0).partition_broadcast(128))
            on("sync", _s)
            return (BCS[nb].n, BCS[2 + nb].n)

        for i in range(nchunk):
            def _sl(sync, i=i):
                G.wait(sync, "delta_a")
                qdma(sync, LDS, out=delta_a[:, :],
                     in_=d_delta[i].rearrange("p a t -> p (a t)"))
                G.wait(sync, "sz_a")
                qdma(sync, LD2, out=sz_a[:, :],
                     in_=d_sz[i].rearrange("p a t -> p (a t)"))
                G.wait(sync, f"u_a{i % 2}")
                qdma(sync, LD3, out=u_ab[i % 2][:, :],
                     in_=d_u[i].rearrange("p a t -> p (a t)"))
            on("sync", _sl)
            ld_ds = LDS.n
            gp_u = LD3.n
            ld_sz = LD2.n

            def _vdtu(vector, ld_ds=ld_ds, gp_u=gp_u):
                vector.wait_ge(LDS.h, ld_ds)
                vector.wait_ge(LD3.h, gp_u)
                G.wait(vector, "dtu_a")
                vector.tensor_tensor(
                    dtu_a[:, :], delta_a[:, :], u_ab[i % 2][:, :], ALU.mult
                ).then_inc(VES.h, 1)
            ve_dtu = VES.add(1)
            on("vector", _vdtu)
            ve_dtu_g = [ve_dtu]

            if i == 0:
                bc_ready = [bc_load(0, 0), bc_load(0, 1)]

            gpc_carry_prev = GPC.n
            for n in range(NST):
                nb = n % 2
                vn_base = VES.n
                sc_bc = bc_ready[n]

                sc_base = SCS.n

                def _se(scalar, i=i, n=n, ld_ds=ld_ds, vn_base=vn_base,
                        prev_vn=prev_vn_base):
                    for dh in range(NCHH):
                        if dh == 0 and n == 0:
                            scalar.wait_ge(LDS.h, ld_ds)
                        if dh >= 4:
                            fa = vn_base + (dh - 4) + 2
                        elif prev_vn is not None:
                            fa = prev_vn + (dh + 4) + 2
                        else:
                            fa = 0
                        if fa:
                            scalar.wait_ge(VES.h, fa)
                        SCS.n += 1
                        scalar.activation(
                            dAb[dh % 4][:, :], delta_a[:, dh * T:(dh + 1) * T],
                            AF.Exp, scale=a_sb[:, dh * 16 + n:dh * 16 + n + 1]
                        ).then_inc(SCS.h, 1)
                on("scalar", _se)
                sc_base2 = sc_base

                def _vn(vector, i=i, n=n, nb=nb, vn_base=vn_base,
                        sc_bc=sc_bc, sc_base=sc_base2, ve_dtu=ve_dtu,
                        gpcw=gpc_carry_prev):
                    if n == 0:
                        vector.wait_ge(VES.h, ve_dtu)
                    vector.wait_ge(VES.h, vn_base)
                    vector.wait_ge(BCS[nb].h, sc_bc[0])
                    vector.wait_ge(BCS[2 + nb].h, sc_bc[1])
                    VES.n += 1
                    vector.tensor_tensor(
                        dbu_b[:, :].rearrange("p (a t) -> p a t", t=T),
                        dtu_a[:, :].rearrange("p (a t) -> p a t", t=T),
                        Bb[nb][:, :].unsqueeze(1).broadcast_to([128, NCHH, T]),
                        ALU.mult).then_inc(VES.h, 1)
                    vector.wait_ge(VES.h, VES.n)
                    for dh in range(NCHH):
                        vector.wait_ge(SCS.h, sc_base + dh + 1)
                        VES.n += 1
                        vector.tensor_tensor_scan(
                            hps[:, dh, :], dAb[dh % 4][:, :],
                            dbu_b[:, dh * T:(dh + 1) * T],
                            hcar[:, n * 8 + dh:n * 8 + dh + 1],
                            ALU.mult, ALU.add).then_inc(VES.h, 1)
                    vector.wait_ge(VES.h, VES.n)
                    vector.wait_ge(GPC.h, gpcw)
                    VES.n += 1
                    if n == 0:
                        vector.tensor_tensor(
                            yacc[0][:, :].rearrange("p (a t) -> p a t", t=T),
                            hps[:, :, :],
                            Cb[nb][:, :].unsqueeze(1).broadcast_to(
                                [128, NCHH, T]), ALU.mult).then_inc(VES.h, 1)
                    else:
                        vector.tensor_tensor(
                            yp_blk[nb][:, :].rearrange("p (a t) -> p a t", t=T),
                            hps[:, :, :],
                            Cb[nb][:, :].unsqueeze(1).broadcast_to(
                                [128, NCHH, T]), ALU.mult).then_inc(VES.h, 1)
                on("vector", _vn)
                tts_last = vn_base + 1 + NCHH
                ve_y = VES.n
                G.set(f"Bb{nb}", VES, vn_base + 1)
                G.set(f"Cb{nb}", VES, ve_y)

                def _sc(scalar, n=n, vn_base=vn_base):
                    for dh in range(NCHH):
                        scalar.wait_ge(VES.h, vn_base + 2 + dh)
                        SCS.n += 1
                        scalar.copy(
                            hcar[:, n * 8 + dh:n * 8 + dh + 1],
                            big8[:, dh * T + T - 1:dh * T + T]
                        ).then_inc(SCS.h, 1)
                on("scalar", _sc)

                gpc_carry_prev = GPC.n
                if n >= 2:
                    def _gp(gpsimd, pn=n - 1, gate=vn_base + 1 + NCHH):
                        gpsimd.wait_ge(VES.h, gate)
                        gpsimd.wait_ge(GPC.h, GPC.n)
                        GPC.n += 1
                        gpsimd.tensor_tensor(
                            yacc[pn % 2][:, :], yacc[1 - pn % 2][:, :],
                            yp_blk[pn % 2][:, :], ALU.add).then_inc(GPC.h, 1)
                    on("gpsimd", _gp)
                if n == NST - 1:
                    def _gpl(gpsimd, n=n, ve_y=ve_y):
                        gpsimd.wait_ge(VES.h, ve_y)
                        gpsimd.wait_ge(GPC.h, GPC.n)
                        GPC.n += 1
                        gpsimd.tensor_tensor(
                            yacc[n % 2][:, :], yacc[1 - n % 2][:, :],
                            yp_blk[n % 2][:, :], ALU.add).then_inc(GPC.h, 1)
                    on("gpsimd", _gpl)
                gpc_add_last = GPC.n

                if n + 2 < NST:
                    bc_ready.append(bc_load(i, n + 2))
                elif i + 1 < nchunk:
                    bc_ready.append(bc_load(i + 1, n + 2 - NST))
                else:
                    bc_ready.append(None)

                prev_vn_base = vn_base
            bc_ready = bc_ready[NST:]

            def _vyf(vector, i=i, ld_sz=ld_sz, gpl=gpc_add_last):
                vector.wait_ge(VES.h, VES.n)
                vector.wait_ge(GPC.h, gpl)
                vector.wait_ge(LD2.h, ld_sz)
                ins = None
                for dh in range(NCHH):
                    ins = vector.scalar_tensor_tensor(
                        yp_blk[0][:, dh * T:(dh + 1) * T],
                        u_ab[i % 2][:, dh * T:(dh + 1) * T],
                        ds_sb[:, dh:dh + 1],
                        yacc[(NST - 1) % 2][:, dh * T:(dh + 1) * T],
                        ALU.mult, ALU.add)
                VES.n += 1
                ins.then_inc(VES.h, 1)
                vector.wait_ge(VES.h, VES.n)
                G.wait(vector, "ybf")
                vector.tensor_tensor(
                    ybf[:, :], yp_blk[0][:, :], sz_a[:, :],
                    ALU.mult).then_inc(VES.h, 1)
            on("vector", _vyf)
            ve_yf = VES.add(1)
            G.set("delta_a", SCS, SCS.n)
            G.set(f"u_a{i % 2}", VES, ve_yf)
            G.set("sz_a", VES, ve_yf)
            G.set("dtu_a", VES, ve_yf)

            def _syy(sync, i=i, ve_yf=ve_yf):
                sync.wait_ge(VES.h, ve_yf)
                qdma(sync, SPS, out=d_y[i], in_=ybf[:, :])
            on("sync", _syy)
            sp_y = SPS.n
            G.set("ybf", SPS, sp_y)

        sp_all_y = SPS.n

        def _gx(gpsimd, sp_all_y=sp_all_y):
            gpsimd.wait_ge(SPS.h, sp_all_y)
            pb = gdyn(gpsimd, "peer")
            own = gdyn(gpsimd, "own")
            pag = gdyn(gpsimd, "peerag")
            qdma(gpsimd, GPS, out=d_y4p,
                 in_=d_y[bass.ds(pb, nslot)])
            qdma(gpsimd, GPS, out=d_yown,
                 in_=d_y[bass.ds(own, nslot)])
            gpsimd.wait_ge(GPS.h, GPS.n)
            CCS.n += 1
            gpsimd.collective_compute(
                "AllGather", ALU.bypass, replica_groups=GROUPS,
                ins=[d_y4p], outs=[d_yag]).then_inc(CCS.h, 1)
            gpsimd.wait_ge(CCS.h, CCS.n)
            qdma(gpsimd, GPS, out=d_ypeer,
                 in_=d_yag[bass.ds(pag, 1)].squeeze(0))
        gps_pre_xch = GPS.n
        on("gpsimd", _gx)
        gp_xch = GPS.n

        m2 = (VES.n, SPS.n, LDS.n, SCS.n, gps_pre_xch)

        def _bar2(eng, m2=m2):
            eng.wait_ge(VES.h, m2[0])
            eng.wait_ge(SPS.h, m2[1])
            eng.wait_ge(LDS.h, m2[2])
            eng.wait_ge(SCS.h, m2[3])
            eng.wait_ge(GPS.h, m2[4])
        for en in ("sync", "vector", "scalar", "tensor", "gpsimd"):
            on(en, _bar2)
        s2.close()

        s3 = ExitStack()
        wout_sb = s3.enter_context(nc.sbuf_tensor("wout_sb", [128, NCH, D], BF16))
        qr_sb = s3.enter_context(nc.sbuf_tensor("qr_sb", [128, KD, D], BF16))
        xs3 = s3.enter_context(nc.sbuf_tensor("xs3", [128, NSUB, D], F32))
        xn3 = s3.enter_context(nc.sbuf_tensor("xn3", [128, NSUB, D], F32))
        q3 = s3.enter_context(nc.sbuf_tensor("q3", [128, NSUB, D], BF16))
        qT3 = s3.enter_context(nc.sbuf_tensor("qT3", [128, KD, T], BF16))
        ffn3 = s3.enter_context(nc.sbuf_tensor("ffn3", [128, NSUB, D], F32))
        comb3 = s3.enter_context(nc.sbuf_tensor("comb3", [128, NSUB, D], F32))
        ybf3 = s3.enter_context(nc.sbuf_tensor("ybf3", [128, NCH, T], BF16))
        outs3 = [s3.enter_context(nc.sbuf_tensor(f"out{b}", [128, D], F32))
                 for b in range(2)]
        qt3 = s3.enter_context(nc.sbuf_tensor("qt3", [128, D], F32))
        am3 = s3.enter_context(nc.sbuf_tensor("am3", [128, NSUB], F32))
        ls3 = s3.enter_context(nc.sbuf_tensor("ls3", [128, NSUB], F32))
        rls3 = s3.enter_context(nc.sbuf_tensor("rls3", [128, NSUB], F32))
        sq3 = s3.enter_context(nc.sbuf_tensor("sq3", [128, D], F32))
        st3 = {n: s3.enter_context(nc.sbuf_tensor("s3_" + n, [128, NSUB], F32))
               for n in ("xsum", "xsq", "mcol", "varc", "veps", "rstd",
                         "sc1", "sc2")}

        w3 = ExitStack()
        wstg = w3.enter_context(nc.sbuf_tensor("wstg", [128, KD * D], F32))
        for hh in range(2):
            def _sw(sync, hh=hh):
                if hh == 1:
                    G.wait(sync, "wstg")
                qdma(sync, LDS,
                     out=wstg[:, :].rearrange("p (j d) -> p j d", d=D),
                     in_=ap_wo[hh * KD:(hh + 1) * KD].rearrange(
                         "j p d -> p j d"))
            on("sync", _sw)
            ldw = LDS.n

            def _vw3(vector, hh=hh, ldw=ldw):
                vector.wait_ge(LDS.h, ldw)
                vector.tensor_copy(
                    wout_sb[:, hh * KD:(hh + 1) * KD, :],
                    wstg[:, :].rearrange("p (j d) -> p j d", d=D)
                ).then_inc(VES.h, 1)
            ve = VES.add(1)
            on("vector", _vw3)
            G.set("wstg", VES, ve)

        def _sq3(sync):
            qdma(sync, LDS,
                 out=qr_sb[:, :, :].rearrange("p k e -> p (k e)"), in_=d_qr)
        on("sync", _sq3)
        ld_qr = LDS.n

        ve_w3 = VES.n

        def _bar3(eng, ve_w3=ve_w3):
            eng.wait_ge(VES.h, ve_w3)
        for en in ("sync", "vector", "scalar", "tensor", "gpsimd"):
            on(en, _bar3)
        w3.close()

        for j in range(nslot):
            def _sx3(sync, j=j):
                G.wait(sync, "xs3")
                qdma(sync, LDS, out=xs3[:, :, :],
                     in_=ap_xo[j * T:(j + 1) * T, :].rearrange(
                         "(s p) d -> p s d", p=128))
            on("sync", _sx3)
            ld_x3 = LDS.n

            def _gy(sync, j=j):
                G.wait(sync, "ybf3")
                sync.wait_ge(GPS.h, gp_xch)
                qdma(sync, LDS,
                     out=ybf3[:, 0:NCHH, :].rearrange("p a t -> p (a t)"),
                     in_=d_yown[j])
                qdma(sync, LDS,
                     out=ybf3[:, NCHH:NCH, :].rearrange("p a t -> p (a t)"),
                     in_=d_ypeer[j])
            on("sync", _gy)
            gp_y = LDS.n

            ve_xn3 = emit_ln(xs3, xn3, st3, sq3, ld_x3)

            def _vq(vector, j=j, ve_xn3=ve_xn3):
                def vch(ins):
                    VES.n += 1
                    ins.then_inc(VES.h, 1)
                    vector.wait_ge(VES.h, VES.n)

                vector.wait_ge(VES.h, ve_xn3)
                ins = None
                for s in range(NSUB):
                    ins = vector.tensor_reduce(
                        am3[:, s:s + 1], xn3[:, s, :], AX, ALU.max,
                        apply_absolute_value=True)
                vch(ins)
                vch(vector.tensor_scalar(
                    ls3[:, :], am3[:, :], 1e-12, I127, ALU.max, ALU.mult))
                vch(vector.reciprocal(rls3[:, :], ls3[:, :]))
                ins = None
                for s in range(NSUB):
                    vch(vector.tensor_scalar(
                        qt3[:, :], xn3[:, s, :], rls3[:, s:s + 1], None,
                        ALU.mult))
                    ins = vector.tensor_scalar(
                        q3[:, s, :], qt3[:, :], RC, -RC, ALU.add, ALU.add)
                    if s < NSUB - 1:
                        vch(ins)
                ins.then_inc(VES.h, 1)
            on("vector", _vq)
            ve_q3 = VES.add(1)

            for k in range(KD):
                bk = k % 2

                def _pt3(tensor, k=k, bk=bk, ve_q3=ve_q3):
                    if k == 0:
                        tensor.wait_ge(VES.h, ve_q3)
                    G.wait(tensor, f"ps{bk}")
                    pv = ps[bk][:, 0:256].bitcast(BF16)
                    ins = None
                    for s in range(NSUB):
                        ins = tensor.transpose(
                            pv[:, s * 128:(s + 1) * 128],
                            q3[:, s, k * 128:(k + 1) * 128], identb[:, :])
                    ins.then_inc(TPS.h, 1)
                tp = TPS.add(1)
                on("tensor", _pt3)

                def _vt3(vector, k=k, bk=bk, tp=tp):
                    vector.wait_ge(TPS.h, tp)
                    vector.tensor_copy(
                        qT3[:, k, :], ps[bk][:, 0:256].bitcast(BF16)
                    ).then_inc(VES.h, 1)
                ve = VES.add(1)
                on("vector", _vt3)
                G.set(f"ps{bk}", VES, ve)
            ve_qT = VES.n

            for s in range(NSUB):
                def _pf(tensor, j=j, s=s, ve_qT=ve_qT, ld_qr=ld_qr):
                    if j == 0 and s == 0:
                        tensor.wait_ge(LDS.h, ld_qr)
                    if s == 0:
                        tensor.wait_ge(VES.h, ve_qT)
                    G.wait(tensor, "ps2")
                    G.wait(tensor, "ps3")
                    ins = None
                    for k in range(KD):
                        stat = qT3[:, k, s * 128:(s + 1) * 128]
                        tensor.matmul(
                            ps[2][:, :], stat, qr_sb[:, k, 0:T],
                            start=(k == 0), stop=(k == KD - 1))
                        ins = tensor.matmul(
                            ps[3][:, :], stat, qr_sb[:, k, T:D],
                            start=(k == 0), stop=(k == KD - 1))
                    ins.then_inc(TPS.h, 1)
                tp_f = TPS.add(1)
                on("tensor", _pf)

                def _sdq(scalar, s=s, tp_f=tp_f):
                    scalar.wait_ge(TPS.h, tp_f)
                    scalar.activation(
                        ffn3[:, s, 0:T], ps[2][:, :], AF.Copy,
                        scale=ls3[:, s:s + 1])
                    scalar.activation(
                        ffn3[:, s, T:D], ps[3][:, :], AF.Copy,
                        scale=ls3[:, s:s + 1]).then_inc(SCS.h, 1)
                sc_dq = SCS.add(1)
                on("scalar", _sdq)
                G.set("ps2", SCS, sc_dq)
                G.set("ps3", SCS, sc_dq)

                def _vdq(vector, s=s, sc_dq=sc_dq):
                    vector.wait_ge(SCS.h, sc_dq)
                    VES.n += 1
                    vector.tensor_tensor(
                        ffn3[:, s, :], ffn3[:, s, :], rsb[:, :], ALU.mult
                    ).then_inc(VES.h, 1)
                    vector.wait_ge(VES.h, VES.n)
                    vector.tensor_tensor(
                        comb3[:, s, :], xs3[:, s, :], ffn3[:, s, :], ALU.add
                    ).then_inc(VES.h, 1)
                on("vector", _vdq)
                ve_cmb = VES.add(1)

                def _po(tensor, j=j, s=s, gp_y=gp_y):
                    if s == 0:
                        tensor.wait_ge(LDS.h, gp_y)
                    G.wait(tensor, "ps4")
                    G.wait(tensor, "ps5")
                    ins = None
                    for jj in range(NCH):
                        stat = ybf3[:, jj, s * 128:(s + 1) * 128]
                        tensor.matmul(
                            ps[4][:, :], stat, wout_sb[:, jj, 0:T],
                            start=(jj == 0), stop=(jj == NCH - 1))
                        ins = tensor.matmul(
                            ps[5][:, :], stat, wout_sb[:, jj, T:D],
                            start=(jj == 0), stop=(jj == NCH - 1))
                    ins.then_inc(TPS.h, 1)
                tp_o = TPS.add(1)
                on("tensor", _po)

                ob = s % 2

                def _vo(vector, s=s, ob=ob, tp_o=tp_o, ve_cmb=ve_cmb):
                    vector.wait_ge(TPS.h, tp_o)
                    vector.wait_ge(VES.h, ve_cmb)
                    G.wait(vector, f"out{ob}")
                    vector.tensor_tensor(
                        outs3[ob][:, 0:T], comb3[:, s, 0:T], ps[4][:, :],
                        ALU.add)
                    vector.tensor_tensor(
                        outs3[ob][:, T:D], comb3[:, s, T:D], ps[5][:, :],
                        ALU.add).then_inc(VES.h, 1)
                ve_o = VES.add(1)
                on("vector", _vo)
                G.set("ps4", VES, ve_o)
                G.set("ps5", VES, ve_o)

                def _syo(sync, j=j, s=s, ob=ob, ve_o=ve_o):
                    sync.wait_ge(VES.h, ve_o)
                    qdma(sync, STS,
                         out=ap_out[j * T + s * 128:j * T + (s + 1) * 128, :],
                         in_=outs3[ob][:, :])
                on("sync", _syo)
                stn = STS.n
                G.set(f"out{ob}", STS, stn)

            G.set("xs3", VES, VES.n)
            G.set("ybf3", TPS, TPS.n)

        def _fin(sync):
            sync.wait_ge(STS.h, STS.n)
        on("sync", _fin)

        s3.close()
    return nc


def kernel(L0_attn_x, L0_attn_W_in, L0_attn_conv_w, L0_attn_conv_b,
           L0_attn_W_xproj, L0_attn_W_dt, L0_attn_b_dt, L0_attn_A_log,
           L0_attn_D_skip, L0_attn_W_out, L0_ffn_rhs):
    B, L, _ = L0_attn_x.shape
    LH = L // 2
    f32 = np.float32
    x_np = np.asarray(L0_attn_x, f32)
    w_in = np.asarray(L0_attn_W_in, f32)
    conv_w = np.asarray(L0_attn_conv_w, f32)
    conv_b = np.asarray(L0_attn_conv_b, f32)
    wxp = np.asarray(L0_attn_W_xproj, f32)
    wdt = np.asarray(L0_attn_W_dt, f32)
    bdt = np.asarray(L0_attn_b_dt, f32)
    alog = np.asarray(L0_attn_A_log, f32)
    dskip = np.asarray(L0_attn_D_skip, f32)
    wout = np.asarray(L0_attn_W_out, f32)
    rhs = np.asarray(L0_ffn_rhs, f32)

    wxp_pad = np.concatenate(
        [wxp, np.zeros((DI, 128 - wxp.shape[1]), f32)], axis=1)
    wdt_pad = np.concatenate([wdt, np.zeros((128 - wdt.shape[0], DI), f32)],
                             axis=0)

    in_maps = []
    for c in range(8):
        beta, phi = divmod(c, 2)
        sl = slice(phi * DIH, (phi + 1) * DIH)
        slp = slice((1 - phi) * DIH, (2 - phi) * DIH)
        xb = np.ascontiguousarray(x_np[beta])
        perm = np.concatenate([np.arange(phi * DIH, (phi + 1) * DIH),
                               np.arange((1 - phi) * DIH, (2 - phi) * DIH)])
        m = {
            "x": xb,
            "xown": np.ascontiguousarray(xb[phi * LH:(phi + 1) * LH]),
            "wix": np.ascontiguousarray(
                w_in[:, 0:DI][:, perm]).reshape(KD, 128, DI),
            "wiz": np.ascontiguousarray(
                w_in[:, DI + phi * DIH: DI + (phi + 1) * DIH]
            ).reshape(KD, 128, DIH),
            "convw": np.ascontiguousarray(
                conv_w[:, 0, :][perm].reshape(NCH, 128, 4).transpose(1, 0, 2)
            ).reshape(128, NCH * 4),
            "convb": np.ascontiguousarray(conv_b[perm].reshape(NCH, 128).T),
            "wxp": np.ascontiguousarray(
                wxp_pad[perm].reshape(NCH, 128, 128)),
            "wdt": np.ascontiguousarray(wdt_pad[:, sl]),
            "bdt": np.ascontiguousarray(bdt[sl].reshape(NCHH, 128).T),
            "alog": np.ascontiguousarray(
                alog[sl].reshape(NCHH, 128, NST).transpose(1, 0, 2)
            ).reshape(128, 128),
            "dskip": np.ascontiguousarray(dskip[sl].reshape(NCHH, 128).T),
            "wout": np.ascontiguousarray(
                np.concatenate([wout[sl], wout[slp]], axis=0)
            ).reshape(NCH, 128, D),
            "rhs": np.ascontiguousarray(rhs).reshape(KD, 128, D),
        }
        in_maps.append(m)

    nc = build_nc(L=L)
    res = run_bass_kernel_spmd(nc, in_maps, core_ids=list(range(8)))

    out = np.empty((B, L, D), np.float32)
    for c in range(8):
        beta, phi = divmod(c, 2)
        out[beta, phi * LH:(phi + 1) * LH, :] = res.results[c]["out_half"]
    return out
